# version 11
# baseline (speedup 1.0000x reference)
import math
from contextlib import ExitStack

import numpy as np
import concourse.bass as bass
import concourse.mybir as mybir
from concourse.bass_utils import run_bass_kernel_spmd

F32 = mybir.dt.float32
BF16 = mybir.dt.bfloat16
I32 = mybir.dt.int32
Alu = mybir.AluOpType
Act = mybir.ActivationFunctionType
AX = mybir.AxisListType

B, T, D = 8, 2048, 1024
NCH = 8
NT = 4
TTS = 512
NH, DKH, DVH = 16, 64, 128
CRET = 128
EPS = 1e-5
DI = 3072
MI = 2048
NE = 8


class Prog:
    ENGINES = ("sync", "vector", "scalar", "tensor", "gpsimd")
    PERSIST = ("bar", "st", "ldx", "ldc", "ldw0", "ldw0b", "ldw1", "pcast",
               "l0aSp00", "l0aSp01", "l0aSp10", "l0aSp11", "l0aSp20", "l0aSp21")

    def __init__(self, nc, ctx, taps=()):
        self.nc = nc
        self.ctx = ctx
        self.taps = list(taps)
        self.sems = {}
        self.base = {}
        self.cnt = {}
        self.pcnt = {}
        allp = [ctx.enter_context(nc.semaphore(f"pool{i}")) for i in range(87)]
        self.pool = allp[:52]
        self.gpool = allp[52:]
        self.pool_i = 0
        self.gpool_i = 0
        self.kind = {}
        self._persist = {n: ctx.enter_context(nc.semaphore("s_" + n))
                         for n in self.PERSIST}
        for h in allp + list(self._persist.values()):
            self.pcnt[h.name] = 0
        self.phase_names = set()
        self.bar = self.sem("bar")
        self.nbar = 0
        self.sem("st")

    def sem(self, name, gp=False):
        if name not in self.sems:
            if name in self._persist:
                h = self._persist[name]
            elif gp:
                h = self.gpool[self.gpool_i % len(self.gpool)]
                self.gpool_i += 1
                self.kind[name] = "gp"
            else:
                h = self.pool[self.pool_i % len(self.pool)]
                self.pool_i += 1
                self.kind[name] = "hw"
            self.sems[name] = h
            self.base[name] = self.pcnt[h.name]
            self.cnt[name] = 0
        return self.sems[name]

    def inc(self, ins, name, n=1, quiesce=True):
        gp = "qPoolDynamic" in ins.concise()
        if name not in self.sems:
            h = self.sem(name, gp=gp)
        else:
            h = self.sems[name]
            if gp and self.kind.get(name) == "hw" and name not in self._persist:
                raise RuntimeError(f"sem {name}: gpsimd DMA inc on hw-pool sem "
                                   "(bind order bug - inc before wait)")
        ins.then_inc(h, n)
        self.cnt[name] += n
        self.pcnt[h.name] += n
        if quiesce:
            self.phase_names.add(name)
        return ins

    def wait(self, e, name, val=None):
        h = self.sem(name)
        e.wait_ge(h, self.base[name] + (self.cnt[name] if val is None else val))

    def val(self, name):
        self.sem(name)
        return self.cnt[name]

    def sb(self, name, shape, dtype, stack=None):
        return (stack or self.ctx).enter_context(self.nc.sbuf_tensor(name, shape, dtype))

    def ps(self, name, shape, dtype=F32, stack=None):
        return (stack or self.ctx).enter_context(self.nc.psum_tensor(name, shape, dtype))

    def phase(self, name):
        return _Phase(self, name)


class _Phase:
    def __init__(self, prog, name):
        self.p = prog
        self.name = name

    def __enter__(self):
        self.block = self.p.nc.Block(self.name)
        self.block.__enter__()
        return self

    def eng(self, which, f):
        p = self.p

        def wrapped(e):
            if p.nbar:
                e.wait_ge(p.bar, 5 * p.nbar)
            f(e)
        getattr(self.block, which)(wrapped)

    def __exit__(self, *exc):
        if exc[0] is None:
            p = self.p
            qn = [(n, p.base[n] + p.cnt[n]) for n in sorted(p.phase_names)
                  if n != "bar"]
            p.phase_names = set()
            if qn:
                def quiesce(e):
                    if p.nbar:
                        e.wait_ge(p.bar, 5 * p.nbar)
                    for n, v in qn:
                        e.wait_ge(p.sems[n], v)
                self.block.sync(quiesce)
            for which in Prog.ENGINES:
                def endf(e):
                    if p.nbar:
                        e.wait_ge(p.bar, 5 * p.nbar)
                    e.drain().then_inc(p.bar, 1)
                    p.cnt["bar"] += 1
                    p.pcnt[p.bar.name] += 1
                getattr(self.block, which)(endf)
            p.nbar += 1
        return self.block.__exit__(*exc)


def host_constants():
    c = {}
    c["ident_f32"] = np.eye(128, dtype=np.float32)
    c["ones_f32"] = np.ones((128, 1), np.float32)
    c["ones_row"] = np.ones((1, 128), np.float32)
    c["zero_col"] = np.zeros((128, 1), np.float32)
    c["halfpi_col"] = np.full((128, 1), math.pi / 2, np.float32)
    half = 1.0 / np.power(10000.0, np.linspace(0.0, 1.0, DKH // 2))
    angle = np.repeat(half, 2)
    idx = np.arange(T, dtype=np.float64)
    sin = np.sin(idx[:, None] * angle[None, :])
    cos = np.cos(idx[:, None] * angle[None, :])
    c["rot_cos"] = np.tile(cos.T, (2, 1)).astype(np.float32)
    sign = np.where(np.arange(DKH) % 2 == 0, -1.0, 1.0)
    c["rot_sin"] = np.tile((sin * sign[None, :]).T, (2, 1)).astype(np.float32)
    decay = np.log(1.0 - 2.0 ** (-5.0 - np.arange(NH, dtype=np.float64)))
    r = np.arange(CRET)
    dl = np.exp((r[None, :] - r[:, None])[None] * decay[:, None, None])
    dl = np.where(r[None, :] >= r[:, None], dl, 0.0)
    c["dlt"] = np.ascontiguousarray(
        dl.transpose(1, 0, 2).reshape(CRET, NH * CRET)).astype(np.float32)
    c["alpha"] = np.exp((CRET - 1 - r)[:, None] * decay[None, :]).astype(np.float32)
    beta = np.exp((r + 1)[None, :] * decay[:, None])
    c["beta_bc"] = np.broadcast_to(
        beta.reshape(1, NH * CRET), (128, NH * CRET)).astype(np.float32).copy()
    lam = np.exp(decay)
    n = np.arange(T, dtype=np.float64)
    rowsum = (1.0 - lam[:, None] ** (n[None, :] + 1.0)) / (1.0 - lam[:, None])
    c["rnorm"] = (1.0 / np.sqrt(rowsum)).astype(np.float32)
    eh = np.zeros((128, NH * NH), np.float32)
    for h_ in range(NH):
        eh[:, h_ * NH + h_] = 1.0
    c["ehot"] = eh
    c["lam_pc"] = [float(v) for v in lam ** CRET]
    return c


CONSTS = None


def build_program(taps=(), upto=None, start=None):
    global CONSTS
    if CONSTS is None:
        CONSTS = host_constants()
    nc = bass.Bass("TRN2", target_bir_lowering=False, debug=False)
    dI = {}

    def din(name, shape, dtype=F32):
        dI[name] = nc.dram_tensor(name, list(shape), dtype, kind="ExternalInput").ap()

    din("xT", (D, T))
    for w in ("wk_t", "wv_t", "wq_t", "wkp_t", "wqp_t", "wo_t"):
        din(w, (D, D))
    for v in ("bk", "bv", "bq", "bkp", "bqp", "bo", "ps_col", "ln_g", "ln_b",
              "c0", "c1", "q_b", "k_b", "o_b", "sb2"):
        din(v, (128, NCH))
    din("w_up", (DI, D)); din("w_gate", (DI, D)); din("w_down", (D, DI))
    din("wmod0", (D, 6 * D)); din("wmod1", (D, 6 * D))
    din("q_w", (D, D)); din("k_w", (D, D)); din("v_w", (D, 2 * D)); din("g_w", (D, 2 * D))
    din("o_w", (2 * D, D))
    din("v_b_row", (1, 2 * D)); din("g_b", (128, 16))
    din("ehot", (128, NH * NH))
    din("gate_wp", (128, NCH * NE)); din("gate_bbc", (128, NE))
    for nm in ("w1_g", "w1_l", "w3_g", "w3_l"):
        din(nm, (NE * MI, D))
    din("w2", (NE * MI, D))
    for nm in ("b1_g", "b1_l", "b3_g", "b3_l"):
        din(nm + "_c", (NE * 128, 16))
    din("b2_c", (NE * 128, NCH))
    for nm in ("sw1_g", "sw1_l", "sw3_g", "sw3_l"):
        din(nm, (MI, D))
    din("sw2", (MI, D))
    for nm in ("sb1_g", "sb1_l", "sb3_g", "sb3_l"):
        din(nm + "_c", (128, 16))
    din("ident_f32", (128, 128))
    din("ones_f32", (128, 1)); din("ones_row", (1, 128))
    din("zero_col", (128, 1)); din("halfpi_col", (128, 1))
    din("ident_bf", (128, 128), BF16)
    din("rot_cos", (128, T)); din("rot_sin", (128, T))
    din("dlt", (CRET, NH * CRET)); din("alpha", (CRET, NH))
    din("beta_bc", (128, NH * CRET)); din("rnorm", (NH, T))

    out = nc.dram_tensor("out_fm", [D, T], F32, kind="ExternalOutput").ap()

    scr = {
        "mod": nc.dram_tensor("scr_mod", [2, 6 * D], F32, kind="Internal").ap(),
        "row": nc.dram_tensor("scr_row", [64, T], F32, kind="Internal").ap(),
        "P": nc.dram_tensor("scr_P", [DI, T], F32, kind="Internal").ap(),
        "G": nc.dram_tensor("scr_G", [DI, T], F32, kind="Internal").ap(),
        "O": nc.dram_tensor("scr_O", [2 * D, T], F32, kind="Internal").ap(),
        "x": nc.dram_tensor("scr_x", [D, T], F32, kind="Internal").ap(),
        "comb": nc.dram_tensor("scr_comb", [NE, T], F32, kind="Internal").ap(),
        "kq": nc.dram_tensor("scr_kq", [2 * D, T], BF16, kind="Internal").ap(),
        "UP": nc.dram_tensor("scr_UP", [DI, T], mybir.dt.int16, kind="Internal").ap(),
    }
    for nm in ("w1_g", "w1_l", "w3_g", "w3_l"):
        scr["c" + nm] = nc.dram_tensor("scr_c" + nm, [NE * MI, D], BF16,
                                       kind="Internal").ap()
        scr["cs" + nm[1:]] = nc.dram_tensor("scr_cs" + nm[1:], [MI, D], BF16,
                                            kind="Internal").ap()
    scr["cw2"] = nc.dram_tensor("scr_cw2", [NE * MI, D], BF16,
                                kind="Internal").ap()
    scr["csw2"] = nc.dram_tensor("scr_csw2", [MI, D], BF16,
                                 kind="Internal").ap()
    cc = [(nc.dram_tensor(f"cc_in{i}", [1, 8], F32, kind="Internal").ap(),
           nc.dram_tensor(f"cc_out{i}", [1, 8], F32, kind="Internal").ap())
          for i in range(4)]

    with ExitStack() as ctx:
        p = Prog(nc, ctx, taps)
        em = Emit(p, dI, out, scr, cc, taps, upto)
        em.start = start
        em.run()
    return nc


class Emit:
    def __init__(self, p, dI, out, scr, cc, taps, upto):
        self.p = p
        self.dI = dI
        self.out = out
        self.scr = scr
        self.cc = cc
        self.taps = list(taps)
        self.upto = upto

    def tap(self, name, src_list):
        if name not in self.taps:
            return
        p = self.p
        ap0 = src_list[0]
        n = len(src_list)
        t = p.nc.dram_tensor("tap_" + name, [n * ap0.shape[0]] + list(ap0.shape[1:]),
                             ap0.dtype, kind="ExternalOutput").ap()
        with p.phase("tap_" + name) as ph:
            def f(s):
                for i, a in enumerate(src_list):
                    p.inc(s.dma_start(
                        out=t[i * ap0.shape[0]:(i + 1) * ap0.shape[0]], in_=a), "st", 16)
            ph.eng("sync", f)

    def tap_dram(self, name, dram_ap, nrows):
        if name not in self.taps:
            return
        p = self.p
        t = p.nc.dram_tensor("tap_" + name, [nrows, T], dram_ap.dtype,
                             kind="ExternalOutput").ap()
        with p.phase("tapd_" + name) as ph:
            def f(s):
                for c in range(nrows // 128):
                    p.inc(s.dma_start(out=t[c * 128:(c + 1) * 128, :],
                                      in_=dram_ap[c * 128:(c + 1) * 128, :]),
                          "st", 16)
            ph.eng("sync", f)

    def stop(self, name):
        return self.upto is not None and name == self.upto

    def run(self):
        p = self.p
        self.alloc_persistent()
        self.p0_loads()
        if getattr(self, "start", None) != "l1":
            self.mod_layer(0)
            self.tap("mod0", [self.modfm[0].ap()])
            if self.stop("mod0"):
                return self.finish()
            r = self.block_l0()
            if r:
                return self.finish()
        r = self.block_l1()
        if r:
            return self.finish()
        self.finish(final=True)

    def block_l0(self):
        p = self.p
        with ExitStack() as sc0:
            hb = [p.sb(f"hb0_{c}", [128, T], BF16, sc0) for c in range(NCH)]
            with ExitStack() as sc1:
                wk = [p.sb(f"wkc{c}", [128, D], BF16, sc1) for c in range(NCH)]
                wq = [p.sb(f"wqc{c}", [128, D], BF16, sc1) for c in range(NCH)]
                wv = [p.sb(f"wvc{c}", [128, D], BF16, sc1) for c in range(NCH)]
                with p.phase("l0w1") as ph:
                    def f_g(g):
                        for c in range(NCH):
                            for dst, s_ in ((wk, "wk_t"), (wq, "wq_t"), (wv, "wv_t")):
                                p.inc(g.dma_start(
                                    out=dst[c][:],
                                    in_=self.dI[s_][c * 128:(c + 1) * 128, :]),
                                    "ldw0", 16)
                    ph.eng("gpsimd", f_g)
                self.ln_mod(dst=hb, ca=self.modfm[0][:, 8:16],
                            sa=self.modfm[0][:, 0:8], tag="ln1",
                            resid_g=self.modfm[0][:, 16:24])
                self.tap("ln1", [h.ap() for h in hb])
                if self.stop("ln1"):
                    return True
                self.l0_attn_a(hb, wk, wq, wv)
            self.l0_attn_b()
        self.tap("l0attn", [xc.ap() for xc in self.xres])
        if self.stop("l0attn"):
            return True
        with ExitStack() as scf:
            if self.l0_ffn(scf):
                return True
        self.tap("l0ffn", [xc.ap() for xc in self.xres])
        return self.stop("l0ffn")

    def block_l1(self):
        p = self.p
        self.mod_layer(1)
        self.tap("mod1", [self.modfm[1].ap()])
        if self.stop("mod1"):
            return True
        with ExitStack() as sc0:
            hb = [p.sb(f"hb1_{c}", [128, T], BF16, sc0) for c in range(NCH)]
            self.ln_mod(dst=hb, ca=self.modfm[1][:, 8:16],
                        sa=self.modfm[1][:, 0:8], tag="ln3")
            self.tap("ln3", [h.ap() for h in hb])
            if self.stop("ln3"):
                return True
            self.l1_attn(hb, sc0)
        self.tap("l1attn", [xc.ap() for xc in self.xres])
        if self.stop("l1attn"):
            return True
        with ExitStack() as scm:
            hb = [p.sb(f"hbm_{c}", [128, T], BF16, scm) for c in range(NCH)]
            self.ln_mod(dst=hb, ca=self.modfm[1][:, 32:40],
                        sa=self.modfm[1][:, 24:32], tag="ln4",
                        spill32=self.scr["P"])
            if self.stop("ln4"):
                return True
            if self.l1_moe(hb, scm):
                return True
        return False

    def finish(self, final=False):
        p = self.p
        with p.phase("fin") as ph:
            def f(s):
                if final:
                    for c in range(NCH):
                        p.inc(s.dma_start(out=self.out[c * 128:(c + 1) * 128, :],
                                          in_=self.xres[c][:]), "st", 16)
                p.wait(s, "st")
            ph.eng("sync", f)

    def alloc_persistent(self):
        p = self.p
        self.xres = [p.sb(f"xres{c}", [128, T], F32) for c in range(NCH)]
        self.identf = p.sb("identf", [128, 128], F32)
        self.identb = p.sb("identb", [128, 128], BF16)
        self.onesf = p.sb("onesf", [128, 1], F32)
        self.onesr = p.sb("onesr", [1, 128], F32)
        self.zcol = p.sb("zcol", [128, 1], F32)
        self.hpicol = p.sb("hpicol", [128, 1], F32)
        self.onesb = p.sb("onesb", [128, 1], BF16)
        self.cols = {}
        for v in ("bk", "bv", "bq", "bkp", "bqp", "bo", "ps_col", "ln_g", "ln_b",
                  "c0", "c1", "q_b", "k_b", "o_b", "sb2"):
            self.cols[v] = p.sb("col_" + v, [128, 8], F32)
        self.modfm = [p.sb(f"modfm{l}", [128, 48], F32) for l in range(2)]
        self.negps = p.sb("negps", [128, 8], F32)

    def p0_loads(self):
        p, dI = self.p, self.dI
        with p.phase("p0") as ph:
            def f_sync(s):
                for c in range(NCH):
                    p.inc(s.dma_start(out=self.xres[c][:],
                                      in_=dI["xT"][c * 128:(c + 1) * 128, :]), "ldx", 16)
                p.inc(s.dma_start(out=self.identf[:], in_=dI["ident_f32"]), "ldc", 16)
                p.inc(s.dma_start(out=self.identb[:], in_=dI["ident_bf"]), "ldc", 16)
                p.inc(s.dma_start(out=self.onesf[:], in_=dI["ones_f32"]), "ldc", 16)
                p.inc(s.dma_start(out=self.onesr[:], in_=dI["ones_row"]), "ldc", 16)
                p.inc(s.dma_start(out=self.zcol[:], in_=dI["zero_col"]), "ldc", 16)
                p.inc(s.dma_start(out=self.hpicol[:], in_=dI["halfpi_col"]), "ldc", 16)
                for v in ("bk", "bv", "bq", "bkp", "bqp", "bo", "ps_col", "ln_g",
                          "ln_b", "c0", "c1", "q_b", "k_b", "o_b", "sb2"):
                    p.inc(s.dma_start(out=self.cols[v][:], in_=dI[v]), "ldc", 16)
            ph.eng("sync", f_sync)

            def f_vec(v):
                p.wait(v, "ldc")
                v.tensor_copy(self.onesb[:], self.onesf[:])
                v.tensor_scalar_mul(self.negps[:], self.cols["ps_col"][:], -1.0)
                v.drain()
            ph.eng("vector", f_vec)


    def mod_layer(self, layer):
        p, dI = self.p, self.dI
        cname = "c0" if layer == 0 else "c1"
        wname = "wmod0" if layer == 0 else "wmod1"
        with ExitStack() as sc:
            wmodb = [p.sb(f"wmodb{layer}_{j}", [128, 6 * D], F32, sc) for j in range(2)]
            with p.phase(f"mod{layer}") as ph:
                csil = p.sb(f"modcs{layer}", [128, 8], F32, sc)
                mps = p.ps(f"modps{layer}", [128, 48], F32, sc)
                sS, sV, sM, sF = (f"mod{layer}" + x for x in "SVMF")
                sW = [f"mod{layer}w{j}" for j in range(2)]

                def f_g(g):
                    for c in range(NCH):
                        if c >= 2:
                            p.wait(g, sF, c - 1)
                        p.inc(g.dma_start(out=wmodb[c % 2][:],
                                          in_=dI[wname][c * 128:(c + 1) * 128, :]),
                              sW[c % 2], 16)
                ph.eng("gpsimd", f_g)

                def f_sc(s):
                    p.wait(s, "ldc")
                    s.activation(csil[:], self.cols[cname][:], Act.Silu)
                    p.inc(s.drain(), sS, 1)
                ph.eng("scalar", f_sc)

                def f_v(v):
                    p.wait(v, sF, p.val(sF) + NCH)
                    v.tensor_copy(self.modfm[layer][:], mps[:])
                    v.drain()
                ph.eng("vector", f_v)

                def f_t(t):
                    p.wait(t, sS, 1)
                    for c in range(NCH):
                        p.wait(t, sW[c % 2], 16 * (c // 2 + 1))
                        for j in range(48):
                            ins = t.matmul(mps[:, j:j + 1],
                                           wmodb[c % 2][:, j * 128:(j + 1) * 128],
                                           csil[:, c:c + 1],
                                           start=(c == 0 and j == 0),
                                           stop=(c == NCH - 1 and j == 47))
                        p.inc(ins, sF, 1)
                ph.eng("tensor", f_t)

        with p.phase(f"modg{layer}") as ph:
            def f_v(v):
                with ExitStack() as s2:
                    gtmp = self.p.sb(f"modgt{layer}", [128, 24], F32, s2)
                    for off in (16, 40):
                        g = self.modfm[layer][:, off:off + 8]
                        sgn, mag, neg = gtmp[:, 0:8], gtmp[:, 8:16], gtmp[:, 16:24]
                        v.tensor_scalar(sgn, g, 0.0, None, Alu.is_ge)
                        v.tensor_scalar_mul(neg, g, -1.0)
                        v.drain()
                        v.tensor_scalar(sgn, sgn, 2.0, -1.0, Alu.mult, Alu.add)
                        v.tensor_tensor(mag, g, neg, Alu.max)
                        v.drain()
                        v.tensor_scalar_max(mag, mag, 0.25)
                        v.drain()
                        v.tensor_tensor(g, sgn, mag, Alu.mult)
                        v.drain()
            ph.eng("vector", f_v)

    def ln_mod(self, dst, ca, sa, tag, resid_g=None, spill32=None):
        p = self.p
        with ExitStack() as sc:
            rows = [p.sb(tag + f"_r{i}", [1, T], F32, sc) for i in range(4)]
            arow, brow, rmean, rtmp = (r.ap() for r in rows)
            sc_st = sc.enter_context(ExitStack())
            stat_s = p.ps(tag + "_sts", [1, T], F32, sc_st)
            stat_q = p.ps(tag + "_stq", [1, T], F32, sc_st)
            sq = [p.sb(tag + f"_sq{j}", [128, T], F32, sc)
                  for j in range(2 if resid_g is not None else 1)]

            with p.phase(tag + "a") as ph:
                sQ, sM = tag + "Q", tag + "M"
                mbase, qbase = p.val(sM), p.val(sQ)

                def f_sc(s):
                    p.wait(s, "ldx")
                    for c in range(NCH):
                        if c >= 1:
                            p.wait(s, sM, mbase + 2 * c)
                        p.inc(s.activation(sq[0][:], self.xres[c][:], Act.Square),
                              sQ, 1)
                ph.eng("scalar", f_sc)

                def f_t(t):
                    p.wait(t, "ldx")
                    p.wait(t, "ldc")
                    for c in range(NCH):
                        for n in range(NT):
                            ins = t.matmul(stat_s[0:1, n * TTS:(n + 1) * TTS],
                                           self.onesf[:],
                                           self.xres[c][:, n * TTS:(n + 1) * TTS],
                                           start=(c == 0), stop=(c == NCH - 1))
                        p.inc(ins, sM, 1)
                        p.wait(t, sQ, qbase + c + 1)
                        for n in range(NT):
                            ins = t.matmul(stat_q[0:1, n * TTS:(n + 1) * TTS],
                                           self.onesf[:],
                                           sq[0][:, n * TTS:(n + 1) * TTS],
                                           start=(c == 0), stop=(c == NCH - 1))
                        p.inc(ins, sM, 1)
                ph.eng("tensor", f_t)

            with p.phase(tag + "b") as ph:
                sW, sS, sR = tag + "W", tag + "S", tag + "R"

                def f_v(v):
                    v.tensor_scalar_mul(rmean, stat_s[0:1, :], 1.0 / D)
                    v.tensor_scalar_mul(rtmp, stat_q[0:1, :], 1.0 / D)
                    v.drain()
                    v.tensor_tensor(arow, rmean, rmean, Alu.mult)
                    v.drain()
                    v.tensor_tensor(rtmp, rtmp, arow, Alu.subtract)
                    v.drain()
                    v.tensor_scalar_add(rtmp, rtmp, EPS)
                    v.drain()
                    v.reciprocal(rtmp, rtmp)
                    p.inc(v.drain(), sW, 1)
                    p.wait(v, sS, 1)
                    v.tensor_tensor(brow, rmean, arow, Alu.mult)
                    v.drain()
                    v.tensor_scalar_mul(brow, brow, -1.0)
                    v.drain()
                    p.inc(v.drain(), sR, 1)
                ph.eng("vector", f_v)

                def f_sc(s):
                    p.wait(s, sW, 1)
                    p.inc(s.activation(arow, rtmp, Act.Sqrt), sS, 1)
                ph.eng("scalar", f_sc)

            sc_st.close()
            with p.phase(tag + "c") as ph:
                abps = [p.ps(tag + f"_ab{j}", [128, 512], F32, sc) for j in range(4)]
                ca1 = p.sb(tag + "_ca1", [128, 8], F32, sc)
                sB, sA = tag + "B", tag + "A"
                if spill32 is not None:
                    hfb = [p.sb(tag + f"_hf{j}", [128, TTS], F32, sc)
                           for j in range(2)]
                    sH = tag + "H"
                    sHD = [tag + "HD0", tag + "HD1"]

                    def f_sy(s):
                        for j in range(NT * NCH):
                            n, c = j // NCH, j % NCH
                            p.wait(s, sH, j + 1)
                            p.inc(s.dma_start(
                                out=spill32[c * 128:(c + 1) * 128,
                                            n * TTS:(n + 1) * TTS],
                                in_=hfb[j % 2][:]), sHD[j % 2], 16)
                    ph.eng("sync", f_sy)

                def f_t(t):
                    for n in range(NT):
                        if n >= 2:
                            p.wait(t, sA, NCH * (n - 1))
                        ins = t.matmul(abps[2 * (n % 2)][:], self.onesr[:],
                                       arow[:, n * TTS:(n + 1) * TTS],
                                       start=True, stop=True)
                        ins2 = t.matmul(abps[2 * (n % 2) + 1][:], self.onesr[:],
                                        brow[:, n * TTS:(n + 1) * TTS],
                                        start=True, stop=True)
                        p.inc(ins2, sB, 1)
                ph.eng("tensor", f_t)

                def f_v(v):
                    v.tensor_scalar_add(ca1[:], ca, 1.0)
                    v.drain()
                    for n in range(NT):
                        p.wait(v, sB, n + 1)
                        a = abps[2 * (n % 2)]
                        bb = abps[2 * (n % 2) + 1]
                        tl = slice(n * TTS, (n + 1) * TTS)
                        for c in range(NCH):
                            tmp = sq[0][:, 0:TTS]
                            v.tensor_tensor(tmp, self.xres[c][:, tl], a[:], Alu.mult)
                            v.drain()
                            v.tensor_tensor(tmp, tmp, bb[:], Alu.add)
                            v.drain()
                            if resid_g is None and spill32 is not None:
                                j = n * NCH + c
                                if j >= 2:
                                    p.wait(v, sHD[j % 2], 16 * (j // 2))
                                hf = hfb[j % 2]
                                v.tensor_scalar(hf[:], tmp,
                                                ca1[:, c:c + 1], sa[:, c:c + 1],
                                                Alu.mult, Alu.add)
                                v.drain()
                                p.inc(v.drain(), sH, 1)
                                ins = v.tensor_copy(dst[c][:, tl], hf[:])
                                v.drain()
                            elif resid_g is None:
                                ins = v.tensor_scalar(dst[c][:, tl], tmp,
                                                      ca1[:, c:c + 1], sa[:, c:c + 1],
                                                      Alu.mult, Alu.add)
                                v.drain()
                            else:
                                hf = sq[-1][:, 0:TTS]
                                v.tensor_scalar(hf, tmp, ca1[:, c:c + 1],
                                                sa[:, c:c + 1], Alu.mult, Alu.add)
                                v.drain()
                                v.tensor_copy(dst[c][:, tl], hf)
                                v.tensor_scalar_mul(tmp, hf, resid_g[:, c:c + 1])
                                v.drain()
                                ins = v.tensor_tensor(self.xres[c][:, tl],
                                                      self.xres[c][:, tl], tmp, Alu.add)
                                v.drain()
                            p.inc(ins, sA, 1)
                ph.eng("vector", f_v)

    def l0_attn_a(self, hb, wk, wq, wv):
        p = self.p
        with ExitStack() as sca, p.phase("l0a") as ph:
            ppk = [p.ps(f"l0apk{j}", [128, TTS], F32, sca) for j in range(2)]
            ppq = [p.ps(f"l0apq{j}", [128, TTS], F32, sca) for j in range(2)]
            ppv = [p.ps(f"l0apv{j}", [128, TTS], F32, sca) for j in range(2)]
            ktile = [p.sb(f"l0ak{j}", [128, TTS], BF16, sca) for j in range(2)]
            qtile = [p.sb(f"l0aq{j}", [128, TTS], BF16, sca) for j in range(2)]
            vtile = [p.sb(f"l0av{j}", [128, TTS], F32, sca) for j in range(2)]
            sM, sE = "l0aM", "l0aE"
            sSp = [[f"l0aSp{t_}{j}" for j in range(2)] for t_ in range(3)]

            def f_t(t):
                p.wait(t, "ldw0")
                for i in range(NCH * NT):
                    c, n = divmod(i, NT)
                    if i >= 2:
                        p.wait(t, sE, 3 * (i - 1))
                    for wmat, bank in ((wk, ppk), (wq, ppq), (wv, ppv)):
                        for k in range(NCH):
                            ins = t.matmul(
                                bank[i % 2][:],
                                wmat[k][:, c * 128:(c + 1) * 128],
                                hb[k][:, n * TTS:(n + 1) * TTS],
                                start=(k == 0), stop=(k == NCH - 1))
                        p.inc(ins, sM, 1)
            ph.eng("tensor", f_t)

            def f_sc(s):
                for i in range(NCH * NT):
                    c, n = divmod(i, NT)
                    if i >= 2:
                        for t_ in range(3):
                            p.wait(s, sSp[t_][i % 2], 16 * ((i - 2) // 2 + 1))
                    p.wait(s, sM, 3 * i + 1)
                    p.inc(s.activation(ktile[i % 2][:], ppk[i % 2][:], Act.Identity,
                                       bias=self.cols["bk"][:, c:c + 1]), sE, 1)
                    p.wait(s, sM, 3 * i + 2)
                    p.inc(s.activation(qtile[i % 2][:], ppq[i % 2][:], Act.Identity,
                                       bias=self.cols["bq"][:, c:c + 1]), sE, 1)
                    p.wait(s, sM, 3 * i + 3)
                    p.inc(s.activation(vtile[i % 2][:], ppv[i % 2][:], Act.Identity,
                                       bias=self.cols["bv"][:, c:c + 1]), sE, 1)
            ph.eng("scalar", f_sc)

            def f_sync(s):
                for i in range(NCH * NT):
                    c, n = divmod(i, NT)
                    tl = slice(n * TTS, (n + 1) * TTS)
                    p.wait(s, sE, 3 * i + 3)
                    p.inc(s.dma_start(out=self.scr["kq"][c * 128:(c + 1) * 128, tl],
                                      in_=ktile[i % 2][:]), sSp[0][i % 2], 16)
                    p.inc(s.dma_start(out=self.scr["kq"][D + c * 128:D + (c + 1) * 128, tl],
                                      in_=qtile[i % 2][:]), sSp[1][i % 2], 16)
                    p.inc(s.dma_start(out=self.scr["P"][c * 128:(c + 1) * 128, tl],
                                      in_=vtile[i % 2][:]), sSp[2][i % 2], 16)
                for t_ in range(3):
                    for j in range(2):
                        p.wait(s, sSp[t_][j])
            ph.eng("sync", f_sync)

    def l0_attn_b(self):
        p, dI = self.p, self.dI
        with ExitStack() as scb:
            wkp = [p.sb(f"wkpc{c}", [128, D], BF16, scb) for c in range(NCH)]
            wqp = [p.sb(f"wqpc{c}", [128, D], BF16, scb) for c in range(NCH)]
            wo = [p.sb(f"woc{c}", [128, D], BF16, scb) for c in range(NCH)]
            gabo = p.sb("l0gabo", [128, 8], F32, scb)
            with p.phase("l0wb") as ph:
                def f_g(g):
                    for c in range(NCH):
                        for dst, s_ in ((wkp, "wkp_t"), (wqp, "wqp_t"), (wo, "wo_t")):
                            p.inc(g.dma_start(out=dst[c][:],
                                              in_=dI[s_][c * 128:(c + 1) * 128, :]),
                                  "ldw0b", 16)
                ph.eng("gpsimd", f_g)

                def f_v(v):
                    v.tensor_tensor(gabo[:], self.modfm[0][:, 16:24],
                                    self.cols["bo"][:], Alu.mult)
                    v.drain()
                ph.eng("vector", f_v)

            for n in range(NT):
                self._l0_tile(n, wkp, wqp, wo, gabo)

    def _l0_tile(self, n, wkp, wqp, wo, gabo):
        p = self.p
        tl = slice(n * TTS, (n + 1) * TTS)
        tg = f"l0b{n}"
        C = self.cols
        with ExitStack() as sc, p.phase(tg) as ph:
            keyf = [p.sb(tg + f"keyf{c}", [128, TTS], BF16, sc) for c in range(NCH)]
            qryf = [p.sb(tg + f"qryf{c}", [128, TTS], BF16, sc) for c in range(NCH)]
            kp = p.sb(tg + "kp", [128, TTS], F32, sc)
            skp = p.sb(tg + "skp", [128, TTS], F32, sc)
            ckp = p.sb(tg + "ckp", [128, TTS], F32, sc)
            qp = p.sb(tg + "qp", [128, TTS], F32, sc)
            sqp = p.sb(tg + "sqp", [128, TTS], F32, sc)
            cqp = p.sb(tg + "cqp", [128, TTS], F32, sc)
            val = [p.sb(tg + f"val{j}", [128, TTS], F32, sc) for j in range(2)]
            scn = p.sb(tg + "scn", [128, TTS], F32, sc)
            mr = p.sb(tg + "mr", [128, TTS], F32, sc)
            mi = p.sb(tg + "mi", [128, TTS], F32, sc)
            reti = [p.sb(tg + f"ret{c}", [128, TTS], F32, sc) for c in range(NCH)]
            sqb = [p.sb(tg + f"sqb{j}", [128, TTS], F32, sc) for j in range(2)]
            lno = keyf
            tres = val
            arow, brow = mr[0:1, :], mi[0:1, :]
            rmean, rtmp = scn[0:1, :], ckp[0:1, :]
            ppk = p.ps(tg + "ppk", [128, TTS], F32, sc)
            ppq = p.ps(tg + "ppq", [128, TTS], F32, sc)
            ppo = [p.ps(tg + f"ppo{j}", [128, TTS], F32, sc) for j in range(2)]
            pst_s = p.ps(tg + "psts", [1, TTS], F32, sc)
            pst_q = p.ps(tg + "pstq", [1, TTS], F32, sc)
            pab = [p.ps(tg + f"pab{j}", [128, TTS], F32, sc) for j in range(2)]
            (sKQ, sM, sE, sKP, sQP, sC, sQ2, sMs, sW, sS, sR, sB, sA, sO,
             sF, sX) = (tg + x_ for x_ in
                        ("KQ", "M", "E", "KP", "QP", "C", "Q2", "Ms", "W",
                         "S", "R", "B", "A", "O", "F", "X"))
            sL = [tg + "L0", tg + "L1"]

            def f_sync(s):
                for t_ in range(3):
                    for j in range(2):
                        p.wait(s, f"l0aSp{t_}{j}")
                for c in range(NCH):
                    p.inc(s.dma_start(out=keyf[c][:],
                                      in_=self.scr["kq"][c * 128:(c + 1) * 128, tl]),
                          sKQ, 16)
                    p.inc(s.dma_start(out=qryf[c][:],
                                      in_=self.scr["kq"][D + c * 128:D + (c + 1) * 128, tl]),
                          sKQ, 16)
                for c in range(NCH):
                    if c >= 2:
                        p.wait(s, sC, c - 1)
                    p.inc(s.dma_start(out=val[c % 2][:],
                                      in_=self.scr["P"][c * 128:(c + 1) * 128, tl]),
                          sL[c % 2], 16)
            ph.eng("sync", f_sync)

            def f_t(t):
                p.wait(t, "ldw0b")
                p.wait(t, sKQ, 16 * 2 * NCH)
                for c in range(NCH):
                    if c >= 1:
                        p.wait(t, sE, 2 * c)
                    for k in range(NCH):
                        ins = t.matmul(ppk[:], wkp[k][:, c * 128:(c + 1) * 128],
                                       keyf[k][:], start=(k == 0), stop=(k == NCH - 1))
                    p.inc(ins, sM, 1)
                    for k in range(NCH):
                        ins = t.matmul(ppq[:], wqp[k][:, c * 128:(c + 1) * 128],
                                       qryf[k][:], start=(k == 0), stop=(k == NCH - 1))
                    p.inc(ins, sM, 1)
                for c in range(NCH):
                    p.wait(t, sC, c + 1)
                    ins = t.matmul(pst_s[0:1, :], self.onesf[:], reti[c][:],
                                   start=(c == 0), stop=(c == NCH - 1))
                    p.wait(t, sQ2, c + 1)
                    ins = t.matmul(pst_q[0:1, :], self.onesf[:], sqb[c % 2][:],
                                   start=(c == 0), stop=(c == NCH - 1))
                    p.inc(ins, sMs, 1)
                p.wait(t, sR, 1)
                ins = t.matmul(pab[0][:], self.onesr[:], arow, start=True, stop=True)
                ins2 = t.matmul(pab[1][:], self.onesr[:], brow, start=True, stop=True)
                p.inc(ins2, sB, 1)
                for c in range(NCH):
                    p.wait(t, sA, NCH)
                    if c >= 2:
                        p.wait(t, sF, c - 1)
                    for k in range(NCH):
                        ins = t.matmul(ppo[c % 2][:], wo[k][:, c * 128:(c + 1) * 128],
                                       lno[k][:], start=(k == 0), stop=(k == NCH - 1))
                    p.inc(ins, sO, 1)
            ph.eng("tensor", f_t)

            def f_sc(s):
                for c in range(NCH):
                    p.wait(s, sM, 2 * c + 1)
                    if c >= 1:
                        p.wait(s, sC, c)
                    s.activation(kp[:], ppk[:], Act.Tanh, bias=C["bkp"][:, c:c + 1])
                    p.inc(s.drain(), sE, 1)
                    s.activation(skp[:], kp[:], Act.Sin, scale=C["ps_col"][:, c:c + 1], bias=self.zcol[:, 0:1])
                    s.drain()
                    s.activation(ckp[:], kp[:], Act.Abs)
                    s.drain()
                    s.activation(kp[:], ckp[:], Act.Sin,
                                 scale=self.negps[:, c:c + 1], bias=self.hpicol[:, 0:1])
                    p.inc(s.drain(), sKP, 1)
                    p.wait(s, sM, 2 * c + 2)
                    s.activation(qp[:], ppq[:], Act.Tanh, bias=C["bqp"][:, c:c + 1])
                    p.inc(s.drain(), sE, 1)
                    s.activation(sqp[:], qp[:], Act.Sin, scale=C["ps_col"][:, c:c + 1], bias=self.zcol[:, 0:1])
                    s.drain()
                    s.activation(cqp[:], qp[:], Act.Abs)
                    s.drain()
                    s.activation(qp[:], cqp[:], Act.Sin,
                                 scale=self.negps[:, c:c + 1], bias=self.hpicol[:, 0:1])
                    p.inc(s.drain(), sQP, 1)
                for c in range(NCH):
                    p.wait(s, sC, c + 1)
                    if c >= 2:
                        p.wait(s, sMs, c - 1)
                    p.inc(s.activation(sqb[c % 2][:], reti[c][:], Act.Square), sQ2, 1)
                p.wait(s, sW, 1)
                p.inc(s.activation(arow, rtmp, Act.Sqrt), sS, 1)
                for c in range(NCH):
                    p.wait(s, sO, c + 1)
                    if c >= 2:
                        p.wait(s, sX, c - 1)
                    p.inc(s.activation(tres[c % 2][:], ppo[c % 2][:], Act.Identity,
                                       scale=self.modfm[0][:, 16 + c:17 + c],
                                       bias=gabo[:, c:c + 1]), sF, 1)
            ph.eng("scalar", f_sc)

            def f_v(v):
                for c in range(NCH):
                    p.wait(v, sKP, c + 1)
                    p.wait(v, sL[c % 2], 16 * (c // 2 + 1))
                    v.tensor_tensor(ckp[:], val[c % 2][:], kp[:], Alu.mult)
                    v.drain()
                    v.tensor_tensor(skp[:], val[c % 2][:], skp[:], Alu.mult)
                    v.drain()
                    for src, dstb in ((ckp, mr), (skp, mi)):
                        v.tensor_tensor_scan(scn[:], src[:], src[:], 0.0,
                                             Alu.add, Alu.bypass)
                        v.drain()
                        v.tensor_copy(dstb[:, 0:64], scn[:, 0:64])
                        s3 = scn.ap().rearrange("pp (a b) -> pp a b", b=64)
                        bnd = s3[:, 0:7, 63:64].broadcast_to((128, 7, 64))
                        v.drain()
                        v.tensor_tensor(
                            dstb.ap().rearrange("pp (a b) -> pp a b", b=64)[:, 1:8, :],
                            s3[:, 1:8, :], bnd, Alu.subtract)
                        v.drain()
                    p.wait(v, sQP, c + 1)
                    v.tensor_tensor(mr[:], mr[:], qp[:], Alu.mult)
                    v.drain()
                    v.tensor_tensor(mi[:], mi[:], sqp[:], Alu.mult)
                    v.drain()
                    ins = v.tensor_tensor(reti[c][:], mr[:], mi[:], Alu.add)
                    v.drain()
                    p.inc(ins, sC, 1)
                p.wait(v, sMs, NCH)
                es = 1.0 / 32.0
                v.tensor_scalar_mul(rmean, pst_s[0:1, :], es / D)
                v.tensor_scalar_mul(rtmp, pst_q[0:1, :], es * es / D)
                v.drain()
                v.tensor_tensor(arow, rmean, rmean, Alu.mult)
                v.drain()
                v.tensor_tensor(rtmp, rtmp, arow, Alu.subtract)
                v.drain()
                v.tensor_scalar_add(rtmp, rtmp, EPS)
                v.drain()
                v.reciprocal(rtmp, rtmp)
                p.inc(v.drain(), sW, 1)
                p.wait(v, sS, 1)
                v.tensor_tensor(brow, rmean, arow, Alu.mult)
                v.drain()
                v.tensor_scalar_mul(brow, brow, -1.0)
                v.tensor_scalar_mul(arow, arow, es)
                v.drain()
                p.inc(v.drain(), sR, 1)
                p.wait(v, sB, 1)
                for c in range(NCH):
                    v.tensor_tensor(sqb[0][:], reti[c][:], pab[0][:], Alu.mult)
                    v.drain()
                    v.tensor_tensor(sqb[0][:], sqb[0][:], pab[1][:], Alu.add)
                    v.drain()
                    ins = v.tensor_scalar(lno[c][:], sqb[0][:],
                                          C["ln_g"][:, c:c + 1], C["ln_b"][:, c:c + 1],
                                          Alu.mult, Alu.add)
                    v.drain()
                    p.inc(ins, sA, 1)
                for c in range(NCH):
                    p.wait(v, sF, c + 1)
                    ins = v.tensor_tensor(self.xres[c][:, tl], self.xres[c][:, tl],
                                          tres[c % 2][:], Alu.add)
                    v.drain()
                    p.inc(ins, sX, 1)
            ph.eng("vector", f_v)

    def emit_armax(self, tag, cols, ccpair, res):
        p = self.p
        cc_in, cc_out = ccpair
        n = len(cols)
        with ExitStack() as sc, p.phase(tag) as ph:
            gbuf = p.sb(tag + "_g", [8, 128], F32, sc)
            gcol = p.sb(tag + "_c", [8, 1], F32, sc)
            zrow = p.sb(tag + "_z", [1, 8], F32, sc)
            sD, sZ, sV, sG = (tag + x for x in "DZVG")

            def f_v(v):
                v.memset(zrow[:], 0.0)
                p.inc(v.drain(), sZ, 1)
                p.wait(v, sD, 16)
                ins = v.tensor_reduce(gcol[0:n, :], gbuf[0:n, :], AX.X, Alu.max)
                v.drain()
                p.inc(ins, sV, 1)
            ph.eng("vector", f_v)

            def f_sync(s):
                p.wait(s, sZ, 1)
                p.inc(s.dma_start(out=cc_in, in_=zrow[:]), sD + "z", 16)
                for i, col in enumerate(cols):
                    p.inc(s.dma_start(out=self.scr["row"][32 + i:33 + i, 0:128],
                                      in_=col), sD + "a", 16)
                p.wait(s, sD + "a", p.val(sD + "a"))
                p.inc(s.dma_start(out=gbuf[0:n, :],
                                  in_=self.scr["row"][32:32 + n, 0:128]), sD, 16)
                p.wait(s, sV, 1)
                p.wait(s, sD + "z", 16)
                p.inc(s.dma_start(out=cc_in[0:1, 0:n], in_=gcol[0:n, :]), sD + "i", 16)
                p.wait(s, sG, 1)
                p.inc(s.dma_start(out=res[:], in_=cc_out), sD + "o", 16)
                p.wait(s, sD + "o", 16)
            ph.eng("sync", f_sync)

            def f_g(g):
                p.wait(g, sD + "i", 16)
                p.inc(g.collective_compute(
                    "AllReduce", Alu.max,
                    replica_groups=[list(range(B))],
                    ins=[cc_in], outs=[cc_out]), sG, 1)
            ph.eng("gpsimd", f_g)
        return res

    def bcast_cols(self, tag, pairs):
        p = self.p
        with ExitStack() as sc, p.phase(tag) as ph:
            pp = p.ps(tag + "_ps", [128, max(1, len(pairs))], F32, sc)
            sM = tag + "M"

            def f_t(t):
                for i, (r, _) in enumerate(pairs):
                    ins = t.matmul(pp[:, i:i + 1], self.onesr[:], r,
                                   start=(i == 0), stop=(i == len(pairs) - 1))
                p.inc(ins, sM, 1)
            ph.eng("tensor", f_t)

            def f_sc(s):
                p.wait(s, sM, 1)
                for i, (_, oc) in enumerate(pairs):
                    s.copy(oc, pp[:, i:i + 1])
                s.drain()
            ph.eng("scalar", f_sc)

    def l0_ffn(self, scf):
        p, dI = self.p, self.dI
        sca = p.sb("ffn_sca", [1, 16], F32, scf)
        arres = [p.sb(f"ffn_arr{i}", [1, 8], F32, scf) for i in range(4)]
        mxh = p.sb("ffn_mxh", [128, 1], F32, scf)
        rxc = p.sb("ffn_rxc", [128, 1], F32, scf)
        mxu = p.sb("ffn_mxu", [128, 1], F32, scf)
        mxg = p.sb("ffn_mxg", [128, 1], F32, scf)
        scq = ExitStack()
        xq = [p.sb(f"ffn_xq{c}", [128, T], BF16, scq) for c in range(NCH)]

        with ExitStack() as sch:
            h2 = [p.sb(f"h2_{c}", [128, T], F32, sch) for c in range(NCH)]
            self.ln_mod(dst=h2, ca=self.modfm[0][:, 32:40],
                        sa=self.modfm[0][:, 24:32], tag="ln2")
            self.tap("ln2", [h.ap() for h in h2])
            if self.stop("ln2"):
                return True

            with ExitStack() as sc, p.phase("ffn_mh") as ph:
                tcol = p.sb("ffn_mh_t", [128, 1], F32, sc)

                def f_v(v):
                    for c in range(NCH):
                        dst = mxh[:] if c == 0 else tcol[:]
                        v.tensor_reduce(dst, h2[c][:], AX.X, Alu.max,
                                        apply_absolute_value=True)
                        v.drain()
                        if c:
                            v.tensor_tensor(mxh[:], mxh[:], tcol[:], Alu.max)
                            v.drain()
                ph.eng("vector", f_v)

            ar1 = self.emit_armax("ffn_ar1", [mxh[:]], self.cc[0], arres[0])
            with p.phase("ffn_s1") as ph:
                def f_v(v):
                    v.tensor_scalar_max(sca[0:1, 0:1], ar1[0:1, 0:1], 32767.0e-12)
                    v.drain()
                    v.reciprocal(sca[0:1, 1:2], sca[0:1, 0:1])
                    v.drain()
                    v.tensor_scalar_mul(sca[0:1, 1:2], sca[0:1, 1:2], 32767.0)
                    v.drain()
                ph.eng("vector", f_v)
            self.bcast_cols("ffn_b1", [(sca[0:1, 1:2], rxc[:])])

            with ExitStack() as sc, p.phase("ffn_xq") as ph:
                u = p.sb("ffn_xq_u", [128, T], F32, sc)
                ui = p.sb("ffn_xq_i", [128, T], I32, sc)

                def f_v(v):
                    for c in range(NCH):
                        v.tensor_scalar_mul(u[:], h2[c][:], rxc[:, 0:1])
                        v.drain()
                        v.tensor_copy(ui[:], u[:])
                        v.drain()
                        v.tensor_scalar(xq[c][:], ui[:], -32768.0, 32767.0,
                                        Alu.max, Alu.min)
                        v.drain()
                ph.eng("vector", f_v)
        self.tap("ffnxq", [x.ap() for x in xq])
        if self.stop("ffnxq"):
            return True

        self._ffn_passA(xq, "w_up", self.scr["P"], mxu, "fAu")
        self.tap_dram("fAuP", self.scr["P"], DI)
        self._ffn_passA(xq, "w_gate", self.scr["G"], mxg, "fAg")
        scq.close()
        self.tap_dram("fAgG", self.scr["G"], DI)
        ar2 = self.emit_armax("ffn_ar2", [mxu[:], mxg[:]], self.cc[1], arres[1])

        with p.phase("ffn_s2") as ph:
            def f_v(v):
                v.tensor_scalar_max(sca[0:1, 2:4], ar2[0:1, 0:2], 1e-30)
                v.drain()
                v.reciprocal(sca[0:1, 4:6], sca[0:1, 2:4])
                v.drain()
                v.tensor_scalar_mul(sca[0:1, 4:6], sca[0:1, 4:6], 32767.0)
                v.drain()
                v.tensor_tensor(sca[0:1, 6:7], sca[0:1, 0:1], sca[0:1, 3:4],
                                Alu.mult)
                v.drain()
                v.tensor_scalar_mul(sca[0:1, 6:7], sca[0:1, 6:7],
                                    1.0 / (32767.0 * 32767.0))
                v.drain()
            ph.eng("vector", f_v)
        kuc = p.sb("ffn_kuc", [128, 1], F32, scf)
        kgc = p.sb("ffn_kgc", [128, 1], F32, scf)
        sgc = p.sb("ffn_sgc", [128, 1], F32, scf)
        self.bcast_cols("ffn_b2", [(sca[0:1, 4:5], kuc[:]),
                                   (sca[0:1, 5:6], kgc[:]),
                                   (sca[0:1, 6:7], sgc[:])])

        mx2 = p.sb("ffn_mx2", [128, 1], F32, scf)
        mx2g = p.sb("ffn_mx2g", [128, 1], F32, scf)
        self._ffn_passB(kuc, kgc, sgc, mx2, mx2g)
        self.tap_dram("fBUP", self.scr["UP"], DI)
        self.tap_dram("fBG2", self.scr["G"], DI)
        ar3 = self.emit_armax("ffn_ar3", [mx2[:], mx2g[:]], self.cc[2],
                              arres[2])
        with p.phase("ffn_s3") as ph:
            def f_v(v):
                v.tensor_scalar_max(sca[0:1, 7:8], ar3[0:1, 0:1], 1e-30)
                v.drain()
                v.tensor_tensor(sca[0:1, 7:8], sca[0:1, 7:8], ar3[0:1, 1:2],
                                Alu.max)
                v.drain()
                v.reciprocal(sca[0:1, 8:9], sca[0:1, 7:8])
                v.drain()
                v.tensor_scalar_mul(sca[0:1, 8:9], sca[0:1, 8:9], 32767.0)
                v.drain()
            ph.eng("vector", f_v)
        k2c = p.sb("ffn_k2c", [128, 1], F32, scf)
        self.bcast_cols("ffn_b3", [(sca[0:1, 8:9], k2c[:])])

        mxo = p.sb("ffn_mxo", [128, 1], F32, scf)
        mxog = p.sb("ffn_mxog", [128, 1], F32, scf)
        self._ffn_passC(k2c, mxo, mxog)
        self.tap_dram("fCO", self.scr["P"], DI)
        ar4 = self.emit_armax("ffn_ar4", [mxo[:], mxog[:]], self.cc[3],
                              arres[3])
        with p.phase("ffn_s4") as ph:
            def f_v(v):
                v.tensor_scalar_max(sca[0:1, 9:10], ar4[0:1, 0:1], 1e-30)
                v.drain()
                v.tensor_tensor(sca[0:1, 9:10], sca[0:1, 9:10], ar4[0:1, 1:2],
                                Alu.max)
                v.drain()
                v.reciprocal(sca[0:1, 10:11], sca[0:1, 9:10])
                v.drain()
                v.tensor_scalar_mul(sca[0:1, 10:11], sca[0:1, 10:11], 32767.0)
                v.drain()
                v.tensor_tensor(sca[0:1, 11:12], sca[0:1, 6:7], sca[0:1, 7:8],
                                Alu.mult)
                v.drain()
                v.tensor_scalar_mul(sca[0:1, 11:12], sca[0:1, 11:12],
                                    1.0 / (65536.0 * 32767.0))
                v.drain()
                v.tensor_tensor(sca[0:1, 12:13], sca[0:1, 0:1], sca[0:1, 2:3],
                                Alu.mult)
                v.drain()
                v.tensor_scalar_mul(sca[0:1, 12:13], sca[0:1, 12:13],
                                    1.0 / (32767.0 * 32767.0))
                v.drain()
                v.tensor_tensor(sca[0:1, 11:12], sca[0:1, 11:12],
                                sca[0:1, 12:13], Alu.mult)
                v.drain()
                v.tensor_tensor(sca[0:1, 11:12], sca[0:1, 11:12],
                                sca[0:1, 9:10], Alu.mult)
                v.drain()
                v.tensor_scalar_mul(sca[0:1, 11:12], sca[0:1, 11:12],
                                    1.0 / 32767.0)
                v.drain()
            ph.eng("vector", f_v)
        koc = p.sb("ffn_koc", [128, 1], F32, scf)
        soc = p.sb("ffn_soc", [128, 1], F32, scf)
        self.bcast_cols("ffn_b4", [(sca[0:1, 10:11], koc[:]),
                                   (sca[0:1, 11:12], soc[:])])
        gbso = p.sb("ffn_gbso", [128, 8], F32, scf)
        with p.phase("ffn_gb") as ph:
            def f_v(v):
                v.tensor_tensor(gbso[:], self.modfm[0][:, 40:48],
                                soc[:, 0:1].broadcast_to((128, 8)), Alu.mult)
                v.drain()
            ph.eng("vector", f_v)

        self._ffn_passD(koc, gbso)
        return False

    def _ffn_passA(self, xq, wname, spill, mxcol, tg):
        p, dI = self.p, self.dI
        ECH = DI // 128
        with ExitStack() as sc, p.phase(tg) as ph:
            wnat = [p.sb(tg + f"wn{j}", [128, D], F32, sc) for j in range(2)]
            srec = p.sb(tg + "srec", [128, 32], F32, sc)
            sblk = p.sb(tg + "sblk", [128, 32], F32, sc)
            ui = p.sb(tg + "ui", [128, D], I32, sc)
            wqb = [p.sb(tg + f"wqb{j}", [128, D], BF16, sc) for j in range(2)]
            wsl = [p.sb(tg + f"wsl{j}", [128, D], BF16, sc) for j in range(2)]
            ptile = [p.sb(tg + f"pt{j}", [128, T], F32, sc) for j in range(2)]
            tcol = p.sb(tg + "tc", [128, 1], F32, sc)
            ptr = p.ps(tg + "ptr", [128, D], BF16, sc)
            pmm = [p.ps(tg + f"pmm{j}", [128, TTS], F32, sc) for j in range(2)]
            sW = [tg + f"W{j}" for j in range(2)]
            sDq, sTr, sWs, sMM, sPE, sRd, sSp = (tg + x for x in
                                                 ("q", "t", "s", "m", "e", "r", "p"))

            def f_g(g):
                for e in range(ECH):
                    if e >= 2:
                        p.wait(g, sDq, e - 1)
                    p.inc(g.dma_start(out=wnat[e % 2][:],
                                      in_=dI[wname][e * 128:(e + 1) * 128, :]),
                          sW[e % 2], 16)
            ph.eng("gpsimd", f_g)

            def f_v(v):
                for e in range(ECH):
                    p.wait(v, sW[e % 2], 16 * (e // 2 + 1))
                    if e >= 2:
                        p.wait(v, sTr, 8 * (e - 1))
                    w3 = wnat[e % 2].ap().rearrange("pp (a b) -> pp a b", b=32)
                    v.tensor_reduce(sblk[:], w3, AX.X, Alu.max,
                                    apply_absolute_value=True)
                    v.drain()
                    v.tensor_scalar(sblk[:], sblk[:], 7.0e-12, 1.0 / 7.0,
                                    Alu.max, Alu.mult)
                    v.drain()
                    v.reciprocal(srec[:], sblk[:])
                    v.drain()
                    v.tensor_tensor(
                        w3, w3,
                        srec.ap().rearrange("pp a -> pp a ()").broadcast_to((128, 32, 32)),
                        Alu.mult)
                    v.drain()
                    v.tensor_copy(ui[:], wnat[e % 2][:])
                    v.drain()
                    v.tensor_scalar(wnat[e % 2][:], ui[:], -8.0, 7.0,
                                    Alu.max, Alu.min)
                    v.drain()
                    ins = v.tensor_tensor(
                        wqb[e % 2].ap().rearrange("pp (a b) -> pp a b", b=32),
                        w3,
                        sblk.ap().rearrange("pp a -> pp a ()").broadcast_to((128, 32, 32)),
                        Alu.mult)
                    v.drain()
                    p.inc(ins, sDq, 1)
                    p.wait(v, sPE, 4 * e + 4)
                    v.tensor_reduce(tcol[:], ptile[e % 2][:], AX.X, Alu.max,
                                    apply_absolute_value=True)
                    v.drain()
                    if e == 0:
                        ins = v.tensor_copy(mxcol[:], tcol[:])
                    else:
                        ins = v.tensor_tensor(mxcol[:], mxcol[:], tcol[:], Alu.max)
                    v.drain()
                    p.inc(ins, sRd, 1)
            ph.eng("vector", f_v)

            def f_t(t):
                p.wait(t, "ldc")
                for e in range(ECH):
                    p.wait(t, sDq, e + 1)
                    p.wait(t, sWs, 8 * e)
                    for k in range(NCH):
                        p.inc(t.transpose(ptr[:, k * 128:(k + 1) * 128],
                                          wqb[e % 2][:, k * 128:(k + 1) * 128],
                                          self.identb[:]), sTr, 1)
                    p.wait(t, sWs, 8 * e + 8)
                    for n in range(NT):
                        if e * NT + n >= 2:
                            p.wait(t, sPE, (e * NT + n) - 1)
                        for k in range(NCH):
                            ins = t.matmul(pmm[n % 2][:],
                                           wsl[e % 2][:, k * 128:(k + 1) * 128],
                                           xq[k][:, n * TTS:(n + 1) * TTS],
                                           start=(k == 0), stop=(k == NCH - 1))
                        p.inc(ins, sMM, 1)
            ph.eng("tensor", f_t)

            def f_sc(s):
                for e in range(ECH):
                    p.wait(s, sTr, 8 * e + 8)
                    for k in range(NCH):
                        p.inc(s.copy(wsl[e % 2][:, k * 128:(k + 1) * 128],
                                     ptr[:, k * 128:(k + 1) * 128]), sWs, 1)
                    for n in range(NT):
                        p.wait(s, sMM, e * NT + n + 1)
                        if e >= 2 and n == 0:
                            p.wait(s, sSp, 16 * (e - 1))
                        p.inc(s.copy(ptile[e % 2][:, n * TTS:(n + 1) * TTS],
                                     pmm[n % 2][:]), sPE, 1)
            ph.eng("scalar", f_sc)

            def f_sync(s):
                for e in range(ECH):
                    p.wait(s, sRd, e + 1)
                    p.inc(s.dma_start(out=spill[e * 128:(e + 1) * 128, :],
                                      in_=ptile[e % 2][:]), sSp, 16)
                p.wait(s, sSp, 16 * ECH)
            ph.eng("sync", f_sync)

    def _ffn_passB(self, kuc, kgc, sgc, mx2, mx2g):
        p = self.p
        ECH = DI // 128
        ENGOF = ["g" if e % 3 == 2 else "v" for e in range(ECH)]
        LOC, cnt = [], {"v": 0, "g": 0}
        for e in range(ECH):
            LOC.append(cnt[ENGOF[e]])
            cnt[ENGOF[e]] += 1
        NV, NG = cnt["v"], cnt["g"]
        with ExitStack() as sc, p.phase("fB") as ph:
            ENG = ("v", "g")
            pt = {en: [p.sb(f"fBp{en}{j}", [128, T], F32, sc) for j in range(2)]
                  for en in ENG}
            gt = {en: [p.sb(f"fBg{en}{j}", [128, T], F32, sc) for j in range(2)]
                  for en in ENG}
            u = {en: p.sb(f"fBu{en}", [128, T], F32, sc) for en in ENG}
            ui = {en: p.sb(f"fBi{en}", [128, T], I32, sc) for en in ENG}
            upi = {en: [p.sb(f"fBq{en}{j}", [128, T], mybir.dt.int16, sc)
                        for j in range(2)] for en in ENG}
            sig = {en: p.sb(f"fBs{en}", [128, T], F32, sc) for en in ENG}
            tcol = p.sb("fBtc", [128, 1], F32, sc)
            sL = {en: [f"fBL{en}{j}" for j in range(2)] for en in ENG}
            sV = {en: f"fBV{en}" for en in ENG}
            sS = {en: f"fBS{en}" for en in ENG}
            sSg = {en: f"fBG{en}" for en in ENG}
            sSp = {en: f"fBSp{en}" for en in ENG}
            sVr = "fBVr"

            def f_sync(s):
                spills = []
                for e in range(ECH):
                    en, j = ENGOF[e], LOC[e]
                    if len(spills) >= 2:
                        self._fB_spill(s, spills.pop(0), gt, upi, sV, sVr,
                                       sSp)
                    if j >= 2:
                        p.wait(s, sSp[en], 64 * (j - 1))
                        if en == "g":
                            p.wait(s, sVr, j - 1)
                    p.inc(s.dma_start(
                        out=pt[en][j % 2][:],
                        in_=self.scr["P"][e * 128:(e + 1) * 128, :]),
                        sL[en][j % 2], 16)
                    p.inc(s.dma_start(
                        out=gt[en][j % 2][:],
                        in_=self.scr["G"][e * 128:(e + 1) * 128, :]),
                        sL[en][j % 2], 16)
                    spills.append((e, en, j))
                for sp in spills:
                    self._fB_spill(s, sp, gt, upi, sV, sVr, sSp)
                for en in ENG:
                    p.wait(s, sSp[en], 64 * cnt[en])
            ph.eng("sync", f_sync)

            def f_sc(s):
                for e in range(ECH):
                    en, j = ENGOF[e], LOC[e]
                    p.wait(s, sS[en], j + 1)
                    s.activation(sig[en][:], gt[en][j % 2][:], Act.Sigmoid,
                                 scale=sgc[:, 0:1], bias=self.zcol[:, 0:1])
                    p.inc(s.drain(), sSg[en], 1)
            ph.eng("scalar", f_sc)

            def chunk(v, en, j):
                p.wait(v, sL[en][j % 2], 32 * (j // 2 + 1))
                if j >= 2:
                    p.wait(v, sSp[en], 64 * (j - 1))
                v.tensor_scalar_mul(u[en][:], pt[en][j % 2][:], kuc[:, 0:1])
                v.drain()
                v.tensor_copy(ui[en][:], u[en][:])
                v.drain()
                v.tensor_scalar(upi[en][j % 2][:], ui[en][:],
                                -32768.0, 32767.0, Alu.max, Alu.min)
                v.drain()
                v.tensor_scalar_mul(u[en][:], gt[en][j % 2][:], kgc[:, 0:1])
                v.drain()
                v.tensor_copy(ui[en][:], u[en][:])
                v.drain()
                v.tensor_scalar(gt[en][j % 2][:], ui[en][:],
                                -32768.0, 32767.0, Alu.max, Alu.min)
                p.inc(v.drain(), sS[en], 1)
                p.wait(v, sSg[en], j + 1)
                v.tensor_scalar_mul(u[en][:], sig[en][:], 65536.0)
                v.drain()
                v.tensor_copy(ui[en][:], u[en][:])
                v.drain()
                v.tensor_scalar(sig[en][:], ui[en][:], 0.0, 65535.0,
                                Alu.max, Alu.min)
                v.drain()
                v.tensor_tensor(gt[en][j % 2][:], gt[en][j % 2][:],
                                sig[en][:], Alu.mult)
                v.drain()
                if en == "v":
                    v.tensor_reduce(tcol[:], gt[en][j % 2][:], AX.X,
                                    Alu.max, apply_absolute_value=True)
                    v.drain()
                    if j == 0:
                        v.tensor_copy(mx2[:], tcol[:])
                    else:
                        v.tensor_tensor(mx2[:], mx2[:], tcol[:], Alu.max)
                p.inc(v.drain(), sV[en], 1)

            def f_v(v):
                tc2 = tcol
                jg = 0

                def gmax():
                    nonlocal jg
                    p.wait(v, sV["g"], jg + 1)
                    v.tensor_reduce(tc2[:], gt["g"][jg % 2][:], AX.X,
                                    Alu.max, apply_absolute_value=True)
                    v.drain()
                    if jg == 0:
                        v.tensor_copy(mx2g[:], tc2[:])
                    else:
                        v.tensor_tensor(mx2g[:], mx2g[:], tc2[:], Alu.max)
                    p.inc(v.drain(), sVr, 1)
                    jg += 1
                for j in range(NV):
                    chunk(v, "v", j)
                    if j % 2 == 1 and jg < NG:
                        gmax()
                while jg < NG:
                    gmax()
            ph.eng("vector", f_v)

            def f_g(g):
                for j in range(NG):
                    chunk(g, "g", j)
            ph.eng("gpsimd", f_g)

    def _fB_spill(self, s, sp, gt, upi, sV, sVr, sSp):
        p = self.p
        e, en, j = sp
        p.wait(s, sV[en], j + 1)
        if en == "g":
            pass
        p.inc(s.dma_start(
            out=self.scr["UP"][e * 128:(e + 1) * 128, :],
            in_=upi[en][j % 2][:]), sSp[en], 32)
        p.inc(s.dma_start(
            out=self.scr["G"][e * 128:(e + 1) * 128, :],
            in_=gt[en][j % 2][:]), sSp[en], 32)

    def _ffn_passC(self, k2c, mxo, mxog):
        p = self.p
        ECH = DI // 128
        ENGOF = ["g" if e % 3 == 2 else "v" for e in range(ECH)]
        LOC, cnt = [], {"v": 0, "g": 0}
        for e in range(ECH):
            LOC.append(cnt[ENGOF[e]])
            cnt[ENGOF[e]] += 1
        NV, NG = cnt["v"], cnt["g"]
        with ExitStack() as sc, p.phase("fC") as ph:
            ENG = ("v", "g")
            gt = {en: [p.sb(f"fCg{en}{j}", [128, T], F32, sc) for j in range(2)]
                  for en in ENG}
            upi = {en: [p.sb(f"fCu{en}{j}", [128, T], mybir.dt.int16, sc)
                        for j in range(2)] for en in ENG}
            u = {en: p.sb(f"fCw{en}", [128, T], F32, sc) for en in ENG}
            ui = {en: p.sb(f"fCi{en}", [128, T], I32, sc) for en in ENG}
            tcol = p.sb("fCtc", [128, 1], F32, sc)
            sL = {en: [f"fCL{en}{j}" for j in range(2)] for en in ENG}
            sV = {en: f"fCV{en}" for en in ENG}
            sSp = {en: f"fCSp{en}" for en in ENG}
            sVr = "fCVr"

            def f_sync(s):
                spills = []
                for e in range(ECH):
                    en, j = ENGOF[e], LOC[e]
                    if len(spills) >= 2:
                        ee, een, jj = spills.pop(0)
                        p.wait(s, sV[een], jj + 1)
                        p.inc(s.dma_start(
                            out=self.scr["P"][ee * 128:(ee + 1) * 128, :],
                            in_=gt[een][jj % 2][:]), sSp[een], 16)
                    if j >= 2:
                        p.wait(s, sSp[en], 16 * (j - 1))
                        if en == "g":
                            p.wait(s, sVr, j - 1)
                    p.inc(s.dma_start(
                        out=gt[en][j % 2][:],
                        in_=self.scr["G"][e * 128:(e + 1) * 128, :]),
                        sL[en][j % 2], 16)
                    p.inc(s.dma_start(
                        out=upi[en][j % 2][:],
                        in_=self.scr["UP"][e * 128:(e + 1) * 128, :]),
                        sL[en][j % 2], 16)
                    spills.append((e, en, j))
                for ee, een, jj in spills:
                    p.wait(s, sV[een], jj + 1)
                    p.inc(s.dma_start(
                        out=self.scr["P"][ee * 128:(ee + 1) * 128, :],
                        in_=gt[een][jj % 2][:]), sSp[een], 16)
                for en in ENG:
                    p.wait(s, sSp[en], 16 * cnt[en])
            ph.eng("sync", f_sync)

            def chunk(v, en, j):
                p.wait(v, sL[en][j % 2], 32 * (j // 2 + 1))
                v.tensor_scalar_mul(u[en][:], gt[en][j % 2][:], k2c[:, 0:1])
                v.drain()
                v.tensor_copy(ui[en][:], u[en][:])
                v.drain()
                v.tensor_scalar(u[en][:], ui[en][:], -32768.0, 32767.0,
                                Alu.max, Alu.min)
                v.drain()
                if j >= 2:
                    p.wait(v, sSp[en], 16 * (j - 1))
                v.tensor_tensor(gt[en][j % 2][:], u[en][:],
                                upi[en][j % 2][:], Alu.mult)
                v.drain()
                if en == "v":
                    v.tensor_reduce(tcol[:], gt[en][j % 2][:], AX.X,
                                    Alu.max, apply_absolute_value=True)
                    v.drain()
                    if j == 0:
                        v.tensor_copy(mxo[:], tcol[:])
                    else:
                        v.tensor_tensor(mxo[:], mxo[:], tcol[:], Alu.max)
                p.inc(v.drain(), sV[en], 1)

            def f_v(v):
                jg = 0

                def gmax():
                    nonlocal jg
                    p.wait(v, sV["g"], jg + 1)
                    v.tensor_reduce(tcol[:], gt["g"][jg % 2][:], AX.X,
                                    Alu.max, apply_absolute_value=True)
                    v.drain()
                    if jg == 0:
                        v.tensor_copy(mxog[:], tcol[:])
                    else:
                        v.tensor_tensor(mxog[:], mxog[:], tcol[:], Alu.max)
                    p.inc(v.drain(), sVr, 1)
                    jg += 1
                for j in range(NV):
                    chunk(v, "v", j)
                    if j % 2 == 1 and jg < NG:
                        gmax()
                while jg < NG:
                    gmax()
            ph.eng("vector", f_v)

            def f_g(g):
                for j in range(NG):
                    chunk(g, "g", j)
            ph.eng("gpsimd", f_g)

    def _ffn_passD(self, koc, gbso):
        p, dI = self.p, self.dI
        KCH = DI // 128
        with ExitStack() as scw:
            wdq = [p.sb(f"fDw{c}", [128, 3 * D], BF16, scw) for c in range(NCH)]
            with ExitStack() as sc, p.phase("fDw") as ph:
                wnat = [p.sb(f"fDn{j}", [128, 3 * D], F32, sc) for j in range(2)]
                srec = p.sb("fDsr", [128, 96], F32, sc)
                sblk = p.sb("fDsb", [128, 96], F32, sc)
                ui = p.sb("fDi", [128, 3 * D], I32, sc)
                wqb = [p.sb(f"fDq{j}", [128, 3 * D], BF16, sc) for j in range(2)]
                ptr = p.ps("fDptr", [128, 1024], BF16, sc)
                sW = [f"fDW{j}" for j in range(2)]
                sDq, sTr, sWs = "fDq", "fDt", "fDs"

                def f_g(g):
                    for c in range(NCH):
                        if c >= 2:
                            p.wait(g, sDq, c - 1)
                        p.inc(g.dma_start(out=wnat[c % 2][:],
                                          in_=dI["w_down"][c * 128:(c + 1) * 128, :]),
                              sW[c % 2], 16)
                ph.eng("gpsimd", f_g)

                def f_v(v):
                    for c in range(NCH):
                        p.wait(v, sW[c % 2], 16 * (c // 2 + 1))
                        if c >= 2:
                            p.wait(v, sTr, 24 * (c - 1))
                        w3 = wnat[c % 2].ap().rearrange("pp (a b) -> pp a b", b=32)
                        v.tensor_reduce(sblk[:], w3, AX.X, Alu.max,
                                        apply_absolute_value=True)
                        v.drain()
                        v.tensor_scalar(sblk[:], sblk[:], 7.0e-12, 1.0 / 7.0,
                                        Alu.max, Alu.mult)
                        v.drain()
                        v.reciprocal(srec[:], sblk[:])
                        v.drain()
                        v.tensor_tensor(
                            w3, w3,
                            srec.ap().rearrange("pp a -> pp a ()").broadcast_to((128, 96, 32)),
                            Alu.mult)
                        v.drain()
                        v.tensor_copy(ui[:], wnat[c % 2][:])
                        v.drain()
                        v.tensor_scalar(wnat[c % 2][:], ui[:], -8.0, 7.0,
                                        Alu.max, Alu.min)
                        v.drain()
                        ins = v.tensor_tensor(
                            wqb[c % 2].ap().rearrange("pp (a b) -> pp a b", b=32),
                            w3,
                            sblk.ap().rearrange("pp a -> pp a ()").broadcast_to((128, 96, 32)),
                            Alu.mult)
                        v.drain()
                        p.inc(ins, sDq, 1)
                ph.eng("vector", f_v)

                def f_t(t):
                    p.wait(t, "ldc")
                    for c in range(NCH):
                        p.wait(t, sDq, c + 1)
                        for g8 in range(3):
                            p.wait(t, sWs, (c * 3 + g8) * 8)
                            for k8 in range(8):
                                k = g8 * 8 + k8
                                p.inc(t.transpose(
                                    ptr[:, k8 * 128:(k8 + 1) * 128],
                                    wqb[c % 2][:, k * 128:(k + 1) * 128],
                                    self.identb[:]), sTr, 1)
                ph.eng("tensor", f_t)

                def f_sc(s):
                    for c in range(NCH):
                        for g8 in range(3):
                            p.wait(s, sTr, (c * 3 + g8) * 8 + 8)
                            for k8 in range(8):
                                k = g8 * 8 + k8
                                p.inc(s.copy(wdq[c][:, k * 128:(k + 1) * 128],
                                             ptr[:, k8 * 128:(k8 + 1) * 128]),
                                      sWs, 1)
                ph.eng("scalar", f_sc)

            with ExitStack() as sc, p.phase("fD") as ph:
                ot = [p.sb(f"fDo{j}", [128, TTS], F32, sc) for j in range(2)]
                obf = [p.sb(f"fDb{k}", [128, TTS], BF16, sc) for k in range(KCH)]
                u2 = p.sb("fDu2", [128, TTS], F32, sc)
                ui2 = p.sb("fDi2", [128, TTS], I32, sc)
                tres = [p.sb(f"fDt{j}", [128, TTS], F32, sc) for j in range(2)]
                pdo = [p.ps(f"fDp{j}", [128, TTS], F32, sc) for j in range(8)]
                sL = [f"fDL{j}" for j in range(2)]
                sQ, sMM, sEv, sX = "fDQ", "fDM", "fDE", "fDX"

                def f_sync(s):
                    for n in range(NT):
                        for k in range(KCH):
                            i = n * KCH + k
                            if i >= 2:
                                p.wait(s, sQ, i - 1)
                            p.inc(s.dma_start(
                                out=ot[i % 2][:],
                                in_=self.scr["P"][k * 128:(k + 1) * 128,
                                                  n * TTS:(n + 1) * TTS]),
                                sL[i % 2], 16)
                ph.eng("sync", f_sync)

                def f_v(v):
                    for n in range(NT):
                        for k in range(KCH):
                            i = n * KCH + k
                            p.wait(v, sL[i % 2], 16 * (i // 2 + 1))
                            if n >= 1 and k == 0:
                                p.wait(v, sMM, 8 * n)
                            v.tensor_scalar_mul(u2[:], ot[i % 2][:], koc[:, 0:1])
                            v.drain()
                            v.tensor_copy(ui2[:], u2[:])
                            v.drain()
                            ins = v.tensor_scalar(obf[k][:], ui2[:],
                                                  -32768.0, 32767.0,
                                                  Alu.max, Alu.min)
                            v.drain()
                            p.inc(ins, sQ, 1)
                        for c in range(NCH):
                            p.wait(v, sEv, n * NCH + c + 1)
                            ins = v.tensor_tensor(
                                self.xres[c][:, n * TTS:(n + 1) * TTS],
                                self.xres[c][:, n * TTS:(n + 1) * TTS],
                                tres[c % 2][:], Alu.add)
                            v.drain()
                            p.inc(ins, sX, 1)
                ph.eng("vector", f_v)

                def f_t(t):
                    for n in range(NT):
                        p.wait(t, sQ, (n + 1) * KCH)
                        for c in range(NCH):
                            if n >= 1:
                                p.wait(t, sEv, (n - 1) * NCH + c + 1)
                            for k in range(KCH):
                                ins = t.matmul(pdo[c][:],
                                               wdq[c][:, k * 128:(k + 1) * 128],
                                               obf[k][:],
                                               start=(k == 0), stop=(k == KCH - 1))
                            p.inc(ins, sMM, 1)
                ph.eng("tensor", f_t)

                def f_sc(s):
                    for n in range(NT):
                        for c in range(NCH):
                            p.wait(s, sMM, n * NCH + c + 1)
                            if n * NCH + c >= 2:
                                p.wait(s, sX, n * NCH + c - 1)
                            p.inc(s.activation(tres[c % 2][:], pdo[c][:],
                                               Act.Identity,
                                               scale=gbso[:, c:c + 1],
                                               bias=self.zcol[:, 0:1]), sEv, 1)
                ph.eng("scalar", f_sc)


    def l1_attn(self, hb, sc0):
        p, dI = self.p, self.dI
        with ExitStack() as sc, p.phase("l1qk") as ph:
            qw = [p.sb(f"l1qw{c}", [128, D], BF16, sc) for c in range(NCH)]
            kw = [p.sb(f"l1kw{c}", [128, D], BF16, sc) for c in range(NCH)]
            rcos = p.sb("l1rc", [128, T], F32, sc)
            rsin = p.sb("l1rs", [128, T], F32, sc)
            kb8 = p.sb("l1kb8", [128, 8], F32, sc)
            qt = p.sb("l1qt", [128, TTS], F32, sc)
            kt = p.sb("l1kt", [128, TTS], F32, sc)
            sh = p.sb("l1sh", [128, TTS], F32, sc)
            t1 = p.sb("l1t1", [128, TTS], F32, sc)
            qtl = [p.sb(f"l1qtl{j}", [128, TTS], BF16, sc) for j in range(2)]
            ktl = [p.sb(f"l1ktl{j}", [128, TTS], BF16, sc) for j in range(2)]
            ppq = p.ps("l1ppq", [128, TTS], F32, sc)
            ppk = p.ps("l1ppk", [128, TTS], F32, sc)
            sW, sM, sE, sV, sB8 = "l1qkW", "l1qkM", "l1qkE", "l1qkV", "l1qkB"
            sQs = [f"l1qkQ{j}" for j in range(2)]
            sKs = [f"l1qkK{j}" for j in range(2)]

            def f_g(g):
                for c in range(NCH):
                    p.inc(g.dma_start(out=qw[c][:],
                                      in_=dI["q_w"][c * 128:(c + 1) * 128, :]), sW, 16)
                    p.inc(g.dma_start(out=kw[c][:],
                                      in_=dI["k_w"][c * 128:(c + 1) * 128, :]), sW, 16)
                p.inc(g.dma_start(out=rcos[:], in_=dI["rot_cos"]), sW, 16)
                p.inc(g.dma_start(out=rsin[:], in_=dI["rot_sin"]), sW, 16)
            ph.eng("gpsimd", f_g)

            def f_v(v):
                p.wait(v, "ldc")
                v.tensor_scalar_mul(kb8[:], self.cols["k_b"][:], 0.125)
                p.inc(v.drain(), sB8, 1)
                for i in range(NCH * NT):
                    c, n = divmod(i, NT)
                    tl = slice(n * TTS, (n + 1) * TTS)
                    p.wait(v, sE, 2 * i + 2)
                    if i >= 2:
                        p.wait(v, sQs[i % 2], 16 * ((i - 2) // 2 + 1))
                        p.wait(v, sKs[i % 2], 16 * ((i - 2) // 2 + 1))
                    for src_, dst in ((qt, qtl), (kt, ktl)):
                        v.stream_shuffle(sh[:], src_[:], [j ^ 1 for j in range(32)])
                        v.drain()
                        v.tensor_tensor(t1[:], src_[:], rcos[:, tl], Alu.mult)
                        v.drain()
                        v.tensor_tensor(sh[:], sh[:], rsin[:, tl], Alu.mult)
                        v.drain()
                        v.tensor_tensor(dst[i % 2][:], t1[:], sh[:], Alu.add)
                        v.drain()
                    p.inc(v.drain(), sV, 1)
            ph.eng("vector", f_v)

            def f_sync(s):
                for i in range(NCH * NT):
                    c, n = divmod(i, NT)
                    tl = slice(n * TTS, (n + 1) * TTS)
                    p.wait(s, sV, i + 1)
                    p.inc(s.dma_start(out=self.scr["kq"][c * 128:(c + 1) * 128, tl],
                                      in_=qtl[i % 2][:]), sQs[i % 2], 16)
                    p.inc(s.dma_start(
                        out=self.scr["kq"][D + c * 128:D + (c + 1) * 128, tl],
                        in_=ktl[i % 2][:]), sKs[i % 2], 16)
                for j in range(2):
                    p.wait(s, sQs[j])
                    p.wait(s, sKs[j])
            ph.eng("sync", f_sync)

            def f_t(t):
                p.wait(t, sW)
                for i in range(NCH * NT):
                    c, n = divmod(i, NT)
                    if i >= 1:
                        p.wait(t, sE, 2 * i)
                    for k in range(NCH):
                        ins = t.matmul(ppq[:], qw[k][:, c * 128:(c + 1) * 128],
                                       hb[k][:, n * TTS:(n + 1) * TTS],
                                       start=(k == 0), stop=(k == NCH - 1))
                    p.inc(ins, sM, 1)
                    for k in range(NCH):
                        ins = t.matmul(ppk[:], kw[k][:, c * 128:(c + 1) * 128],
                                       hb[k][:, n * TTS:(n + 1) * TTS],
                                       start=(k == 0), stop=(k == NCH - 1))
                    p.inc(ins, sM, 1)
            ph.eng("tensor", f_t)

            def f_sc(s):
                p.wait(s, sB8, 1)
                for i in range(NCH * NT):
                    c, n = divmod(i, NT)
                    p.wait(s, sM, 2 * i + 1)
                    if i >= 1:
                        p.wait(s, sV, i)
                    s.activation(qt[:], ppq[:], Act.Identity,
                                 bias=self.cols["q_b"][:, c:c + 1])
                    p.inc(s.drain(), sE, 1)
                    p.wait(s, sM, 2 * i + 2)
                    s.activation(kt[:], ppk[:], Act.Identity, scale=0.125,
                                 bias=kb8[:, c:c + 1])
                    p.inc(s.drain(), sE, 1)
            ph.eng("scalar", f_sc)

        with ExitStack() as sc, p.phase("l1sc") as ph:
            vw = [p.sb(f"l1vw{c}", [128, 2 * D], BF16, sc) for c in range(NCH)]
            dlt = p.sb("l1dlt", [128, NH * CRET], F32, sc)
            alph = p.sb("l1al", [128, NH], F32, sc)
            beta = p.sb("l1be", [128, NH * CRET], F32, sc)
            vbr = p.sb("l1vbr", [1, 2 * D], F32, sc)
            sall = p.sb("l1S", [128, (NH // 2) * CRET], F32, sc)
            zall = p.sb("l1z", [128, NH // 2], F32, sc)
            vtm = [p.sb(f"l1vt{j}", [128, 2 * D], BF16, sc) for j in range(2)]
            qrc = [p.sb(f"l1qrc{j}", [128, D], BF16, sc) for j in range(2)]
            krc = [p.sb(f"l1krc{j}", [128, D], BF16, sc) for j in range(2)]
            smk = p.sb("l1smk", [128, CRET], BF16, sc)
            qrsf = p.sb("l1qrsf", [128, CRET], F32, sc)
            ktm = p.sb("l1ktm", [128, 64], BF16, sc)
            outsb = p.sb("l1out", [128, CRET], F32, sc)
            zst = p.sb("l1zst", [1, NH * CRET], F32, sc)
            ppv = [p.ps("l1pv0", [128, TTS], F32, sc)]
            psT = p.ps("l1psT", [128, CRET], F32, sc)
            psK = p.ps("l1psK", [128, 64], BF16, sc)
            psA = p.ps("l1psA", [128, CRET], F32, sc)
            psB = p.ps("l1psB", [128, CRET], F32, sc)
            psD = p.ps("l1psD", [128, 136], F32, sc)
            psZ1 = p.ps("l1psZ1", [1, CRET], F32, sc)
            psZ2 = p.ps("l1psZ2", [1, CRET], F32, sc)
            (sW, sVM, sVE, sST, sMK, sKT, sKE, sQS, sOA, sOC, sDm, sSU, sZm,
             sZC, sSp, sZD, sZG) = ("l1s" + x for x in
                                    ("W", "VM", "VE", "ST", "MK", "KT", "KE",
                                     "QS", "OA", "OC", "Dm", "SU", "Zm", "ZC",
                                     "Sp", "ZD", "ZG"))
            sQL = [f"l1sQL{j}" for j in range(2)]
            sAE, sZE = "l1sAE", "l1sZE"

            def f_g(g):
                for c in range(NCH):
                    p.inc(g.dma_start(out=vw[c][:],
                                      in_=dI["v_w"][c * 128:(c + 1) * 128, :]), sW, 16)
                p.inc(g.dma_start(out=dlt[:], in_=dI["dlt"]), sW, 16)
                p.inc(g.dma_start(out=alph[:], in_=dI["alpha"]), sW, 16)
                p.inc(g.dma_start(out=beta[:], in_=dI["beta_bc"]), sW, 16)
                p.inc(g.dma_start(out=vbr[:], in_=dI["v_b_row"]), sW, 16)
                g.memset(sall[:], 0.0)
                g.memset(zall[:], 0.0)
                p.inc(g.drain(), sW + "z", 1)
            ph.eng("gpsimd", f_g)

            def f_t(t):
                p.wait(t, sW)
                p.wait(t, sW + "z", 1)
                for i in range(NH):
                    ci = slice(i * CRET, (i + 1) * CRET)
                    p.wait(t, sQL[i % 2], 16 * 16 * (i // 2 + 1))
                    for e in range(4):
                        ie = 4 * i + e
                        if ie >= 1:
                            p.wait(t, sVE, ie)
                        for k in range(NCH):
                            ins = t.matmul(ppv[0][:], hb[k][:, ci],
                                           vw[k][:, e * TTS:(e + 1) * TTS],
                                           start=(k == 0), stop=False)
                        ins = t.matmul(ppv[0][:], self.onesr[:],
                                       vbr[0:1, e * TTS:(e + 1) * TTS],
                                       start=False, stop=True)
                        p.inc(ins, sVM, 1)
                    for h in range(NH):
                        j = i * NH + h
                        hrow = slice((h % 2) * 64, (h % 2) * 64 + 64)
                        hcol = slice(h * CRET, (h + 1) * CRET)
                        if j >= 1:
                            p.wait(t, sMK, j)
                        ins = t.matmul(psT[:], krc[i % 2][hrow, (h // 2) * 128:(h // 2) * 128 + 128],
                                       qrc[i % 2][hrow, (h // 2) * 128:(h // 2) * 128 + 128],
                                       start=True, stop=True)
                        p.inc(ins, sST, 1)
                        if j >= 1:
                            p.wait(t, sKE, j)
                        idsl = self.identb[hrow, hrow]
                        ins = t.transpose(psK[:, 0:64],
                                          krc[i % 2][hrow, (h // 2) * 128:(h // 2) * 128 + 128],
                                          idsl)
                        p.inc(ins, sKT, 1)
                        p.wait(t, sMK, j + 1)
                        p.wait(t, sVE, 4 * i + 4)
                        if j >= 1:
                            p.wait(t, sOC, j)
                        ins = t.matmul(psA[:], vtm[i % 2][:, hcol], smk[:],
                                       start=True, stop=True)
                        p.inc(ins, sOA, 1)
                        p.wait(t, sQS, j + 1)
                        if i >= 1:
                            p.wait(t, sSU, (i - 1) * NH + h + 1)
                        scol = slice((h // 2) * CRET, (h // 2 + 1) * CRET)
                        ins = t.matmul(psB[:], sall[hrow, scol], qrsf[hrow, :],
                                       start=True, stop=True)
                        p.inc(ins, sOA, 1)
                        if j >= 1:
                            p.wait(t, sZC, j)
                        ins = t.matmul(psZ1[0:1, :], self.onesb[:], smk[:],
                                       start=True, stop=True)
                        ins = t.matmul(psZ2[0:1, :],
                                       zall[hrow, h // 2:h // 2 + 1],
                                       qrsf[hrow, :], start=True, stop=True)
                        p.inc(ins, sZm, 1)
                        p.wait(t, sKE, j + 1)
                        if j >= 1:
                            p.wait(t, sSU, j)
                        prow = slice((h % 2) * 64, (h % 2) * 64 + 64)
                        ins = t.matmul(psD[prow, 0:128], ktm[:, 0:64],
                                       vtm[i % 2][:, hcol], start=True, stop=True)
                        ins = t.matmul(psD[prow, 128:129], ktm[:, 0:64],
                                       self.onesb[:], start=True, stop=True)
                        p.inc(ins, sDm, 1)
            ph.eng("tensor", f_t)

            def f_sc(s):
                for i in range(NH):
                    for e in range(4):
                        p.wait(s, sVM, 4 * i + e + 1)
                        if i >= 2 and e == 0:
                            p.wait(s, sDm, (i - 1) * NH)
                        p.inc(s.copy(vtm[i % 2][:, e * TTS:(e + 1) * TTS],
                                     ppv[0][:]), sVE, 1)
                    for h in range(NH):
                        j = i * NH + h
                        hcol2 = slice(h * CRET, (h + 1) * CRET)
                        p.wait(s, sKT, j + 1)
                        p.inc(s.activation(ktm[:, 0:64], psK[:, 0:64], Act.Copy,
                                           scale=alph[:, h:h + 1]), sKE, 1)
                        p.wait(s, sOA, 2 * j + 1)
                        if j >= 1:
                            p.wait(s, sSp, 16 * j)
                        p.inc(s.copy(outsb[:], psA[:]), sAE, 1)
                        p.wait(s, sZm, j + 1)
                        if i >= 1:
                            p.wait(s, sZG, 16 * i)
                        p.inc(s.copy(zst[0:1, hcol2], psZ1[0:1, :]), sZE, 1)
            ph.eng("scalar", f_sc)

            def f_v(v):
                for i in range(NH):
                    ci = slice(i * CRET, (i + 1) * CRET)
                    p.wait(v, sQL[i % 2], 16 * 16 * (i // 2 + 1))
                    for h in range(NH):
                        j = i * NH + h
                        hrow = slice((h % 2) * 64, (h % 2) * 64 + 64)
                        hcol = slice(h * CRET, (h + 1) * CRET)
                        p.wait(v, sST, j + 1)
                        ins = v.tensor_tensor(smk[:], psT[:], dlt[:, hcol],
                                              Alu.mult)
                        v.drain()
                        p.inc(ins, sMK, 1)
                        ins = v.tensor_tensor(
                            qrsf[hrow, :],
                            qrc[i % 2][hrow, (h // 2) * 128:(h // 2) * 128 + 128],
                            beta[hrow, hcol], Alu.mult)
                        v.drain()
                        p.inc(ins, sQS, 1)
                        p.wait(v, sOA, 2 * j + 2)
                        p.wait(v, sAE, j + 1)
                        ins = v.tensor_tensor(outsb[:], outsb[:], psB[:], Alu.add)
                        v.drain()
                        p.inc(ins, sOC, 1)
                        p.wait(v, sZm, j + 1)
                        p.wait(v, sZE, j + 1)
                        ins = v.tensor_tensor(zst[0:1, hcol],
                                              zst[0:1, hcol], psZ2[0:1, :], Alu.add)
                        v.drain()
                        p.inc(ins, sZC, 1)
                        p.wait(v, sDm, j + 1)
                        lam = float(CONSTS["lam_pc"][h])
                        scol = slice((h // 2) * CRET, (h // 2 + 1) * CRET)
                        zcl = slice(h // 2, h // 2 + 1)
                        prow = slice((h % 2) * 64, (h % 2) * 64 + 64)
                        v.tensor_scalar_mul(sall[hrow, scol], sall[hrow, scol], lam)
                        v.tensor_scalar_mul(zall[hrow, zcl], zall[hrow, zcl], lam)
                        v.drain()
                        v.tensor_tensor(sall[hrow, scol], sall[hrow, scol],
                                        psD[prow, 0:128], Alu.add)
                        ins = v.tensor_tensor(zall[hrow, zcl], zall[hrow, zcl],
                                              psD[prow, 128:129], Alu.add)
                        v.drain()
                        p.inc(ins, sSU, 1)
                    p.inc(v.drain(), sZD, 1)
            ph.eng("vector", f_v)

            def f_sync(s):
                def loads(i):
                    ci2 = slice(i * CRET, (i + 1) * CRET)
                    for k in range(NCH):
                        p.inc(s.dma_start(out=qrc[i % 2][:, k * 128:(k + 1) * 128],
                                          in_=self.scr["kq"][k * 128:(k + 1) * 128, ci2]),
                              sQL[i % 2], 16)
                        p.inc(s.dma_start(out=krc[i % 2][:, k * 128:(k + 1) * 128],
                                          in_=self.scr["kq"][D + k * 128:D + (k + 1) * 128, ci2]),
                              sQL[i % 2], 16)
                loads(0)
                loads(1)
                for i in range(NH):
                    for h in range(NH):
                        j = i * NH + h
                        p.wait(s, sOC, j + 1)
                        p.inc(s.dma_start(
                            out=self.scr["O"][h * CRET:(h + 1) * CRET,
                                              i * CRET:(i + 1) * CRET],
                            in_=outsb[:]), sSp, 16)
                    p.wait(s, sZD, i + 1)
                    p.inc(s.dma_start(out=self.scr["row"][40 + i:41 + i, :],
                                      in_=zst[:]), sZG, 16)
                    if i + 2 < NH:
                        p.wait(s, sDm, (i + 1) * NH)
                        loads(i + 2)
                p.wait(s, sSp, 16 * NH * NH)
                p.wait(s, sZG, 16 * NH)
            ph.eng("sync", f_sync)

        sc_st = sc0.enter_context(ExitStack())
        srows = p.sb("l1srows", [NH, T], F32, sc_st)
        qrows = p.sb("l1qrows", [NH, T], F32, sc_st)
        zrows = p.sb("l1zrows", [NH, T], F32, sc_st)

        with p.phase("l1zg") as ph:
            def f_sync(s):
                src = bass.AP(self.scr["row"].tensor, 40 * T,
                              [[128, NH], [T, NH], [1, CRET]])
                p.inc(s.dma_start(out=zrows.ap().rearrange(
                    "hh (ii r) -> hh ii r", r=CRET), in_=src), "l1zgD", 16)
                p.wait(s, "l1zgD", 16)
            ph.eng("sync", f_sync)

        with ExitStack() as sc, p.phase("l1st") as ph:
            ot = [p.sb(f"l1ot{j}", [128, TTS], F32, sc) for j in range(2)]
            sqt = p.sb("l1sqt", [128, TTS], F32, sc)
            ehot = p.sb("l1eh", [128, NH * NH], F32, sc)
            psS = p.ps("l1psS", [NH, TTS], F32, sc)
            psQ = p.ps("l1psQ", [NH, TTS], F32, sc)
            sW, sQ2, sMS, sEV = "l1tW", "l1tQ", "l1tM", "l1tE"
            sLs = [f"l1tL{j}" for j in range(2)]

            def f_g(g):
                p.inc(g.dma_start(out=ehot[:], in_=dI["ehot"]), sW, 16)
            ph.eng("gpsimd", f_g)

            def f_sync(s):
                for n in range(NT):
                    for h in range(NH):
                        i = n * NH + h
                        if i >= 2:
                            p.wait(s, sQ2, i - 1)
                            p.wait(s, sMS, i - 1)
                        p.inc(s.dma_start(
                            out=ot[i % 2][:],
                            in_=self.scr["O"][h * CRET:(h + 1) * CRET,
                                              n * TTS:(n + 1) * TTS]),
                            sLs[i % 2], 16)
            ph.eng("sync", f_sync)

            def f_sc(s):
                for n in range(NT):
                    for h in range(NH):
                        i = n * NH + h
                        p.wait(s, sLs[i % 2], 16 * (i // 2 + 1))
                        if i >= 1:
                            p.wait(s, sMS, i)
                        p.inc(s.activation(sqt[:], ot[i % 2][:], Act.Square),
                              sQ2, 1)
            ph.eng("scalar", f_sc)

            def f_t(t):
                p.wait(t, sW)
                for n in range(NT):
                    if n >= 1:
                        p.wait(t, sEV, n)
                    for h in range(NH):
                        i = n * NH + h
                        p.wait(t, sQ2, i + 1)
                        ins = t.matmul(psS[:], ehot[:, h * NH:(h + 1) * NH],
                                       ot[i % 2][:], start=(h == 0),
                                       stop=(h == NH - 1))
                        ins = t.matmul(psQ[:], ehot[:, h * NH:(h + 1) * NH],
                                       sqt[:], start=(h == 0), stop=(h == NH - 1))
                        p.inc(ins, sMS, 1)
            ph.eng("tensor", f_t)

            def f_v(v):
                for n in range(NT):
                    p.wait(v, sMS, (n + 1) * NH)
                    tl = slice(n * TTS, (n + 1) * TTS)
                    v.tensor_copy(srows[:, tl], psS[:])
                    ins = v.tensor_copy(qrows[:, tl], psQ[:])
                    v.drain()
                    p.inc(ins, sEV, 1)
            ph.eng("vector", f_v)

        with ExitStack() as sc, p.phase("l1rw") as ph:
            rnorm = p.sb("l1rn", [NH, T], F32, sc)
            dd = p.sb("l1dd", [NH, T], F32, sc)
            sig = p.sb("l1sig", [NH, T], F32, sc)
            mu = p.sb("l1mu", [NH, T], F32, sc)
            var = p.sb("l1var", [NH, T], F32, sc)
            aba = p.sb("l1abra", [NH, T], F32, sc)
            abb = p.sb("l1abrb", [NH, T], F32, sc)
            sW, sV1, sS1, sD1 = "l1rW", "l1rV", "l1rS", "l1rD"

            def f_g(g):
                p.inc(g.dma_start(out=rnorm[:], in_=dI["rnorm"]), sW, 16)
            ph.eng("gpsimd", f_g)

            def f_v(v):
                p.wait(v, sW)
                v.tensor_tensor(dd[:], zrows[0:NH, :], rnorm[:], Alu.mult)
                v.drain()
                v.tensor_scalar_mul(sig[:], dd[:], -1.0)
                v.drain()
                v.tensor_tensor(dd[:], dd[:], sig[:], Alu.max)
                v.drain()
                v.tensor_scalar_max(dd[:], dd[:], 1.0)
                v.drain()
                v.reciprocal(dd[:], dd[:])
                v.drain()
                v.tensor_tensor(sig[:], rnorm[:], dd[:], Alu.mult)
                v.drain()
                v.tensor_scalar_mul(mu[:], srows[:], 1.0 / DVH)
                v.tensor_scalar_mul(var[:], qrows[:], 1.0 / DVH)
                v.drain()
                v.tensor_tensor(dd[:], mu[:], mu[:], Alu.mult)
                v.drain()
                v.tensor_tensor(var[:], var[:], dd[:], Alu.subtract)
                v.drain()
                v.tensor_tensor(dd[:], sig[:], sig[:], Alu.mult)
                v.drain()
                v.tensor_tensor(var[:], var[:], dd[:], Alu.mult)
                v.drain()
                v.tensor_scalar_add(var[:], var[:], EPS)
                v.drain()
                v.reciprocal(var[:], var[:])
                p.inc(v.drain(), sV1, 1)
                p.wait(v, sS1, 1)
                v.tensor_tensor(aba[:], sig[:], var[:], Alu.mult)
                v.drain()
                v.tensor_tensor(abb[:], mu[:], aba[:], Alu.mult)
                v.drain()
                v.tensor_scalar_mul(abb[:], abb[:], -1.0)
                v.drain()
                p.inc(v.drain(), sD1, 1)
            ph.eng("vector", f_v)

            def f_sc(s):
                p.wait(s, sV1, 1)
                p.inc(s.activation(var[:], var[:], Act.Sqrt), sS1, 1)
            ph.eng("scalar", f_sc)

            def f_sync(s):
                p.wait(s, sD1, 1)
                p.inc(s.dma_start(out=self.scr["row"][0:NH, :], in_=aba[:]),
                      "l1rSp", 16)
                p.inc(s.dma_start(out=self.scr["row"][NH:2 * NH, :], in_=abb[:]),
                      "l1rSp2", 16)
                p.wait(s, "l1rSp", 16)
                p.wait(s, "l1rSp2", 16)
            ph.eng("sync", f_sync)

        sc_st.close()

        with ExitStack() as sc, p.phase("l1f") as ph:
            gw = [p.sb(f"l1gw{c}", [128, 2 * D], BF16, sc) for c in range(NCH)]
            ow = [p.sb(f"l1ow{c}", [128, D], BF16, sc) for c in range(2 * NCH)]
            gb16 = p.sb("l1gb16", [128, 16], F32, sc)
            gaob = p.sb("l1gaob", [128, 8], F32, sc)
            gsil = p.sb("l1gs", [128, NH * TTS], BF16, sc)
            ot = [p.sb(f"l1fo{j}", [128, TTS], F32, sc) for j in range(2)]
            abca = [p.sb(f"l1aba{j}", [128, TTS], F32, sc) for j in range(2)]
            abcb = [p.sb(f"l1abb{j}", [128, TTS], F32, sc) for j in range(2)]
            fus = p.sb("l1fu", [128, TTS], BF16, sc)
            tre = [p.sb(f"l1tr{j}", [128, TTS], F32, sc) for j in range(2)]
            psO = [p.ps(f"l1pO{c}", [128, TTS], F32, sc) for c in range(8)]
            (sW, sGM, sGE, sFu, sOM, sOE, sXm, sV0) = (
                "l1f" + x for x in ("W", "GM", "GE", "Fu", "OM", "OE", "Xm", "V"))
            sLo = [f"l1fLo{j}" for j in range(2)]
            sLa = [f"l1fLa{j}" for j in range(2)]
            sLb = [f"l1fLb{j}" for j in range(2)]

            def f_g(g):
                for c in range(NCH):
                    p.inc(g.dma_start(out=gw[c][:],
                                      in_=dI["g_w"][c * 128:(c + 1) * 128, :]), sW, 16)
                for c in range(2 * NCH):
                    p.inc(g.dma_start(out=ow[c][:],
                                      in_=dI["o_w"][c * 128:(c + 1) * 128, :]), sW, 16)
                p.inc(g.dma_start(out=gb16[:], in_=dI["g_b"]), sW, 16)
            ph.eng("gpsimd", f_g)

            def f_sync(s):
                for n in range(NT):
                    for h in range(NH):
                        i = n * NH + h
                        if i >= 2:
                            p.wait(s, sFu, i - 1)
                        p.inc(s.dma_start(
                            out=ot[i % 2][:],
                            in_=self.scr["O"][h * CRET:(h + 1) * CRET,
                                              n * TTS:(n + 1) * TTS]),
                            sLo[i % 2], 16)
                        p.inc(s.dma_start(
                            out=abca[i % 2][:],
                            in_=self.scr["row"][h:h + 1, n * TTS:(n + 1) * TTS]
                            .partition_broadcast(128)), sLa[i % 2], 16)
                        p.inc(s.dma_start(
                            out=abcb[i % 2][:],
                            in_=self.scr["row"][NH + h:NH + h + 1,
                                                n * TTS:(n + 1) * TTS]
                            .partition_broadcast(128)), sLb[i % 2], 16)
            ph.eng("sync", f_sync)

            def f_v(v):
                p.wait(v, "ldc")
                v.tensor_tensor(gaob[:], self.modfm[1][:, 16:24],
                                self.cols["o_b"][:], Alu.mult)
                p.inc(v.drain(), sV0, 1)
                for n in range(NT):
                    for h in range(NH):
                        i = n * NH + h
                        p.wait(v, sLo[i % 2], 16 * (i // 2 + 1))
                        p.wait(v, sLa[i % 2], 16 * (i // 2 + 1))
                        p.wait(v, sLb[i % 2], 16 * (i // 2 + 1))
                        p.wait(v, sGE, i + 1)
                        v.tensor_tensor(ot[i % 2][:], ot[i % 2][:],
                                        abca[i % 2][:], Alu.mult)
                        v.drain()
                        v.tensor_tensor(ot[i % 2][:], ot[i % 2][:],
                                        abcb[i % 2][:], Alu.add)
                        v.drain()
                        if i >= 1:
                            p.wait(v, sOM, i)
                        ins = v.tensor_tensor(fus[:], ot[i % 2][:],
                                              gsil[:, h * TTS:(h + 1) * TTS],
                                              Alu.mult)
                        v.drain()
                        p.inc(ins, sFu, 1)
                    for c in range(NCH):
                        p.wait(v, sOE, n * NCH + c + 1)
                        ins = v.tensor_tensor(
                            self.xres[c][:, n * TTS:(n + 1) * TTS],
                            self.xres[c][:, n * TTS:(n + 1) * TTS],
                            tre[c % 2][:], Alu.add)
                        v.drain()
                        p.inc(ins, sXm, 1)
            ph.eng("vector", f_v)

            def f_t(t):
                p.wait(t, sW)
                for n in range(NT):
                    tl = slice(n * TTS, (n + 1) * TTS)
                    for h in range(NH):
                        i = n * NH + h
                        if i >= 2:
                            p.wait(t, sGE, i - 1)
                        if n >= 1 and h < 2:
                            p.wait(t, sOE, (n - 1) * NCH + h + 1)
                        for k in range(NCH):
                            ins = t.matmul(psO[h % 2][:],
                                           gw[k][:, h * 128:(h + 1) * 128],
                                           hb[k][:, tl],
                                           start=(k == 0), stop=(k == NCH - 1))
                        p.inc(ins, sGM, 1)
                    p.wait(t, sGE, (n + 1) * NH)
                    for h in range(NH):
                        i = n * NH + h
                        p.wait(t, sFu, i + 1)
                        for c in range(NCH):
                            if n >= 1 and h == 0:
                                p.wait(t, sOE, (n - 1) * NCH + c + 1)
                            ins = t.matmul(psO[c][:],
                                           ow[h][:, c * 128:(c + 1) * 128],
                                           fus[:], start=(h == 0),
                                           stop=(h == NH - 1))
                        p.inc(ins, sOM, 1)
            ph.eng("tensor", f_t)

            def f_sc(s):
                p.wait(s, sV0, 1)
                for n in range(NT):
                    for h in range(NH):
                        i = n * NH + h
                        p.wait(s, sGM, i + 1)
                        if n >= 1:
                            p.wait(s, sFu, (n - 1) * NH + h + 1)
                        p.inc(s.activation(gsil[:, h * TTS:(h + 1) * TTS],
                                           psO[h % 2][:], Act.Silu,
                                           bias=gb16[:, h:h + 1]), sGE, 1)
                    for c in range(NCH):
                        p.wait(s, sOM, (n + 1) * NH)
                        if n * NCH + c >= 2:
                            p.wait(s, sXm, n * NCH + c - 1)
                        p.inc(s.activation(tre[c % 2][:], psO[c][:], Act.Identity,
                                           scale=self.modfm[1][:, 16 + c:17 + c],
                                           bias=gaob[:, c:c + 1]), sOE, 1)
            ph.eng("scalar", f_sc)


    def l1_moe(self, hb, scm):
        p, dI = self.p, self.dI
        MATS = ("1g", "1l", "3g", "3l")
        NEH = 18
        TH = T // 2
        routed_upto = [sum(1 for x in range(eh + 1) if x % 9 != 8)
                       for eh in range(NEH)]
        with ExitStack() as sc:
            he = [p.sb(f"mxhe{k}", [128, TH], BF16, sc) for k in range(16)]
            wsl = {m: [p.sb(f"mxw{m}{j}", [128, 1024], BF16, sc)
                       for j in range(2)] for m in MATS}
            bcol = {m: [p.sb(f"mxb{m}{j}", [128, 16], F32, sc)
                        for j in range(2)] for m in MATS}
            b2c = [p.sb(f"mxb2{j}", [128, NCH], F32, sc) for j in range(2)]
            w2b = [p.sb(f"mxu{j}", [128, D], BF16, sc) for j in range(4)]
            tb = {nm: [p.sb(f"mxt{nm}{j}", [128, TTS], F32, sc)
                       for j in range(2)] for nm in ("1", "2", "3", "4")}
            cbb = [p.sb(f"mxc{j}", [128, TTS], F32, sc) for j in range(2)]
            dot = [p.sb(f"mxd{j}", [128, TTS], F32, sc) for j in range(4)]
            gwt = p.sb("mxgw", [128, NCH * NE], F32, sc)
            gbb = p.sb("mxgb", [128, NE], F32, sc)
            hfc = [p.sb(f"mxh{j}", [128, 1024], F32, sc) for j in range(2)]
            sco = p.sb("mxsc", [128, NE], F32, sc)
            bia = p.sb("mxbi", [128, NE], F32, sc)
            m1 = p.sb("mxm1", [128, 1], F32, sc)
            cmb = p.sb("mxcm", [128, NE], F32, sc)
            cfm = p.sb("mxcf", [NE, T], F32, sc)
            psA = [p.ps(f"mxp{j}", [128, TTS], F32, sc) for j in range(8)]
            acc_d = self.scr["x"]

            (gW, gK, gVe, gVx, gV, gTr, gTe, gSp) = (
                "mx" + x for x in ("gW", "gK", "gVe", "gVx", "gV", "gTr",
                                   "gTe", "gSp"))
            gHf = ["mxgH0", "mxgH1"]
            (sWb, sMM, sEv, sG, sS, sSw, sDM, sDE, sDC) = (
                "mx" + x for x in ("Wb", "MM", "Ev", "G", "S", "Hw", "DM",
                                   "DE", "DC"))
            sAc = ["mxAc0", "mxAc1", "mxAc2", "mxAc3"]
            sWl = ["mxWl0", "mxWl1"]
            sW2 = ["mxW20", "mxW21", "mxW22", "mxW23"]
            sCb = ["mxCb0", "mxCb1"]

            def HLs(hh, n2):
                return slice((hh * 2 + n2) * TTS, (hh * 2 + n2 + 1) * TTS)

            def LLs(n2):
                return slice(n2 * TTS, (n2 + 1) * TTS)

            def wsrc(shared, e, m, i):
                nm = ("sw" if shared else "w") + m[0] + "_" + m[1]
                r0 = ((0 if shared else e * 16) + i) * 128
                return dI[nm][r0:r0 + 128, :]

            with p.phase("mx") as ph:
                def wsl_load(g, eh, i):
                    e_ = eh % 9
                    sh = e_ == NE
                    gi = eh * 16 + i
                    if gi >= 2:
                        p.wait(g, sMM, 8 * (gi - 1))
                    for m in MATS:
                        p.inc(g.dma_start(out=wsl[m][gi % 2][:],
                                          in_=wsrc(sh, e_, m, i)),
                              sWl[gi % 2], 16)

                def f_g(g):
                    p.inc(g.dma_start(out=gwt[:], in_=dI["gate_wp"]), gW, 16)
                    p.inc(g.dma_start(out=gbb[:], in_=dI["gate_bbc"]), gW, 16)

                    def biases(eh):
                        e_ = eh % 9
                        sh = e_ == NE
                        if eh >= 2:
                            p.wait(g, sEv, 128 * (eh - 1))
                            p.wait(g, sDE, 16 * (eh - 1))
                        for m in MATS:
                            bn = (("sb" if sh else "b") + m[0] + "_"
                                  + m[1] + "_c")
                            src = (dI[bn] if sh
                                   else dI[bn][e_ * 128:(e_ + 1) * 128, :])
                            p.inc(g.dma_start(out=bcol[m][eh % 2][:], in_=src),
                                  sWb, 16)
                        if not sh:
                            p.inc(g.dma_start(
                                out=b2c[eh % 2][:],
                                in_=dI["b2_c"][e_ * 128:(e_ + 1) * 128, :]),
                                sWb, 16)

                    def w2load(eh, ks):
                        e_ = eh % 9
                        sh = e_ == NE
                        for kk in ks:
                            gik = eh * 32 + kk
                            if gik >= 4:
                                p.wait(g, sDM, gik - 3)
                            r0 = (0 if sh else e_ * MI) + (kk % 16) * 128
                            nm = "sw2" if sh else "w2"
                            p.inc(g.dma_start(
                                out=w2b[gik % 4][:],
                                in_=dI[nm][r0:r0 + 128, :]),
                                sW2[gik % 4], 16)

                    biases(0)
                    wsl_load(g, 0, 0)
                    wsl_load(g, 0, 1)
                    for eh in range(NEH):
                        w2load(eh, range(0, 4))
                        for i in range(2, 16):
                            wsl_load(g, eh, i)
                        w2load(eh, range(4, 16))
                        self._moe_accum(g, eh, 0, acc_d, dot, sDC, sAc, HLs)
                        if eh + 1 < NEH:
                            biases(eh + 1)
                            wsl_load(g, eh + 1, 0)
                            wsl_load(g, eh + 1, 1)
                        w2load(eh, range(16, 32))
                        self._moe_accum(g, eh, 1, acc_d, dot, sDC, sAc, HLs)
                ph.eng("gpsimd", f_g)

                def f_sy(s):
                    for q in range(16):
                        if q >= 2:
                            p.wait(s, gK, q - 1)
                        k, hf = q // 2, q % 2
                        p.inc(s.dma_start(
                            out=hfc[q % 2][:],
                            in_=self.scr["P"][k * 128:(k + 1) * 128,
                                              hf * 1024:(hf + 1) * 1024]),
                            gHf[q % 2], 16)
                    p.wait(s, gTe, 16)
                    p.inc(s.dma_start(out=self.scr["comb"], in_=cfm[:]),
                          gSp, 16)
                    for eh in range(NEH):
                        hh, e = divmod(eh, 9)
                        if e != NE:
                            for n2 in range(2):
                                if eh == 0:
                                    p.wait(s, gSp, 16)
                                else:
                                    p.wait(s, sDC,
                                           (eh - 1) * 16 + n2 * 8 + 8)
                                p.inc(s.dma_start(
                                    out=cbb[n2][:],
                                    in_=self.scr["comb"][e:e + 1, HLs(hh, n2)]
                                    .partition_broadcast(128)), sCb[n2], 16)
                ph.eng("sync", f_sy)

                def f_t(t):
                    p.wait(t, gW, 32)
                    for q in range(16):
                        k, hf = q // 2, q % 2
                        p.wait(t, gHf[q % 2], 16 * (q // 2 + 1))
                        for it8 in range(8):
                            ig = hf * 8 + it8
                            ins = t.matmul(psA[0][:, ig * 8:(ig + 1) * 8],
                                           hfc[q % 2][:, it8 * 128:(it8 + 1) * 128],
                                           gwt[:, k * 8:(k + 1) * 8],
                                           start=(q == 0 and it8 == 0),
                                           stop=(q == 15 and it8 == 7))
                        p.inc(ins, gK, 1)
                    for ig in range(16):
                        p.wait(t, gV, ig + 1)
                        if ig >= 1:
                            p.wait(t, gTe, ig)
                        p.inc(t.transpose(psA[1][0:NE, 0:128], cmb[:],
                                          self.identf[:]), gTr, 1)
                    for eh in range(NEH):
                        hh, e = divmod(eh, 9)
                        for i in range(16):
                            gi = eh * 16 + i
                            p.wait(t, sWl[gi % 2], 64 * (gi // 2 + 1))
                            for n2 in range(2):
                                it = i * 2 + n2
                                if it <= 1:
                                    if eh == 0:
                                        if it == 0:
                                            p.wait(t, gTe, 16)
                                    else:
                                        p.wait(t, sDE, 16 * eh)
                                else:
                                    p.wait(t, sEv, eh * 128 + 4 * (it - 1))
                                for mi, m in enumerate(MATS):
                                    bank = (it % 2) * 4 + mi
                                    for k in range(NCH):
                                        ins = t.matmul(
                                            psA[bank][:],
                                            wsl[m][gi % 2][:, k * 128:(k + 1) * 128],
                                            hb[k][:, HLs(hh, n2)],
                                            start=(k == 0), stop=(k == 7))
                                    p.inc(ins, sMM, 1)
                        for n2 in range(2):
                            if n2 == 0:
                                p.wait(t, sEv, 128 * (eh + 1))
                            else:
                                p.wait(t, sDE, eh * 16 + 8)
                            for k in range(16):
                                gik = eh * 32 + n2 * 16 + k
                                p.wait(t, sW2[gik % 4], 16 * (gik // 4 + 1))
                                p.wait(t, sSw, eh * 32 + k * 2 + n2 + 1)
                                for c in range(NCH):
                                    ins = t.matmul(
                                        psA[c][:],
                                        w2b[gik % 4][:, c * 128:(c + 1) * 128],
                                        he[k][:, LLs(n2)],
                                        start=(k == 0), stop=(k == 15))
                                p.inc(ins, sDM, 1)
                ph.eng("tensor", f_t)

                def f_sc(s):
                    for ig in range(16):
                        p.wait(s, gVe, ig + 1)
                        p.inc(s.activation(sco[:], sco[:], Act.Exp), gVx, 1)
                    for eh in range(NEH):
                        hh, e = divmod(eh, 9)
                        shared = e == NE
                        for it in range(32):
                            git = eh * 32 + it
                            i = it // 2
                            j = git % 2
                            if it == 0:
                                p.wait(s, sWb,
                                       16 * sum(4 if x % 9 == NE else 5
                                                for x in range(eh + 1)))
                            p.wait(s, sMM, eh * 128 + 4 * (it + 1))
                            if git >= 2:
                                p.wait(s, sSw, git - 1)
                            bank = (it % 2) * 4
                            p.inc(s.activation(tb["1"][j][:], psA[bank][:],
                                               Act.Identity,
                                               bias=bcol["1g"][eh % 2][:, i:i + 1]),
                                  sEv, 1)
                            p.inc(s.activation(tb["2"][j][:], psA[bank + 2][:],
                                               Act.Identity,
                                               bias=bcol["3g"][eh % 2][:, i:i + 1]),
                                  sEv, 1)
                            p.inc(s.activation(tb["3"][j][:], psA[bank + 1][:],
                                               Act.Identity,
                                               bias=bcol["1l"][eh % 2][:, i:i + 1]),
                                  sEv, 1)
                            p.inc(s.activation(tb["4"][j][:], psA[bank + 3][:],
                                               Act.Identity,
                                               bias=bcol["3l"][eh % 2][:, i:i + 1]),
                                  sEv, 1)
                            p.wait(s, sG, git + 1)
                            p.inc(s.activation(tb["1"][j][:], tb["1"][j][:],
                                               Act.Silu, scale=1.702), sS, 1)
                        for n2 in range(2):
                            for c in range(NCH):
                                gdj = eh * 16 + n2 * 8 + c
                                p.wait(s, sDM, eh * 32 + 16 * (n2 + 1))
                                if gdj >= 4:
                                    p.wait(s, sAc[gdj % 4],
                                           16 * (gdj // 4))
                                bc = (self.cols["sb2"] if shared
                                      else b2c[eh % 2])
                                p.inc(s.activation(dot[gdj % 4][:], psA[c][:],
                                                   Act.Identity,
                                                   bias=bc[:, c:c + 1]),
                                      sDE, 1)
                ph.eng("scalar", f_sc)

                def f_v(v):
                    p.wait(v, gK, 16)
                    for ig in range(16):
                        gsl = psA[0][:, ig * 8:(ig + 1) * 8]
                        v.tensor_reduce(m1[:], gsl, AX.X, Alu.max)
                        v.drain()
                        v.tensor_scalar(sco[:], gsl, m1[:, 0:1], None,
                                        Alu.subtract)
                        v.drain()
                        p.inc(v.drain(), gVe, 1)
                        p.wait(v, gVx, ig + 1)
                        v.tensor_reduce(m1[:], sco[:], AX.X, Alu.add)
                        v.drain()
                        v.reciprocal(m1[:], m1[:])
                        v.drain()
                        v.tensor_scalar(sco[:], sco[:], m1[:, 0:1], None,
                                        Alu.mult)
                        v.drain()
                        v.tensor_tensor(bia[:], sco[:], gbb[:], Alu.add)
                        v.drain()
                        v.tensor_reduce(m1[:], bia[:], AX.X, Alu.max)
                        v.drain()
                        v.tensor_scalar(cmb[:], bia[:], m1[:, 0:1], None,
                                        Alu.is_ge)
                        v.drain()
                        v.scalar_tensor_tensor(bia[:], cmb[:], -1e30, bia[:],
                                               Alu.mult, Alu.add)
                        v.drain()
                        v.tensor_reduce(m1[:], bia[:], AX.X, Alu.max)
                        v.drain()
                        v.tensor_scalar(bia[:], bia[:], m1[:, 0:1], None,
                                        Alu.is_ge)
                        v.drain()
                        v.tensor_tensor(cmb[:], cmb[:], bia[:], Alu.add)
                        v.drain()
                        ins = v.tensor_tensor(cmb[:], cmb[:], sco[:], Alu.mult)
                        v.drain()
                        p.inc(ins, gV, 1)
                        p.wait(v, gTr, ig + 1)
                        ins = v.tensor_copy(cfm[0:NE, ig * 128:(ig + 1) * 128],
                                            psA[1][0:NE, 0:128])
                        v.drain()
                        p.inc(ins, gTe, 1)
                    for eh in range(NEH):
                        hh, e = divmod(eh, 9)
                        shared = e == NE
                        for it in range(32):
                            git = eh * 32 + it
                            i, n2 = it // 2, it % 2
                            j = git % 2
                            p.wait(v, sEv, eh * 128 + 4 * (it + 1))
                            v.tensor_tensor(tb["1"][j][:], tb["1"][j][:],
                                            tb["2"][j][:], Alu.mult)
                            v.drain()
                            ins = v.tensor_scalar(tb["1"][j][:], tb["1"][j][:],
                                                  7.0, None, Alu.min)
                            v.drain()
                            p.inc(ins, sG, 1)
                            v.tensor_tensor(tb["3"][j][:], tb["3"][j][:],
                                            tb["4"][j][:], Alu.mult)
                            v.drain()
                            v.tensor_scalar(tb["3"][j][:], tb["3"][j][:],
                                            7.0, -7.0, Alu.min, Alu.max)
                            v.drain()
                            v.tensor_scalar(tb["3"][j][:], tb["3"][j][:],
                                            1.0 / 1.702, 1.0 / 1.702,
                                            Alu.mult, Alu.add)
                            v.drain()
                            p.wait(v, sS, git + 1)
                            if eh >= 1:
                                p.wait(v, sDM, (eh - 1) * 32 + 16 + i + 1)
                            ins = v.tensor_tensor(he[i][:, LLs(n2)],
                                                  tb["1"][j][:], tb["3"][j][:],
                                                  Alu.mult)
                            v.drain()
                            p.inc(ins, sSw, 1)
                        for n2 in range(2):
                            for c in range(NCH):
                                gdj = eh * 16 + n2 * 8 + c
                                p.wait(v, sDE, gdj + 1)
                                if not shared:
                                    p.wait(v, sCb[n2], 16 * routed_upto[eh])
                                    v.tensor_tensor(dot[gdj % 4][:],
                                                    dot[gdj % 4][:],
                                                    cbb[n2][:], Alu.mult)
                                    v.drain()
                                p.inc(v.drain(), sDC, 1)
                ph.eng("vector", f_v)

        with ExitStack() as sc2, p.phase("mxmg") as ph:
            at = [p.sb(f"mxmg{j}", [128, TH], F32, sc2) for j in range(2)]
            ut = p.sb("mxmgu", [128, TH], F32, sc2)
            sL = ["mxmgL0", "mxmgL1"]
            sV = "mxmgV"

            def f_sy(s):
                for j in range(16):
                    c, hhh = j // 2, j % 2
                    if j >= 2:
                        p.wait(s, sV, j - 1)
                    p.inc(s.dma_start(
                        out=at[j % 2][:],
                        in_=self.scr["x"][c * 128:(c + 1) * 128,
                                          hhh * TH:(hhh + 1) * TH]),
                        sL[j % 2], 16)
            ph.eng("sync", f_sy)

            def f_v(v):
                for j in range(16):
                    c, hhh = j // 2, j % 2
                    hl = slice(hhh * TH, (hhh + 1) * TH)
                    p.wait(v, sL[j % 2], 16 * (j // 2 + 1))
                    v.tensor_scalar(ut[:], at[j % 2][:],
                                    self.modfm[1][:, 40 + c:41 + c], None,
                                    Alu.mult)
                    v.drain()
                    ins = v.tensor_tensor(self.xres[c][:, hl],
                                          self.xres[c][:, hl],
                                          ut[:], Alu.add)
                    v.drain()
                    p.inc(ins, sV, 1)
            ph.eng("vector", f_v)
        self.tap("final", [xc.ap() for xc in self.xres])

    def _moe_accum(self, g, eh, n2, acc_d, dot, sDC, sAc, HLs):
        p = self.p
        hh, e = divmod(eh, 9)
        for c in range(NCH):
            gdj = eh * 16 + n2 * 8 + c
            p.wait(g, sDC, gdj + 1)
            if gdj >= 4:
                p.wait(g, sAc[gdj % 4], 16 * (gdj // 4))
            p.inc(g.dma_start(
                out=acc_d[c * 128:(c + 1) * 128, HLs(hh, n2)],
                in_=dot[gdj % 4][:],
                accum_op=(Alu.bypass if e == 0 else Alu.add)),
                sAc[gdj % 4], 16)


def pack_inputs(inputs):
    global CONSTS
    if CONSTS is None:
        CONSTS = host_constants()
    C = CONSTS
    g = lambda k: np.asarray(inputs[k])
    com = {}
    tr = lambda a: np.ascontiguousarray(np.asarray(a).T.astype(np.float32))
    com["wk_t"] = tr(g("L0_attn_Wk")); com["wv_t"] = tr(g("L0_attn_Wv"))
    com["wq_t"] = tr(g("L0_attn_Wq")); com["wkp_t"] = tr(g("L0_attn_Wkp"))
    com["wqp_t"] = tr(g("L0_attn_Wqp")); com["wo_t"] = tr(g("L0_attn_Wo"))
    com["bk"] = g("L0_attn_bk"); com["bv"] = g("L0_attn_bv"); com["bq"] = g("L0_attn_bq")
    com["bkp"] = g("L0_attn_bkp"); com["bqp"] = g("L0_attn_bqp"); com["bo"] = g("L0_attn_bo")
    com["ps_col"] = g("L0_attn_phase_scale")
    com["ln_g"] = g("L0_attn_ln_g"); com["ln_b"] = g("L0_attn_ln_b")
    com["w_up"] = g("L0_ffn_w_up"); com["w_gate"] = g("L0_ffn_w_gate")
    com["w_down"] = g("L0_ffn_w_down")
    com["wmod0"] = g("L0_wmod"); com["wmod1"] = g("L1_wmod")
    com["q_w"] = g("L1_attn_q_w"); com["k_w"] = g("L1_attn_k_w")
    com["v_w"] = g("L1_attn_v_w"); com["g_w"] = g("L1_attn_g_w")
    com["o_w"] = g("L1_attn_o_w")
    com["q_b"] = g("L1_attn_q_b"); com["k_b"] = g("L1_attn_k_b")
    com["v_b"] = g("L1_attn_v_b"); com["g_b"] = g("L1_attn_g_b")
    com["o_b"] = g("L1_attn_o_b")
    gwT = np.asarray(g("L1_ffn_gate_w")).T
    com["gate_wp"] = gwT.reshape(NCH, 128, NE).transpose(1, 0, 2).reshape(128, NCH * NE)
    com["gate_bbc"] = np.broadcast_to(np.asarray(g("L1_ffn_gate_b"))[None, :],
                                      (128, NE))
    w1 = g("L1_ffn_w1"); w3 = g("L1_ffn_w3")
    wpk = lambda a: np.ascontiguousarray(
        np.asarray(a).reshape(NE, NCH, 128, 16, 128)
        .transpose(0, 3, 2, 1, 4).reshape(NE * MI, D))
    com["w1_g"] = wpk(w1[:, :, 0::2]); com["w1_l"] = wpk(w1[:, :, 1::2])
    com["w3_g"] = wpk(w3[:, :, 0::2]); com["w3_l"] = wpk(w3[:, :, 1::2])
    com["w2"] = np.asarray(g("L1_ffn_w2")).reshape(NE * MI, D)
    b1 = g("L1_ffn_b1"); b3 = g("L1_ffn_b3")
    bc = lambda a: np.asarray(a).reshape(NE, 16, 128).transpose(0, 2, 1).reshape(NE * 128, 16)
    com["b1_g_c"] = bc(b1[:, 0::2]); com["b1_l_c"] = bc(b1[:, 1::2])
    com["b3_g_c"] = bc(b3[:, 0::2]); com["b3_l_c"] = bc(b3[:, 1::2])
    com["b2_c"] = np.asarray(g("L1_ffn_b2")).reshape(NE, NCH, 128).transpose(0, 2, 1).reshape(NE * 128, NCH)
    sw1 = g("L1_ffn_sw1"); sw3 = g("L1_ffn_sw3")
    spk = lambda a: np.ascontiguousarray(
        np.asarray(a).reshape(NCH, 128, 16, 128)
        .transpose(2, 1, 0, 3).reshape(MI, D))
    com["sw1_g"] = spk(sw1[:, 0::2]); com["sw1_l"] = spk(sw1[:, 1::2])
    com["sw3_g"] = spk(sw3[:, 0::2]); com["sw3_l"] = spk(sw3[:, 1::2])
    com["sw2"] = g("L1_ffn_sw2")
    sb1 = g("L1_ffn_sb1"); sb3 = g("L1_ffn_sb3")
    sbc = lambda a: np.asarray(a).reshape(16, 128).T
    com["sb1_g_c"] = sbc(sb1[0::2]); com["sb1_l_c"] = sbc(sb1[1::2])
    com["sb3_g_c"] = sbc(sb3[0::2]); com["sb3_l_c"] = sbc(sb3[1::2])
    com["sb2"] = g("L1_ffn_sb2")
    for k in ("ident_f32", "ones_f32", "ones_row", "zero_col", "halfpi_col",
              "rot_cos", "rot_sin", "dlt", "alpha", "beta_bc", "rnorm", "ehot"):
        com[k] = C[k]
    for v in ("bk", "bv", "bq", "bkp", "bqp", "bo", "ps_col", "ln_g", "ln_b",
              "q_b", "k_b", "o_b", "sb2"):
        com[v] = np.asarray(com[v]).reshape(NCH, 128).T
    com["v_b_row"] = np.asarray(com["v_b"]).reshape(1, 2 * D); del com["v_b"]
    com["g_b"] = np.asarray(com["g_b"]).reshape(16, 128).T
    com = {k: np.ascontiguousarray(np.asarray(v, dtype=np.float32)) for k, v in com.items()}
    import ml_dtypes
    com["ident_bf"] = np.eye(128).astype(ml_dtypes.bfloat16)

    x = np.asarray(inputs["L0_attn_x"])
    c0 = np.asarray(inputs["L0_c"]); c1 = np.asarray(inputs["L1_c"])
    maps = []
    for b in range(B):
        m = dict(com)
        m["xT"] = np.ascontiguousarray(x[b].T.astype(np.float32))
        m["c0"] = np.ascontiguousarray(c0[b].reshape(NCH, 128).T.astype(np.float32))
        m["c1"] = np.ascontiguousarray(c1[b].reshape(NCH, 128).T.astype(np.float32))
        maps.append(m)
    return maps


def kernel(**inputs):
    maps = pack_inputs(inputs)
    nc = build_program()
    res = run_bass_kernel_spmd(nc, maps, core_ids=list(range(B)))
    outs = [np.asarray(res.results[b]["out_fm"]).T for b in range(B)]
    return np.ascontiguousarray(np.stack(outs)).astype(np.float32)


# version 12
# speedup vs baseline: 1.1386x; 1.1386x over previous
import math
from contextlib import ExitStack

import numpy as np
import concourse.bass as bass
import concourse.mybir as mybir
from concourse.bass_utils import run_bass_kernel_spmd

F32 = mybir.dt.float32
BF16 = mybir.dt.bfloat16
I32 = mybir.dt.int32
Alu = mybir.AluOpType
Act = mybir.ActivationFunctionType
AX = mybir.AxisListType

B, T, D = 8, 2048, 1024
NCH = 8
NT = 4
TTS = 512
NH, DKH, DVH = 16, 64, 128
CRET = 128
EPS = 1e-5
DI = 3072
MI = 2048
NE = 8


class Prog:
    ENGINES = ("sync", "vector", "scalar", "tensor", "gpsimd")
    PERSIST = ("bar", "st", "ldx", "ldc", "ldw0", "ldw0b", "ldw1", "pcast",
               "l0aSp00", "l0aSp01", "l0aSp10", "l0aSp11", "l0aSp20", "l0aSp21")

    def __init__(self, nc, ctx, taps=()):
        self.nc = nc
        self.ctx = ctx
        self.taps = list(taps)
        self.sems = {}
        self.base = {}
        self.cnt = {}
        self.pcnt = {}
        allp = [ctx.enter_context(nc.semaphore(f"pool{i}")) for i in range(87)]
        self.pool = allp[:52]
        self.gpool = allp[52:]
        self.pool_i = 0
        self.gpool_i = 0
        self.kind = {}
        self._persist = {n: ctx.enter_context(nc.semaphore("s_" + n))
                         for n in self.PERSIST}
        for h in allp + list(self._persist.values()):
            self.pcnt[h.name] = 0
        self.phase_names = set()
        self.bar = self.sem("bar")
        self.nbar = 0
        self.sem("st")

    def sem(self, name, gp=False):
        if name not in self.sems:
            if name in self._persist:
                h = self._persist[name]
            elif gp:
                h = self.gpool[self.gpool_i % len(self.gpool)]
                self.gpool_i += 1
                self.kind[name] = "gp"
            else:
                h = self.pool[self.pool_i % len(self.pool)]
                self.pool_i += 1
                self.kind[name] = "hw"
            self.sems[name] = h
            self.base[name] = self.pcnt[h.name]
            self.cnt[name] = 0
        return self.sems[name]

    def inc(self, ins, name, n=1, quiesce=True):
        gp = "qPoolDynamic" in ins.concise()
        if name not in self.sems:
            h = self.sem(name, gp=gp)
        else:
            h = self.sems[name]
            if gp and self.kind.get(name) == "hw" and name not in self._persist:
                raise RuntimeError(f"sem {name}: gpsimd DMA inc on hw-pool sem "
                                   "(bind order bug - inc before wait)")
        ins.then_inc(h, n)
        self.cnt[name] += n
        self.pcnt[h.name] += n
        if quiesce:
            self.phase_names.add(name)
        return ins

    def wait(self, e, name, val=None):
        h = self.sem(name)
        e.wait_ge(h, self.base[name] + (self.cnt[name] if val is None else val))

    def val(self, name):
        self.sem(name)
        return self.cnt[name]

    def sb(self, name, shape, dtype, stack=None):
        return (stack or self.ctx).enter_context(self.nc.sbuf_tensor(name, shape, dtype))

    def ps(self, name, shape, dtype=F32, stack=None):
        return (stack or self.ctx).enter_context(self.nc.psum_tensor(name, shape, dtype))

    def phase(self, name):
        return _Phase(self, name)


class _Phase:
    def __init__(self, prog, name):
        self.p = prog
        self.name = name

    def __enter__(self):
        self.block = self.p.nc.Block(self.name)
        self.block.__enter__()
        return self

    def eng(self, which, f):
        p = self.p

        def wrapped(e):
            if p.nbar:
                e.wait_ge(p.bar, 5 * p.nbar)
            f(e)
        getattr(self.block, which)(wrapped)

    def __exit__(self, *exc):
        if exc[0] is None:
            p = self.p
            qn = [(n, p.base[n] + p.cnt[n]) for n in sorted(p.phase_names)
                  if n != "bar"]
            p.phase_names = set()
            if qn:
                def quiesce(e):
                    if p.nbar:
                        e.wait_ge(p.bar, 5 * p.nbar)
                    for n, v in qn:
                        e.wait_ge(p.sems[n], v)
                self.block.sync(quiesce)
            for which in Prog.ENGINES:
                def endf(e):
                    if p.nbar:
                        e.wait_ge(p.bar, 5 * p.nbar)
                    e.drain().then_inc(p.bar, 1)
                    p.cnt["bar"] += 1
                    p.pcnt[p.bar.name] += 1
                getattr(self.block, which)(endf)
            p.nbar += 1
        return self.block.__exit__(*exc)


def host_constants():
    c = {}
    c["ident_f32"] = np.eye(128, dtype=np.float32)
    c["ones_f32"] = np.ones((128, 1), np.float32)
    c["ones_row"] = np.ones((1, 128), np.float32)
    c["zero_col"] = np.zeros((128, 1), np.float32)
    c["halfpi_col"] = np.full((128, 1), math.pi / 2, np.float32)
    half = 1.0 / np.power(10000.0, np.linspace(0.0, 1.0, DKH // 2))
    angle = np.repeat(half, 2)
    idx = np.arange(T, dtype=np.float64)
    sin = np.sin(idx[:, None] * angle[None, :])
    cos = np.cos(idx[:, None] * angle[None, :])
    c["rot_cos"] = np.tile(cos.T, (2, 1)).astype(np.float32)
    sign = np.where(np.arange(DKH) % 2 == 0, -1.0, 1.0)
    c["rot_sin"] = np.tile((sin * sign[None, :]).T, (2, 1)).astype(np.float32)
    decay = np.log(1.0 - 2.0 ** (-5.0 - np.arange(NH, dtype=np.float64)))
    r = np.arange(CRET)
    dl = np.exp((r[None, :] - r[:, None])[None] * decay[:, None, None])
    dl = np.where(r[None, :] >= r[:, None], dl, 0.0)
    c["dlt"] = np.ascontiguousarray(
        dl.transpose(1, 0, 2).reshape(CRET, NH * CRET)).astype(np.float32)
    c["alpha"] = np.exp((CRET - 1 - r)[:, None] * decay[None, :]).astype(np.float32)
    beta = np.exp((r + 1)[None, :] * decay[:, None])
    c["beta_bc"] = np.broadcast_to(
        beta.reshape(1, NH * CRET), (128, NH * CRET)).astype(np.float32).copy()
    lam = np.exp(decay)
    n = np.arange(T, dtype=np.float64)
    rowsum = (1.0 - lam[:, None] ** (n[None, :] + 1.0)) / (1.0 - lam[:, None])
    c["rnorm"] = (1.0 / np.sqrt(rowsum)).astype(np.float32)
    eh = np.zeros((128, NH * NH), np.float32)
    for h_ in range(NH):
        eh[:, h_ * NH + h_] = 1.0
    c["ehot"] = eh
    c["lam_pc"] = [float(v) for v in lam ** CRET]
    return c


CONSTS = None


def build_program(taps=(), upto=None, start=None):
    global CONSTS
    if CONSTS is None:
        CONSTS = host_constants()
    nc = bass.Bass("TRN2", target_bir_lowering=False, debug=False)
    dI = {}

    def din(name, shape, dtype=F32):
        dI[name] = nc.dram_tensor(name, list(shape), dtype, kind="ExternalInput").ap()

    din("xT", (D, T))
    for w in ("wk_t", "wv_t", "wq_t", "wkp_t", "wqp_t", "wo_t"):
        din(w, (D, D))
    for v in ("bk", "bv", "bq", "bkp", "bqp", "bo", "ps_col", "ln_g", "ln_b",
              "c0", "c1", "q_b", "k_b", "o_b", "sb2"):
        din(v, (128, NCH))
    din("w_up", (DI, D)); din("w_gate", (DI, D)); din("w_down", (D, DI))
    din("wmod0", (D, 6 * D)); din("wmod1", (D, 6 * D))
    din("q_w", (D, D)); din("k_w", (D, D)); din("v_w", (D, 2 * D)); din("g_w", (D, 2 * D))
    din("o_w", (2 * D, D))
    din("v_b_row", (1, 2 * D)); din("g_b", (128, 16))
    din("ehot", (128, NH * NH))
    din("gate_wp", (128, NCH * NE)); din("gate_bbc", (128, NE))
    for nm in ("w1_g", "w1_l", "w3_g", "w3_l"):
        din(nm, (NE * MI, D))
    din("w2", (NE * MI, D))
    for nm in ("b1_g", "b1_l", "b3_g", "b3_l"):
        din(nm + "_c", (NE * 128, 16))
    din("b2_c", (NE * 128, NCH))
    for nm in ("sw1_g", "sw1_l", "sw3_g", "sw3_l"):
        din(nm, (MI, D))
    din("sw2", (MI, D))
    for nm in ("sb1_g", "sb1_l", "sb3_g", "sb3_l"):
        din(nm + "_c", (128, 16))
    din("ident_f32", (128, 128))
    din("ones_f32", (128, 1)); din("ones_row", (1, 128))
    din("zero_col", (128, 1)); din("halfpi_col", (128, 1))
    din("ident_bf", (128, 128), BF16)
    din("rot_cos", (128, T)); din("rot_sin", (128, T))
    din("dlt", (CRET, NH * CRET)); din("alpha", (CRET, NH))
    din("beta_bc", (128, NH * CRET)); din("rnorm", (NH, T))

    out = nc.dram_tensor("out_fm", [D, T], F32, kind="ExternalOutput").ap()

    scr = {
        "mod": nc.dram_tensor("scr_mod", [2, 6 * D], F32, kind="Internal").ap(),
        "row": nc.dram_tensor("scr_row", [64, T], F32, kind="Internal").ap(),
        "P": nc.dram_tensor("scr_P", [DI, T], F32, kind="Internal").ap(),
        "G": nc.dram_tensor("scr_G", [DI, T], F32, kind="Internal").ap(),
        "O": nc.dram_tensor("scr_O", [2 * D, T], F32, kind="Internal").ap(),
        "x": nc.dram_tensor("scr_x", [D, T], F32, kind="Internal").ap(),
        "comb": nc.dram_tensor("scr_comb", [NE, T], F32, kind="Internal").ap(),
        "kq": nc.dram_tensor("scr_kq", [2 * D, T], BF16, kind="Internal").ap(),
        "UP": nc.dram_tensor("scr_UP", [DI, T], mybir.dt.int16, kind="Internal").ap(),
    }
    for nm in ("w1_g", "w1_l", "w3_g", "w3_l"):
        scr["c" + nm] = nc.dram_tensor("scr_c" + nm, [NE * MI, D], BF16,
                                       kind="Internal").ap()
        scr["cs" + nm[1:]] = nc.dram_tensor("scr_cs" + nm[1:], [MI, D], BF16,
                                            kind="Internal").ap()
    scr["cw2"] = nc.dram_tensor("scr_cw2", [NE * MI, D], BF16,
                                kind="Internal").ap()
    scr["csw2"] = nc.dram_tensor("scr_csw2", [MI, D], BF16,
                                 kind="Internal").ap()
    cc = [(nc.dram_tensor(f"cc_in{i}", [1, 8], F32, kind="Internal").ap(),
           nc.dram_tensor(f"cc_out{i}", [1, 8], F32, kind="Internal").ap())
          for i in range(4)]

    with ExitStack() as ctx:
        p = Prog(nc, ctx, taps)
        em = Emit(p, dI, out, scr, cc, taps, upto)
        em.start = start
        em.run()
    return nc


class Emit:
    def __init__(self, p, dI, out, scr, cc, taps, upto):
        self.p = p
        self.dI = dI
        self.out = out
        self.scr = scr
        self.cc = cc
        self.taps = list(taps)
        self.upto = upto

    def tap(self, name, src_list):
        if name not in self.taps:
            return
        p = self.p
        ap0 = src_list[0]
        n = len(src_list)
        t = p.nc.dram_tensor("tap_" + name, [n * ap0.shape[0]] + list(ap0.shape[1:]),
                             ap0.dtype, kind="ExternalOutput").ap()
        with p.phase("tap_" + name) as ph:
            def f(s):
                for i, a in enumerate(src_list):
                    p.inc(s.dma_start(
                        out=t[i * ap0.shape[0]:(i + 1) * ap0.shape[0]], in_=a), "st", 16)
            ph.eng("sync", f)

    def tap_dram(self, name, dram_ap, nrows):
        if name not in self.taps:
            return
        p = self.p
        t = p.nc.dram_tensor("tap_" + name, [nrows, T], dram_ap.dtype,
                             kind="ExternalOutput").ap()
        with p.phase("tapd_" + name) as ph:
            def f(s):
                for c in range(nrows // 128):
                    p.inc(s.dma_start(out=t[c * 128:(c + 1) * 128, :],
                                      in_=dram_ap[c * 128:(c + 1) * 128, :]),
                          "st", 16)
            ph.eng("sync", f)

    def stop(self, name):
        return self.upto is not None and name == self.upto

    def run(self):
        p = self.p
        self.alloc_persistent()
        self.p0_loads()
        if getattr(self, "start", None) != "l1":
            self.mod_layer(0)
            self.tap("mod0", [self.modfm[0].ap()])
            if self.stop("mod0"):
                return self.finish()
            r = self.block_l0()
            if r:
                return self.finish()
        r = self.block_l1()
        if r:
            return self.finish()
        self.finish(final=True)

    def block_l0(self):
        p = self.p
        with ExitStack() as sc0:
            hb = [p.sb(f"hb0_{c}", [128, T], BF16, sc0) for c in range(NCH)]
            with ExitStack() as sc1:
                wk = [p.sb(f"wkc{c}", [128, D], BF16, sc1) for c in range(NCH)]
                wq = [p.sb(f"wqc{c}", [128, D], BF16, sc1) for c in range(NCH)]
                wv = [p.sb(f"wvc{c}", [128, D], BF16, sc1) for c in range(NCH)]
                with p.phase("l0w1") as ph:
                    def f_g(g):
                        for c in range(NCH):
                            for dst, s_ in ((wk, "wk_t"), (wq, "wq_t"), (wv, "wv_t")):
                                p.inc(g.dma_start(
                                    out=dst[c][:],
                                    in_=self.dI[s_][c * 128:(c + 1) * 128, :]),
                                    "ldw0", 16)
                    ph.eng("gpsimd", f_g)
                self.ln_mod(dst=hb, ca=self.modfm[0][:, 8:16],
                            sa=self.modfm[0][:, 0:8], tag="ln1",
                            resid_g=self.modfm[0][:, 16:24])
                self.tap("ln1", [h.ap() for h in hb])
                if self.stop("ln1"):
                    return True
                self.l0_attn_a(hb, wk, wq, wv)
            self.l0_attn_b()
        self.tap("l0attn", [xc.ap() for xc in self.xres])
        if self.stop("l0attn"):
            return True
        with ExitStack() as scf:
            if self.l0_ffn(scf):
                return True
        self.tap("l0ffn", [xc.ap() for xc in self.xres])
        return self.stop("l0ffn")

    def block_l1(self):
        p = self.p
        self.mod_layer(1)
        self.tap("mod1", [self.modfm[1].ap()])
        if self.stop("mod1"):
            return True
        with ExitStack() as sc0:
            hb = [p.sb(f"hb1_{c}", [128, T], BF16, sc0) for c in range(NCH)]
            self.ln_mod(dst=hb, ca=self.modfm[1][:, 8:16],
                        sa=self.modfm[1][:, 0:8], tag="ln3")
            self.tap("ln3", [h.ap() for h in hb])
            if self.stop("ln3"):
                return True
            self.l1_attn(hb, sc0)
        self.tap("l1attn", [xc.ap() for xc in self.xres])
        if self.stop("l1attn"):
            return True
        with ExitStack() as scm:
            hb = [p.sb(f"hbm_{c}", [128, T], BF16, scm) for c in range(NCH)]
            self.ln_mod(dst=hb, ca=self.modfm[1][:, 32:40],
                        sa=self.modfm[1][:, 24:32], tag="ln4",
                        spill32=self.scr["P"])
            if self.stop("ln4"):
                return True
            if self.l1_moe(hb, scm):
                return True
        return False

    def finish(self, final=False):
        p = self.p
        with p.phase("fin") as ph:
            def f(s):
                if final:
                    for c in range(NCH):
                        p.inc(s.dma_start(out=self.out[c * 128:(c + 1) * 128, :],
                                          in_=self.xres[c][:]), "st", 16)
                p.wait(s, "st")
            ph.eng("sync", f)

    def alloc_persistent(self):
        p = self.p
        self.xres = [p.sb(f"xres{c}", [128, T], F32) for c in range(NCH)]
        self.identf = p.sb("identf", [128, 128], F32)
        self.identb = p.sb("identb", [128, 128], BF16)
        self.onesf = p.sb("onesf", [128, 1], F32)
        self.onesr = p.sb("onesr", [1, 128], F32)
        self.zcol = p.sb("zcol", [128, 1], F32)
        self.hpicol = p.sb("hpicol", [128, 1], F32)
        self.onesb = p.sb("onesb", [128, 1], BF16)
        self.cols = {}
        for v in ("bk", "bv", "bq", "bkp", "bqp", "bo", "ps_col", "ln_g", "ln_b",
                  "c0", "c1", "q_b", "k_b", "o_b", "sb2"):
            self.cols[v] = p.sb("col_" + v, [128, 8], F32)
        self.modfm = [p.sb(f"modfm{l}", [128, 48], F32) for l in range(2)]
        self.negps = p.sb("negps", [128, 8], F32)

    def p0_loads(self):
        p, dI = self.p, self.dI
        with p.phase("p0") as ph:
            def f_sync(s):
                for c in range(NCH):
                    p.inc(s.dma_start(out=self.xres[c][:],
                                      in_=dI["xT"][c * 128:(c + 1) * 128, :]), "ldx", 16)
                p.inc(s.dma_start(out=self.identf[:], in_=dI["ident_f32"]), "ldc", 16)
                p.inc(s.dma_start(out=self.identb[:], in_=dI["ident_bf"]), "ldc", 16)
                p.inc(s.dma_start(out=self.onesf[:], in_=dI["ones_f32"]), "ldc", 16)
                p.inc(s.dma_start(out=self.onesr[:], in_=dI["ones_row"]), "ldc", 16)
                p.inc(s.dma_start(out=self.zcol[:], in_=dI["zero_col"]), "ldc", 16)
                p.inc(s.dma_start(out=self.hpicol[:], in_=dI["halfpi_col"]), "ldc", 16)
                for v in ("bk", "bv", "bq", "bkp", "bqp", "bo", "ps_col", "ln_g",
                          "ln_b", "c0", "c1", "q_b", "k_b", "o_b", "sb2"):
                    p.inc(s.dma_start(out=self.cols[v][:], in_=dI[v]), "ldc", 16)
            ph.eng("sync", f_sync)

            def f_vec(v):
                p.wait(v, "ldc")
                v.tensor_copy(self.onesb[:], self.onesf[:])
                v.tensor_scalar_mul(self.negps[:], self.cols["ps_col"][:], -1.0)
                v.drain()
            ph.eng("vector", f_vec)


    def mod_layer(self, layer):
        p, dI = self.p, self.dI
        cname = "c0" if layer == 0 else "c1"
        wname = "wmod0" if layer == 0 else "wmod1"
        with ExitStack() as sc:
            wmodb = [p.sb(f"wmodb{layer}_{j}", [128, 6 * D], F32, sc) for j in range(2)]
            with p.phase(f"mod{layer}") as ph:
                csil = p.sb(f"modcs{layer}", [128, 8], F32, sc)
                mps = p.ps(f"modps{layer}", [128, 48], F32, sc)
                sS, sV, sM, sF = (f"mod{layer}" + x for x in "SVMF")
                sW = [f"mod{layer}w{j}" for j in range(2)]

                def f_g(g):
                    for c in range(NCH):
                        if c >= 2:
                            p.wait(g, sF, c - 1)
                        p.inc(g.dma_start(out=wmodb[c % 2][:],
                                          in_=dI[wname][c * 128:(c + 1) * 128, :]),
                              sW[c % 2], 16)
                ph.eng("gpsimd", f_g)

                def f_sc(s):
                    p.wait(s, "ldc")
                    s.activation(csil[:], self.cols[cname][:], Act.Silu)
                    p.inc(s.drain(), sS, 1)
                ph.eng("scalar", f_sc)

                def f_v(v):
                    p.wait(v, sF, p.val(sF) + NCH)
                    v.tensor_copy(self.modfm[layer][:], mps[:])
                    v.drain()
                ph.eng("vector", f_v)

                def f_t(t):
                    p.wait(t, sS, 1)
                    for c in range(NCH):
                        p.wait(t, sW[c % 2], 16 * (c // 2 + 1))
                        for j in range(48):
                            ins = t.matmul(mps[:, j:j + 1],
                                           wmodb[c % 2][:, j * 128:(j + 1) * 128],
                                           csil[:, c:c + 1],
                                           start=(c == 0 and j == 0),
                                           stop=(c == NCH - 1 and j == 47))
                        p.inc(ins, sF, 1)
                ph.eng("tensor", f_t)

        with p.phase(f"modg{layer}") as ph:
            def f_v(v):
                with ExitStack() as s2:
                    gtmp = self.p.sb(f"modgt{layer}", [128, 24], F32, s2)
                    for off in (16, 40):
                        g = self.modfm[layer][:, off:off + 8]
                        sgn, mag, neg = gtmp[:, 0:8], gtmp[:, 8:16], gtmp[:, 16:24]
                        v.tensor_scalar(sgn, g, 0.0, None, Alu.is_ge)
                        v.tensor_scalar_mul(neg, g, -1.0)
                        v.drain()
                        v.tensor_scalar(sgn, sgn, 2.0, -1.0, Alu.mult, Alu.add)
                        v.tensor_tensor(mag, g, neg, Alu.max)
                        v.drain()
                        v.tensor_scalar_max(mag, mag, 0.25)
                        v.drain()
                        v.tensor_tensor(g, sgn, mag, Alu.mult)
                        v.drain()
            ph.eng("vector", f_v)

    def ln_mod(self, dst, ca, sa, tag, resid_g=None, spill32=None):
        p = self.p
        with ExitStack() as sc:
            rows = [p.sb(tag + f"_r{i}", [1, T], F32, sc) for i in range(4)]
            arow, brow, rmean, rtmp = (r.ap() for r in rows)
            sc_st = sc.enter_context(ExitStack())
            stat_s = p.ps(tag + "_sts", [1, T], F32, sc_st)
            stat_q = p.ps(tag + "_stq", [1, T], F32, sc_st)
            sq = [p.sb(tag + f"_sq{j}", [128, T], F32, sc)
                  for j in range(2 if resid_g is not None else 1)]

            with p.phase(tag + "a") as ph:
                sQ, sM = tag + "Q", tag + "M"
                mbase, qbase = p.val(sM), p.val(sQ)

                def f_sc(s):
                    p.wait(s, "ldx")
                    for c in range(NCH):
                        if c >= 1:
                            p.wait(s, sM, mbase + 2 * c)
                        p.inc(s.activation(sq[0][:], self.xres[c][:], Act.Square),
                              sQ, 1)
                ph.eng("scalar", f_sc)

                def f_t(t):
                    p.wait(t, "ldx")
                    p.wait(t, "ldc")
                    for c in range(NCH):
                        for n in range(NT):
                            ins = t.matmul(stat_s[0:1, n * TTS:(n + 1) * TTS],
                                           self.onesf[:],
                                           self.xres[c][:, n * TTS:(n + 1) * TTS],
                                           start=(c == 0), stop=(c == NCH - 1))
                        p.inc(ins, sM, 1)
                        p.wait(t, sQ, qbase + c + 1)
                        for n in range(NT):
                            ins = t.matmul(stat_q[0:1, n * TTS:(n + 1) * TTS],
                                           self.onesf[:],
                                           sq[0][:, n * TTS:(n + 1) * TTS],
                                           start=(c == 0), stop=(c == NCH - 1))
                        p.inc(ins, sM, 1)
                ph.eng("tensor", f_t)

            with p.phase(tag + "b") as ph:
                sW, sS, sR = tag + "W", tag + "S", tag + "R"

                def f_v(v):
                    v.tensor_scalar_mul(rmean, stat_s[0:1, :], 1.0 / D)
                    v.tensor_scalar_mul(rtmp, stat_q[0:1, :], 1.0 / D)
                    v.drain()
                    v.tensor_tensor(arow, rmean, rmean, Alu.mult)
                    v.drain()
                    v.tensor_tensor(rtmp, rtmp, arow, Alu.subtract)
                    v.drain()
                    v.tensor_scalar_add(rtmp, rtmp, EPS)
                    v.drain()
                    v.reciprocal(rtmp, rtmp)
                    p.inc(v.drain(), sW, 1)
                    p.wait(v, sS, 1)
                    v.tensor_tensor(brow, rmean, arow, Alu.mult)
                    v.drain()
                    v.tensor_scalar_mul(brow, brow, -1.0)
                    v.drain()
                    p.inc(v.drain(), sR, 1)
                ph.eng("vector", f_v)

                def f_sc(s):
                    p.wait(s, sW, 1)
                    p.inc(s.activation(arow, rtmp, Act.Sqrt), sS, 1)
                ph.eng("scalar", f_sc)

            sc_st.close()
            with p.phase(tag + "c") as ph:
                abps = [p.ps(tag + f"_ab{j}", [128, 512], F32, sc) for j in range(4)]
                ca1 = p.sb(tag + "_ca1", [128, 8], F32, sc)
                sB, sA = tag + "B", tag + "A"
                if spill32 is not None:
                    hfb = [p.sb(tag + f"_hf{j}", [128, TTS], F32, sc)
                           for j in range(2)]
                    sH = tag + "H"
                    sHD = [tag + "HD0", tag + "HD1"]

                    def f_sy(s):
                        for j in range(NT * NCH):
                            n, c = j // NCH, j % NCH
                            p.wait(s, sH, j + 1)
                            p.inc(s.dma_start(
                                out=spill32[c * 128:(c + 1) * 128,
                                            n * TTS:(n + 1) * TTS],
                                in_=hfb[j % 2][:]), sHD[j % 2], 16)
                    ph.eng("sync", f_sy)

                def f_t(t):
                    for n in range(NT):
                        if n >= 2:
                            p.wait(t, sA, NCH * (n - 1))
                        ins = t.matmul(abps[2 * (n % 2)][:], self.onesr[:],
                                       arow[:, n * TTS:(n + 1) * TTS],
                                       start=True, stop=True)
                        ins2 = t.matmul(abps[2 * (n % 2) + 1][:], self.onesr[:],
                                        brow[:, n * TTS:(n + 1) * TTS],
                                        start=True, stop=True)
                        p.inc(ins2, sB, 1)
                ph.eng("tensor", f_t)

                def f_v(v):
                    v.tensor_scalar_add(ca1[:], ca, 1.0)
                    v.drain()
                    for n in range(NT):
                        p.wait(v, sB, n + 1)
                        a = abps[2 * (n % 2)]
                        bb = abps[2 * (n % 2) + 1]
                        tl = slice(n * TTS, (n + 1) * TTS)
                        for c in range(NCH):
                            tmp = sq[0][:, 0:TTS]
                            v.tensor_tensor(tmp, self.xres[c][:, tl], a[:], Alu.mult)
                            v.drain()
                            v.tensor_tensor(tmp, tmp, bb[:], Alu.add)
                            v.drain()
                            if resid_g is None and spill32 is not None:
                                j = n * NCH + c
                                if j >= 2:
                                    p.wait(v, sHD[j % 2], 16 * (j // 2))
                                hf = hfb[j % 2]
                                v.tensor_scalar(hf[:], tmp,
                                                ca1[:, c:c + 1], sa[:, c:c + 1],
                                                Alu.mult, Alu.add)
                                v.drain()
                                p.inc(v.drain(), sH, 1)
                                ins = v.tensor_copy(dst[c][:, tl], hf[:])
                                v.drain()
                            elif resid_g is None:
                                ins = v.tensor_scalar(dst[c][:, tl], tmp,
                                                      ca1[:, c:c + 1], sa[:, c:c + 1],
                                                      Alu.mult, Alu.add)
                                v.drain()
                            else:
                                hf = sq[-1][:, 0:TTS]
                                v.tensor_scalar(hf, tmp, ca1[:, c:c + 1],
                                                sa[:, c:c + 1], Alu.mult, Alu.add)
                                v.drain()
                                v.tensor_copy(dst[c][:, tl], hf)
                                v.tensor_scalar_mul(tmp, hf, resid_g[:, c:c + 1])
                                v.drain()
                                ins = v.tensor_tensor(self.xres[c][:, tl],
                                                      self.xres[c][:, tl], tmp, Alu.add)
                                v.drain()
                            p.inc(ins, sA, 1)
                ph.eng("vector", f_v)

    def l0_attn_a(self, hb, wk, wq, wv):
        p = self.p
        with ExitStack() as sca, p.phase("l0a") as ph:
            ppk = [p.ps(f"l0apk{j}", [128, TTS], F32, sca) for j in range(2)]
            ppq = [p.ps(f"l0apq{j}", [128, TTS], F32, sca) for j in range(2)]
            ppv = [p.ps(f"l0apv{j}", [128, TTS], F32, sca) for j in range(2)]
            ktile = [p.sb(f"l0ak{j}", [128, TTS], BF16, sca) for j in range(2)]
            qtile = [p.sb(f"l0aq{j}", [128, TTS], BF16, sca) for j in range(2)]
            vtile = [p.sb(f"l0av{j}", [128, TTS], F32, sca) for j in range(2)]
            sM, sE = "l0aM", "l0aE"
            sSp = [[f"l0aSp{t_}{j}" for j in range(2)] for t_ in range(3)]

            def f_t(t):
                p.wait(t, "ldw0")
                for i in range(NCH * NT):
                    c, n = divmod(i, NT)
                    if i >= 2:
                        p.wait(t, sE, 3 * (i - 1))
                    for wmat, bank in ((wk, ppk), (wq, ppq), (wv, ppv)):
                        for k in range(NCH):
                            ins = t.matmul(
                                bank[i % 2][:],
                                wmat[k][:, c * 128:(c + 1) * 128],
                                hb[k][:, n * TTS:(n + 1) * TTS],
                                start=(k == 0), stop=(k == NCH - 1))
                        p.inc(ins, sM, 1)
            ph.eng("tensor", f_t)

            def f_sc(s):
                for i in range(NCH * NT):
                    c, n = divmod(i, NT)
                    if i >= 2:
                        for t_ in range(3):
                            p.wait(s, sSp[t_][i % 2], 16 * ((i - 2) // 2 + 1))
                    p.wait(s, sM, 3 * i + 1)
                    p.inc(s.activation(ktile[i % 2][:], ppk[i % 2][:], Act.Identity,
                                       bias=self.cols["bk"][:, c:c + 1]), sE, 1)
                    p.wait(s, sM, 3 * i + 2)
                    p.inc(s.activation(qtile[i % 2][:], ppq[i % 2][:], Act.Identity,
                                       bias=self.cols["bq"][:, c:c + 1]), sE, 1)
                    p.wait(s, sM, 3 * i + 3)
                    p.inc(s.activation(vtile[i % 2][:], ppv[i % 2][:], Act.Identity,
                                       bias=self.cols["bv"][:, c:c + 1]), sE, 1)
            ph.eng("scalar", f_sc)

            def f_sync(s):
                for i in range(NCH * NT):
                    c, n = divmod(i, NT)
                    tl = slice(n * TTS, (n + 1) * TTS)
                    p.wait(s, sE, 3 * i + 3)
                    p.inc(s.dma_start(out=self.scr["kq"][c * 128:(c + 1) * 128, tl],
                                      in_=ktile[i % 2][:]), sSp[0][i % 2], 16)
                    p.inc(s.dma_start(out=self.scr["kq"][D + c * 128:D + (c + 1) * 128, tl],
                                      in_=qtile[i % 2][:]), sSp[1][i % 2], 16)
                    p.inc(s.dma_start(out=self.scr["P"][c * 128:(c + 1) * 128, tl],
                                      in_=vtile[i % 2][:]), sSp[2][i % 2], 16)
                for t_ in range(3):
                    for j in range(2):
                        p.wait(s, sSp[t_][j])
            ph.eng("sync", f_sync)

    def l0_attn_b(self):
        p, dI = self.p, self.dI
        with ExitStack() as scb:
            wkp = [p.sb(f"wkpc{c}", [128, D], BF16, scb) for c in range(NCH)]
            wqp = [p.sb(f"wqpc{c}", [128, D], BF16, scb) for c in range(NCH)]
            wo = [p.sb(f"woc{c}", [128, D], BF16, scb) for c in range(NCH)]
            gabo = p.sb("l0gabo", [128, 8], F32, scb)
            with p.phase("l0wb") as ph:
                def f_g(g):
                    for c in range(NCH):
                        for dst, s_ in ((wkp, "wkp_t"), (wqp, "wqp_t"), (wo, "wo_t")):
                            p.inc(g.dma_start(out=dst[c][:],
                                              in_=dI[s_][c * 128:(c + 1) * 128, :]),
                                  "ldw0b", 16)
                ph.eng("gpsimd", f_g)

                def f_v(v):
                    v.tensor_tensor(gabo[:], self.modfm[0][:, 16:24],
                                    self.cols["bo"][:], Alu.mult)
                    v.drain()
                ph.eng("vector", f_v)

            for n in range(NT):
                self._l0_tile(n, wkp, wqp, wo, gabo)

    def _l0_tile(self, n, wkp, wqp, wo, gabo):
        p = self.p
        tl = slice(n * TTS, (n + 1) * TTS)
        tg = f"l0b{n}"
        C = self.cols
        with ExitStack() as sc, p.phase(tg) as ph:
            keyf = [p.sb(tg + f"keyf{c}", [128, TTS], BF16, sc) for c in range(NCH)]
            qryf = [p.sb(tg + f"qryf{c}", [128, TTS], BF16, sc) for c in range(NCH)]
            kp = p.sb(tg + "kp", [128, TTS], F32, sc)
            skp = p.sb(tg + "skp", [128, TTS], F32, sc)
            ckp = p.sb(tg + "ckp", [128, TTS], F32, sc)
            qp = p.sb(tg + "qp", [128, TTS], F32, sc)
            sqp = p.sb(tg + "sqp", [128, TTS], F32, sc)
            cqp = p.sb(tg + "cqp", [128, TTS], F32, sc)
            val = [p.sb(tg + f"val{j}", [128, TTS], F32, sc) for j in range(2)]
            scn = p.sb(tg + "scn", [128, TTS], F32, sc)
            mr = p.sb(tg + "mr", [128, TTS], F32, sc)
            mi = p.sb(tg + "mi", [128, TTS], F32, sc)
            reti = [p.sb(tg + f"ret{c}", [128, TTS], F32, sc) for c in range(NCH)]
            sqb = [p.sb(tg + f"sqb{j}", [128, TTS], F32, sc) for j in range(2)]
            lno = keyf
            tres = val
            arow, brow = mr[0:1, :], mi[0:1, :]
            rmean, rtmp = scn[0:1, :], ckp[0:1, :]
            ppk = p.ps(tg + "ppk", [128, TTS], F32, sc)
            ppq = p.ps(tg + "ppq", [128, TTS], F32, sc)
            ppo = [p.ps(tg + f"ppo{j}", [128, TTS], F32, sc) for j in range(2)]
            pst_s = p.ps(tg + "psts", [1, TTS], F32, sc)
            pst_q = p.ps(tg + "pstq", [1, TTS], F32, sc)
            pab = [p.ps(tg + f"pab{j}", [128, TTS], F32, sc) for j in range(2)]
            (sKQ, sM, sE, sKP, sQP, sC, sQ2, sMs, sW, sS, sR, sB, sA, sO,
             sF, sX) = (tg + x_ for x_ in
                        ("KQ", "M", "E", "KP", "QP", "C", "Q2", "Ms", "W",
                         "S", "R", "B", "A", "O", "F", "X"))
            sL = [tg + "L0", tg + "L1"]

            def f_sync(s):
                for t_ in range(3):
                    for j in range(2):
                        p.wait(s, f"l0aSp{t_}{j}")
                for c in range(NCH):
                    p.inc(s.dma_start(out=keyf[c][:],
                                      in_=self.scr["kq"][c * 128:(c + 1) * 128, tl]),
                          sKQ, 16)
                    p.inc(s.dma_start(out=qryf[c][:],
                                      in_=self.scr["kq"][D + c * 128:D + (c + 1) * 128, tl]),
                          sKQ, 16)
                for c in range(NCH):
                    if c >= 2:
                        p.wait(s, sC, c - 1)
                    p.inc(s.dma_start(out=val[c % 2][:],
                                      in_=self.scr["P"][c * 128:(c + 1) * 128, tl]),
                          sL[c % 2], 16)
            ph.eng("sync", f_sync)

            def f_t(t):
                p.wait(t, "ldw0b")
                p.wait(t, sKQ, 16 * 2 * NCH)
                for c in range(NCH):
                    if c >= 1:
                        p.wait(t, sE, 2 * c)
                    for k in range(NCH):
                        ins = t.matmul(ppk[:], wkp[k][:, c * 128:(c + 1) * 128],
                                       keyf[k][:], start=(k == 0), stop=(k == NCH - 1))
                    p.inc(ins, sM, 1)
                    for k in range(NCH):
                        ins = t.matmul(ppq[:], wqp[k][:, c * 128:(c + 1) * 128],
                                       qryf[k][:], start=(k == 0), stop=(k == NCH - 1))
                    p.inc(ins, sM, 1)
                for c in range(NCH):
                    p.wait(t, sC, c + 1)
                    ins = t.matmul(pst_s[0:1, :], self.onesf[:], reti[c][:],
                                   start=(c == 0), stop=(c == NCH - 1))
                    p.wait(t, sQ2, c + 1)
                    ins = t.matmul(pst_q[0:1, :], self.onesf[:], sqb[c % 2][:],
                                   start=(c == 0), stop=(c == NCH - 1))
                    p.inc(ins, sMs, 1)
                p.wait(t, sR, 1)
                ins = t.matmul(pab[0][:], self.onesr[:], arow, start=True, stop=True)
                ins2 = t.matmul(pab[1][:], self.onesr[:], brow, start=True, stop=True)
                p.inc(ins2, sB, 1)
                for c in range(NCH):
                    p.wait(t, sA, NCH)
                    if c >= 2:
                        p.wait(t, sF, c - 1)
                    for k in range(NCH):
                        ins = t.matmul(ppo[c % 2][:], wo[k][:, c * 128:(c + 1) * 128],
                                       lno[k][:], start=(k == 0), stop=(k == NCH - 1))
                    p.inc(ins, sO, 1)
            ph.eng("tensor", f_t)

            def f_sc(s):
                for c in range(NCH):
                    p.wait(s, sM, 2 * c + 1)
                    if c >= 1:
                        p.wait(s, sC, c)
                    s.activation(kp[:], ppk[:], Act.Tanh, bias=C["bkp"][:, c:c + 1])
                    p.inc(s.drain(), sE, 1)
                    s.activation(skp[:], kp[:], Act.Sin, scale=C["ps_col"][:, c:c + 1], bias=self.zcol[:, 0:1])
                    s.drain()
                    s.activation(ckp[:], kp[:], Act.Abs)
                    s.drain()
                    s.activation(kp[:], ckp[:], Act.Sin,
                                 scale=self.negps[:, c:c + 1], bias=self.hpicol[:, 0:1])
                    p.inc(s.drain(), sKP, 1)
                    p.wait(s, sM, 2 * c + 2)
                    s.activation(qp[:], ppq[:], Act.Tanh, bias=C["bqp"][:, c:c + 1])
                    p.inc(s.drain(), sE, 1)
                    s.activation(sqp[:], qp[:], Act.Sin, scale=C["ps_col"][:, c:c + 1], bias=self.zcol[:, 0:1])
                    s.drain()
                    s.activation(cqp[:], qp[:], Act.Abs)
                    s.drain()
                    s.activation(qp[:], cqp[:], Act.Sin,
                                 scale=self.negps[:, c:c + 1], bias=self.hpicol[:, 0:1])
                    p.inc(s.drain(), sQP, 1)
                for c in range(NCH):
                    p.wait(s, sC, c + 1)
                    if c >= 2:
                        p.wait(s, sMs, c - 1)
                    p.inc(s.activation(sqb[c % 2][:], reti[c][:], Act.Square), sQ2, 1)
                p.wait(s, sW, 1)
                p.inc(s.activation(arow, rtmp, Act.Sqrt), sS, 1)
                for c in range(NCH):
                    p.wait(s, sO, c + 1)
                    if c >= 2:
                        p.wait(s, sX, c - 1)
                    p.inc(s.activation(tres[c % 2][:], ppo[c % 2][:], Act.Identity,
                                       scale=self.modfm[0][:, 16 + c:17 + c],
                                       bias=gabo[:, c:c + 1]), sF, 1)
            ph.eng("scalar", f_sc)

            def f_v(v):
                for c in range(NCH):
                    p.wait(v, sKP, c + 1)
                    p.wait(v, sL[c % 2], 16 * (c // 2 + 1))
                    v.tensor_tensor(ckp[:], val[c % 2][:], kp[:], Alu.mult)
                    v.drain()
                    v.tensor_tensor(skp[:], val[c % 2][:], skp[:], Alu.mult)
                    v.drain()
                    for src, dstb in ((ckp, mr), (skp, mi)):
                        v.tensor_tensor_scan(scn[:], src[:], src[:], 0.0,
                                             Alu.add, Alu.bypass)
                        v.drain()
                        v.tensor_copy(dstb[:, 0:64], scn[:, 0:64])
                        s3 = scn.ap().rearrange("pp (a b) -> pp a b", b=64)
                        bnd = s3[:, 0:7, 63:64].broadcast_to((128, 7, 64))
                        v.drain()
                        v.tensor_tensor(
                            dstb.ap().rearrange("pp (a b) -> pp a b", b=64)[:, 1:8, :],
                            s3[:, 1:8, :], bnd, Alu.subtract)
                        v.drain()
                    p.wait(v, sQP, c + 1)
                    v.tensor_tensor(mr[:], mr[:], qp[:], Alu.mult)
                    v.drain()
                    v.tensor_tensor(mi[:], mi[:], sqp[:], Alu.mult)
                    v.drain()
                    ins = v.tensor_tensor(reti[c][:], mr[:], mi[:], Alu.add)
                    v.drain()
                    p.inc(ins, sC, 1)
                p.wait(v, sMs, NCH)
                es = 1.0 / 32.0
                v.tensor_scalar_mul(rmean, pst_s[0:1, :], es / D)
                v.tensor_scalar_mul(rtmp, pst_q[0:1, :], es * es / D)
                v.drain()
                v.tensor_tensor(arow, rmean, rmean, Alu.mult)
                v.drain()
                v.tensor_tensor(rtmp, rtmp, arow, Alu.subtract)
                v.drain()
                v.tensor_scalar_add(rtmp, rtmp, EPS)
                v.drain()
                v.reciprocal(rtmp, rtmp)
                p.inc(v.drain(), sW, 1)
                p.wait(v, sS, 1)
                v.tensor_tensor(brow, rmean, arow, Alu.mult)
                v.drain()
                v.tensor_scalar_mul(brow, brow, -1.0)
                v.tensor_scalar_mul(arow, arow, es)
                v.drain()
                p.inc(v.drain(), sR, 1)
                p.wait(v, sB, 1)
                for c in range(NCH):
                    v.tensor_tensor(sqb[0][:], reti[c][:], pab[0][:], Alu.mult)
                    v.drain()
                    v.tensor_tensor(sqb[0][:], sqb[0][:], pab[1][:], Alu.add)
                    v.drain()
                    ins = v.tensor_scalar(lno[c][:], sqb[0][:],
                                          C["ln_g"][:, c:c + 1], C["ln_b"][:, c:c + 1],
                                          Alu.mult, Alu.add)
                    v.drain()
                    p.inc(ins, sA, 1)
                for c in range(NCH):
                    p.wait(v, sF, c + 1)
                    ins = v.tensor_tensor(self.xres[c][:, tl], self.xres[c][:, tl],
                                          tres[c % 2][:], Alu.add)
                    v.drain()
                    p.inc(ins, sX, 1)
            ph.eng("vector", f_v)

    def emit_armax(self, tag, cols, ccpair, res):
        p = self.p
        cc_in, cc_out = ccpair
        n = len(cols)
        with ExitStack() as sc, p.phase(tag) as ph:
            gbuf = p.sb(tag + "_g", [8, 128], F32, sc)
            gcol = p.sb(tag + "_c", [8, 1], F32, sc)
            zrow = p.sb(tag + "_z", [1, 8], F32, sc)
            sD, sZ, sV, sG = (tag + x for x in "DZVG")

            def f_v(v):
                v.memset(zrow[:], 0.0)
                p.inc(v.drain(), sZ, 1)
                p.wait(v, sD, 16)
                ins = v.tensor_reduce(gcol[0:n, :], gbuf[0:n, :], AX.X, Alu.max)
                v.drain()
                p.inc(ins, sV, 1)
            ph.eng("vector", f_v)

            def f_sync(s):
                p.wait(s, sZ, 1)
                p.inc(s.dma_start(out=cc_in, in_=zrow[:]), sD + "z", 16)
                for i, col in enumerate(cols):
                    p.inc(s.dma_start(out=self.scr["row"][32 + i:33 + i, 0:128],
                                      in_=col), sD + "a", 16)
                p.wait(s, sD + "a", p.val(sD + "a"))
                p.inc(s.dma_start(out=gbuf[0:n, :],
                                  in_=self.scr["row"][32:32 + n, 0:128]), sD, 16)
                p.wait(s, sV, 1)
                p.wait(s, sD + "z", 16)
                p.inc(s.dma_start(out=cc_in[0:1, 0:n], in_=gcol[0:n, :]), sD + "i", 16)
                p.wait(s, sG, 1)
                p.inc(s.dma_start(out=res[:], in_=cc_out), sD + "o", 16)
                p.wait(s, sD + "o", 16)
            ph.eng("sync", f_sync)

            def f_g(g):
                p.wait(g, sD + "i", 16)
                p.inc(g.collective_compute(
                    "AllReduce", Alu.max,
                    replica_groups=[list(range(B))],
                    ins=[cc_in], outs=[cc_out]), sG, 1)
            ph.eng("gpsimd", f_g)
        return res

    def bcast_cols(self, tag, pairs):
        p = self.p
        with ExitStack() as sc, p.phase(tag) as ph:
            pp = p.ps(tag + "_ps", [128, max(1, len(pairs))], F32, sc)
            sM = tag + "M"

            def f_t(t):
                for i, (r, _) in enumerate(pairs):
                    ins = t.matmul(pp[:, i:i + 1], self.onesr[:], r,
                                   start=(i == 0), stop=(i == len(pairs) - 1))
                p.inc(ins, sM, 1)
            ph.eng("tensor", f_t)

            def f_sc(s):
                p.wait(s, sM, 1)
                for i, (_, oc) in enumerate(pairs):
                    s.copy(oc, pp[:, i:i + 1])
                s.drain()
            ph.eng("scalar", f_sc)

    def l0_ffn(self, scf):
        p, dI = self.p, self.dI
        sca = p.sb("ffn_sca", [1, 16], F32, scf)
        arres = [p.sb(f"ffn_arr{i}", [1, 8], F32, scf) for i in range(4)]
        xq = [p.sb(f"ffn_xq{c}", [128, T], BF16, scf) for c in range(NCH)]
        mxh = p.sb("ffn_mxh", [128, 1], F32, scf)
        rxc = p.sb("ffn_rxc", [128, 1], F32, scf)

        with ExitStack() as sch:
            h2 = [p.sb(f"h2_{c}", [128, T], F32, sch) for c in range(NCH)]
            self.ln_mod(dst=h2, ca=self.modfm[0][:, 32:40],
                        sa=self.modfm[0][:, 24:32], tag="ln2")
            self.tap("ln2", [h.ap() for h in h2])
            if self.stop("ln2"):
                return True

            with ExitStack() as sc, p.phase("ffn_mh") as ph:
                tcol = p.sb("ffn_mh_t", [128, 1], F32, sc)

                def f_v(v):
                    for c in range(NCH):
                        dst = mxh[:] if c == 0 else tcol[:]
                        v.tensor_reduce(dst, h2[c][:], AX.X, Alu.max,
                                        apply_absolute_value=True)
                        v.drain()
                        if c:
                            v.tensor_tensor(mxh[:], mxh[:], tcol[:], Alu.max)
                            v.drain()
                ph.eng("vector", f_v)

            ar1 = self.emit_armax("ffn_ar1", [mxh[:]], self.cc[0], arres[0])
            with p.phase("ffn_s1") as ph:
                def f_v(v):
                    v.tensor_scalar_max(sca[0:1, 0:1], ar1[0:1, 0:1], 32767.0e-12)
                    v.drain()
                    v.reciprocal(sca[0:1, 1:2], sca[0:1, 0:1])
                    v.drain()
                    v.tensor_scalar_mul(sca[0:1, 1:2], sca[0:1, 1:2], 32767.0)
                    v.drain()
                ph.eng("vector", f_v)
            self.bcast_cols("ffn_b1", [(sca[0:1, 1:2], rxc[:])])

            with ExitStack() as sc, p.phase("ffn_xq") as ph:
                ui = [p.sb(f"ffn_xq_i{j}", [128, T], I32, sc) for j in range(2)]
                sA, sK = "xqA", "xqK"

                def f_sc(s2):
                    for c in range(NCH):
                        if c >= 2:
                            p.wait(s2, sK, c - 1)
                        p.inc(s2.activation(ui[c % 2][:], h2[c][:],
                                            Act.Identity,
                                            scale=rxc[:, 0:1]), sA, 1)
                ph.eng("scalar", f_sc)

                def f_v(v):
                    for c in range(NCH):
                        p.wait(v, sA, c + 1)
                        ins = v.tensor_scalar(xq[c][:], ui[c % 2][:],
                                              -32768.0, 32767.0,
                                              Alu.max, Alu.min)
                        v.drain()
                        p.inc(ins, sK, 1)
                ph.eng("vector", f_v)
        self.tap("ffnxq", [x.ap() for x in xq])
        if self.stop("ffnxq"):
            return True

        mxu = p.sb("ffn_mxu", [128, 1], F32, scf)
        mxg = p.sb("ffn_mxg", [128, 1], F32, scf)
        self._ffn_passA(xq, "w_up", self.scr["P"], mxu, "fAu")
        self.tap_dram("fAuP", self.scr["P"], DI)
        self._ffn_passA(xq, "w_gate", self.scr["G"], mxg, "fAg")
        self.tap_dram("fAgG", self.scr["G"], DI)
        ar2 = self.emit_armax("ffn_ar2", [mxu[:], mxg[:]], self.cc[1], arres[1])

        with p.phase("ffn_s2") as ph:
            def f_v(v):
                v.tensor_scalar_max(sca[0:1, 2:4], ar2[0:1, 0:2], 1e-30)
                v.drain()
                v.reciprocal(sca[0:1, 4:6], sca[0:1, 2:4])
                v.drain()
                v.tensor_scalar_mul(sca[0:1, 4:6], sca[0:1, 4:6], 32767.0)
                v.drain()
                v.tensor_tensor(sca[0:1, 6:7], sca[0:1, 0:1], sca[0:1, 3:4],
                                Alu.mult)
                v.drain()
                v.tensor_scalar_mul(sca[0:1, 6:7], sca[0:1, 6:7],
                                    1.0 / (32767.0 * 32767.0))
                v.drain()
            ph.eng("vector", f_v)
        kuc = p.sb("ffn_kuc", [128, 1], F32, scf)
        kgc = p.sb("ffn_kgc", [128, 1], F32, scf)
        sgc = p.sb("ffn_sgc", [128, 1], F32, scf)
        self.bcast_cols("ffn_b2", [(sca[0:1, 4:5], kuc[:]),
                                   (sca[0:1, 5:6], kgc[:]),
                                   (sca[0:1, 6:7], sgc[:])])

        mx2 = p.sb("ffn_mx2", [128, 1], F32, scf)
        self._ffn_passB(kuc, kgc, sgc, mx2)
        self.tap_dram("fBUP", self.scr["UP"], DI)
        self.tap_dram("fBG2", self.scr["G"], DI)
        ar3 = self.emit_armax("ffn_ar3", [mx2[:]], self.cc[2], arres[2])
        with p.phase("ffn_s3") as ph:
            def f_v(v):
                v.tensor_scalar_max(sca[0:1, 7:8], ar3[0:1, 0:1], 1e-30)
                v.drain()
                v.reciprocal(sca[0:1, 8:9], sca[0:1, 7:8])
                v.drain()
                v.tensor_scalar_mul(sca[0:1, 8:9], sca[0:1, 8:9], 32767.0)
                v.drain()
            ph.eng("vector", f_v)
        k2c = p.sb("ffn_k2c", [128, 1], F32, scf)
        self.bcast_cols("ffn_b3", [(sca[0:1, 8:9], k2c[:])])

        mxo = p.sb("ffn_mxo", [128, 1], F32, scf)
        self._ffn_passC(k2c, mxo)
        self.tap_dram("fCO", self.scr["P"], DI)
        ar4 = self.emit_armax("ffn_ar4", [mxo[:]], self.cc[3], arres[3])
        with p.phase("ffn_s4") as ph:
            def f_v(v):
                v.tensor_scalar_max(sca[0:1, 9:10], ar4[0:1, 0:1], 1e-30)
                v.drain()
                v.reciprocal(sca[0:1, 10:11], sca[0:1, 9:10])
                v.drain()
                v.tensor_scalar_mul(sca[0:1, 10:11], sca[0:1, 10:11], 32767.0)
                v.drain()
                v.tensor_tensor(sca[0:1, 11:12], sca[0:1, 6:7], sca[0:1, 7:8],
                                Alu.mult)
                v.drain()
                v.tensor_scalar_mul(sca[0:1, 11:12], sca[0:1, 11:12],
                                    1.0 / (65536.0 * 32767.0))
                v.drain()
                v.tensor_tensor(sca[0:1, 12:13], sca[0:1, 0:1], sca[0:1, 2:3],
                                Alu.mult)
                v.drain()
                v.tensor_scalar_mul(sca[0:1, 12:13], sca[0:1, 12:13],
                                    1.0 / (32767.0 * 32767.0))
                v.drain()
                v.tensor_tensor(sca[0:1, 11:12], sca[0:1, 11:12],
                                sca[0:1, 12:13], Alu.mult)
                v.drain()
                v.tensor_tensor(sca[0:1, 11:12], sca[0:1, 11:12],
                                sca[0:1, 9:10], Alu.mult)
                v.drain()
                v.tensor_scalar_mul(sca[0:1, 11:12], sca[0:1, 11:12],
                                    1.0 / 32767.0)
                v.drain()
            ph.eng("vector", f_v)
        koc = p.sb("ffn_koc", [128, 1], F32, scf)
        soc = p.sb("ffn_soc", [128, 1], F32, scf)
        self.bcast_cols("ffn_b4", [(sca[0:1, 10:11], koc[:]),
                                   (sca[0:1, 11:12], soc[:])])
        gbso = p.sb("ffn_gbso", [128, 8], F32, scf)
        with p.phase("ffn_gb") as ph:
            def f_v(v):
                v.tensor_tensor(gbso[:], self.modfm[0][:, 40:48],
                                soc[:, 0:1].broadcast_to((128, 8)), Alu.mult)
                v.drain()
            ph.eng("vector", f_v)

        self._ffn_passD(koc, gbso)
        return False

    def _ffn_passA(self, xq, wname, spill, mxcol, tg):
        p, dI = self.p, self.dI
        ECH = DI // 128
        with ExitStack() as sc, p.phase(tg) as ph:
            wnat = [p.sb(tg + f"wn{j}", [128, D], F32, sc) for j in range(2)]
            srec = p.sb(tg + "srec", [128, 32], F32, sc)
            sblk = p.sb(tg + "sblk", [128, 32], F32, sc)
            ui = p.sb(tg + "ui", [128, D], I32, sc)
            wqb = [p.sb(tg + f"wqb{j}", [128, D], BF16, sc) for j in range(2)]
            wsl = [p.sb(tg + f"wsl{j}", [128, D], BF16, sc) for j in range(2)]
            ptile = [p.sb(tg + f"pt{j}", [128, T], F32, sc) for j in range(2)]
            tcol = p.sb(tg + "tc", [128, 1], F32, sc)
            ptr = p.ps(tg + "ptr", [128, D], BF16, sc)
            pmm = [p.ps(tg + f"pmm{j}", [128, TTS], F32, sc) for j in range(2)]
            sW = [tg + f"W{j}" for j in range(2)]
            sDq, sTr, sWs, sMM, sPE, sRd, sSp = (tg + x for x in
                                                 ("q", "t", "s", "m", "e", "r", "p"))

            def f_g(g):
                for e in range(ECH):
                    if e >= 2:
                        p.wait(g, sDq, e - 1)
                    p.inc(g.dma_start(out=wnat[e % 2][:],
                                      in_=dI[wname][e * 128:(e + 1) * 128, :]),
                          sW[e % 2], 16)
            ph.eng("gpsimd", f_g)

            def f_v(v):
                for e in range(ECH):
                    p.wait(v, sW[e % 2], 16 * (e // 2 + 1))
                    if e >= 2:
                        p.wait(v, sTr, 8 * (e - 1))
                    w3 = wnat[e % 2].ap().rearrange("pp (a b) -> pp a b", b=32)
                    v.tensor_reduce(sblk[:], w3, AX.X, Alu.max,
                                    apply_absolute_value=True)
                    v.drain()
                    v.tensor_scalar(sblk[:], sblk[:], 7.0e-12, 1.0 / 7.0,
                                    Alu.max, Alu.mult)
                    v.drain()
                    v.reciprocal(srec[:], sblk[:])
                    v.drain()
                    v.tensor_tensor(
                        w3, w3,
                        srec.ap().rearrange("pp a -> pp a ()").broadcast_to((128, 32, 32)),
                        Alu.mult)
                    v.drain()
                    v.tensor_copy(ui[:], wnat[e % 2][:])
                    v.drain()
                    v.tensor_scalar(wnat[e % 2][:], ui[:], -8.0, 7.0,
                                    Alu.max, Alu.min)
                    v.drain()
                    ins = v.tensor_tensor(
                        wqb[e % 2].ap().rearrange("pp (a b) -> pp a b", b=32),
                        w3,
                        sblk.ap().rearrange("pp a -> pp a ()").broadcast_to((128, 32, 32)),
                        Alu.mult)
                    v.drain()
                    p.inc(ins, sDq, 1)
                    p.wait(v, sPE, 4 * e + 4)
                    v.tensor_reduce(tcol[:], ptile[e % 2][:], AX.X, Alu.max,
                                    apply_absolute_value=True)
                    v.drain()
                    if e == 0:
                        ins = v.tensor_copy(mxcol[:], tcol[:])
                    else:
                        ins = v.tensor_tensor(mxcol[:], mxcol[:], tcol[:], Alu.max)
                    v.drain()
                    p.inc(ins, sRd, 1)
            ph.eng("vector", f_v)

            def f_t(t):
                p.wait(t, "ldc")
                for e in range(ECH):
                    p.wait(t, sDq, e + 1)
                    p.wait(t, sWs, 8 * e)
                    for k in range(NCH):
                        p.inc(t.transpose(ptr[:, k * 128:(k + 1) * 128],
                                          wqb[e % 2][:, k * 128:(k + 1) * 128],
                                          self.identb[:]), sTr, 1)
                    p.wait(t, sWs, 8 * e + 8)
                    for n in range(NT):
                        if e * NT + n >= 2:
                            p.wait(t, sPE, (e * NT + n) - 1)
                        for k in range(NCH):
                            ins = t.matmul(pmm[n % 2][:],
                                           wsl[e % 2][:, k * 128:(k + 1) * 128],
                                           xq[k][:, n * TTS:(n + 1) * TTS],
                                           start=(k == 0), stop=(k == NCH - 1))
                        p.inc(ins, sMM, 1)
            ph.eng("tensor", f_t)

            def f_sc(s):
                for e in range(ECH):
                    p.wait(s, sTr, 8 * e + 8)
                    for k in range(NCH):
                        p.inc(s.copy(wsl[e % 2][:, k * 128:(k + 1) * 128],
                                     ptr[:, k * 128:(k + 1) * 128]), sWs, 1)
                    for n in range(NT):
                        p.wait(s, sMM, e * NT + n + 1)
                        if e >= 2 and n == 0:
                            p.wait(s, sSp, 16 * (e - 1))
                        p.inc(s.copy(ptile[e % 2][:, n * TTS:(n + 1) * TTS],
                                     pmm[n % 2][:]), sPE, 1)
            ph.eng("scalar", f_sc)

            def f_sync(s):
                for e in range(ECH):
                    p.wait(s, sRd, e + 1)
                    p.inc(s.dma_start(out=spill[e * 128:(e + 1) * 128, :],
                                      in_=ptile[e % 2][:]), sSp, 16)
                p.wait(s, sSp, 16 * ECH)
            ph.eng("sync", f_sync)

    def _ffn_passB(self, kuc, kgc, sgc, mx2):
        p = self.p
        ECH = DI // 128
        with ExitStack() as sc, p.phase("fB") as ph:
            pt = [p.sb(f"fBp{j}", [128, T], F32, sc) for j in range(2)]
            gt = [p.sb(f"fBg{j}", [128, T], F32, sc) for j in range(2)]
            u = p.sb("fBu", [128, T], F32, sc)
            ui = p.sb("fBi", [128, T], I32, sc)
            uiA = p.sb("fBiA", [128, T], I32, sc)
            uiB = p.sb("fBiB", [128, T], I32, sc)
            upi = [p.sb(f"fBup{j}", [128, T], mybir.dt.int16, sc) for j in range(2)]
            sig = p.sb("fBsig", [128, T], F32, sc)
            tcol = p.sb("fBtc", [128, 1], F32, sc)
            sLp = [f"fBLp{j}" for j in range(2)]
            sLg = [f"fBLg{j}" for j in range(2)]
            sV, sS, sSg = "fBV", "fBS", "fBSg"
            sA1, sA2, sVa, sVb = "fBA1", "fBA2", "fBva", "fBvb"
            sSpU, sSpG = "fBSpU", "fBSpG"

            def f_sync(s):
                for e in range(ECH + 2):
                    if e >= 2:
                        p.wait(s, sV, e - 1)
                        p.inc(s.dma_start(
                            out=self.scr["UP"][(e - 2) * 128:(e - 1) * 128, :],
                            in_=upi[e % 2][:]), sSpU, 16)
                        p.inc(s.dma_start(
                            out=self.scr["G"][(e - 2) * 128:(e - 1) * 128, :],
                            in_=gt[e % 2][:]), sSpG, 16)
                    if e < ECH:
                        if e >= 2:
                            p.wait(s, sSpG, 16 * (e - 1))
                        p.inc(s.dma_start(
                            out=pt[e % 2][:],
                            in_=self.scr["P"][e * 128:(e + 1) * 128, :]),
                            sLp[e % 2], 16)
                        p.inc(s.dma_start(
                            out=gt[e % 2][:],
                            in_=self.scr["G"][e * 128:(e + 1) * 128, :]),
                            sLg[e % 2], 16)
                p.wait(s, sSpU, 16 * ECH)
                p.wait(s, sSpG, 16 * ECH)
            ph.eng("sync", f_sync)

            def f_sc(s):
                for e in range(ECH):
                    p.wait(s, sLp[e % 2], 16 * (e // 2 + 1))
                    if e >= 1:
                        p.wait(s, sVa, e)
                    p.inc(s.activation(uiA[:], pt[e % 2][:], Act.Identity,
                                       scale=kuc[:, 0:1]), sA1, 1)
                    p.wait(s, sLg[e % 2], 16 * (e // 2 + 1))
                    if e >= 1:
                        p.wait(s, sVb, e)
                    p.inc(s.activation(uiB[:], gt[e % 2][:], Act.Identity,
                                       scale=kgc[:, 0:1]), sA2, 1)
                    p.wait(s, sS, e + 1)
                    s.activation(sig[:], gt[e % 2][:], Act.Sigmoid,
                                 scale=sgc[:, 0:1], bias=self.zcol[:, 0:1])
                    p.inc(s.drain(), sSg, 1)
            ph.eng("scalar", f_sc)

            def f_v(v):
                for e in range(ECH):
                    p.wait(v, sA1, e + 1)
                    if e >= 2:
                        p.wait(v, sSpU, 16 * (e - 1))
                    ins = v.tensor_scalar(upi[e % 2][:], uiA[:],
                                          -32768.0, 32767.0,
                                          Alu.max, Alu.min)
                    v.drain()
                    p.inc(ins, sVa, 1)
                    p.wait(v, sA2, e + 1)
                    if e >= 2:
                        p.wait(v, sSpG, 16 * (e - 1))
                    ins = v.tensor_scalar(gt[e % 2][:], uiB[:],
                                          -32768.0, 32767.0,
                                          Alu.max, Alu.min)
                    v.drain()
                    p.inc(ins, sVb, 1)
                    p.inc(v.drain(), sS, 1)
                    p.wait(v, sSg, e + 1)
                    v.tensor_scalar_mul(u[:], sig[:], 65536.0)
                    v.drain()
                    v.tensor_copy(ui[:], u[:])
                    v.drain()
                    v.tensor_scalar(sig[:], ui[:], 0.0, 65535.0, Alu.max, Alu.min)
                    v.drain()
                    v.tensor_tensor(gt[e % 2][:], gt[e % 2][:], sig[:], Alu.mult)
                    v.drain()
                    v.tensor_reduce(tcol[:], gt[e % 2][:], AX.X, Alu.max,
                                    apply_absolute_value=True)
                    v.drain()
                    if e == 0:
                        v.tensor_copy(mx2[:], tcol[:])
                    else:
                        v.tensor_tensor(mx2[:], mx2[:], tcol[:], Alu.max)
                    p.inc(v.drain(), sV, 1)
            ph.eng("vector", f_v)

    def _ffn_passC(self, k2c, mxo):
        p = self.p
        ECH = DI // 128
        with ExitStack() as sc, p.phase("fC") as ph:
            gt = [p.sb(f"fCg{j}", [128, T], F32, sc) for j in range(2)]
            upi = [p.sb(f"fCu{j}", [128, T], mybir.dt.int16, sc) for j in range(2)]
            u = p.sb("fCuu", [128, T], F32, sc)
            ui = p.sb("fCi", [128, T], I32, sc)
            tcol = p.sb("fCtc", [128, 1], F32, sc)
            sLg = [f"fCLg{j}" for j in range(2)]
            sLu = [f"fCLu{j}" for j in range(2)]
            sV, sSp = "fCV", "fCSp"
            sA1, sVa = "fCA1", "fCva"

            def f_sync(s):
                for e in range(ECH + 2):
                    if e >= 2:
                        p.wait(s, sV, e - 1)
                        p.inc(s.dma_start(
                            out=self.scr["P"][(e - 2) * 128:(e - 1) * 128, :],
                            in_=gt[e % 2][:]), sSp, 16)
                    if e < ECH:
                        if e >= 2:
                            p.wait(s, sSp, 16 * (e - 1))
                        p.inc(s.dma_start(
                            out=gt[e % 2][:],
                            in_=self.scr["G"][e * 128:(e + 1) * 128, :]),
                            sLg[e % 2], 16)
                        p.inc(s.dma_start(
                            out=upi[e % 2][:],
                            in_=self.scr["UP"][e * 128:(e + 1) * 128, :]),
                            sLu[e % 2], 16)
                p.wait(s, sSp, 16 * ECH)
            ph.eng("sync", f_sync)

            def f_sc(s):
                for e in range(ECH):
                    p.wait(s, sLg[e % 2], 16 * (e // 2 + 1))
                    if e >= 1:
                        p.wait(s, sVa, e)
                    p.inc(s.activation(ui[:], gt[e % 2][:], Act.Identity,
                                       scale=k2c[:, 0:1]), sA1, 1)
            ph.eng("scalar", f_sc)

            def f_v(v):
                for e in range(ECH):
                    p.wait(v, sA1, e + 1)
                    p.wait(v, sLu[e % 2], 16 * (e // 2 + 1))
                    ins = v.tensor_scalar(u[:], ui[:], -32768.0, 32767.0,
                                          Alu.max, Alu.min)
                    v.drain()
                    p.inc(ins, sVa, 1)
                    if e >= 2:
                        p.wait(v, sSp, 16 * (e - 1))
                    v.tensor_tensor(gt[e % 2][:], u[:], upi[e % 2][:], Alu.mult)
                    v.drain()
                    v.tensor_reduce(tcol[:], gt[e % 2][:], AX.X, Alu.max,
                                    apply_absolute_value=True)
                    v.drain()
                    if e == 0:
                        v.tensor_copy(mxo[:], tcol[:])
                    else:
                        v.tensor_tensor(mxo[:], mxo[:], tcol[:], Alu.max)
                    p.inc(v.drain(), sV, 1)
            ph.eng("vector", f_v)

    def _ffn_passD(self, koc, gbso):
        p, dI = self.p, self.dI
        KCH = DI // 128
        with ExitStack() as scw:
            wdq = [p.sb(f"fDw{c}", [128, 3 * D], BF16, scw) for c in range(NCH)]
            with ExitStack() as sc, p.phase("fDw") as ph:
                wnat = [p.sb(f"fDn{j}", [128, 3 * D], F32, sc) for j in range(2)]
                srec = p.sb("fDsr", [128, 96], F32, sc)
                sblk = p.sb("fDsb", [128, 96], F32, sc)
                ui = p.sb("fDi", [128, 3 * D], I32, sc)
                wqb = [p.sb(f"fDq{j}", [128, 3 * D], BF16, sc) for j in range(2)]
                ptr = p.ps("fDptr", [128, 1024], BF16, sc)
                sW = [f"fDW{j}" for j in range(2)]
                sDq, sTr, sWs = "fDq", "fDt", "fDs"

                def f_g(g):
                    for c in range(NCH):
                        if c >= 2:
                            p.wait(g, sDq, c - 1)
                        p.inc(g.dma_start(out=wnat[c % 2][:],
                                          in_=dI["w_down"][c * 128:(c + 1) * 128, :]),
                              sW[c % 2], 16)
                ph.eng("gpsimd", f_g)

                def f_v(v):
                    for c in range(NCH):
                        p.wait(v, sW[c % 2], 16 * (c // 2 + 1))
                        if c >= 2:
                            p.wait(v, sTr, 24 * (c - 1))
                        w3 = wnat[c % 2].ap().rearrange("pp (a b) -> pp a b", b=32)
                        v.tensor_reduce(sblk[:], w3, AX.X, Alu.max,
                                        apply_absolute_value=True)
                        v.drain()
                        v.tensor_scalar(sblk[:], sblk[:], 7.0e-12, 1.0 / 7.0,
                                        Alu.max, Alu.mult)
                        v.drain()
                        v.reciprocal(srec[:], sblk[:])
                        v.drain()
                        v.tensor_tensor(
                            w3, w3,
                            srec.ap().rearrange("pp a -> pp a ()").broadcast_to((128, 96, 32)),
                            Alu.mult)
                        v.drain()
                        v.tensor_copy(ui[:], wnat[c % 2][:])
                        v.drain()
                        v.tensor_scalar(wnat[c % 2][:], ui[:], -8.0, 7.0,
                                        Alu.max, Alu.min)
                        v.drain()
                        ins = v.tensor_tensor(
                            wqb[c % 2].ap().rearrange("pp (a b) -> pp a b", b=32),
                            w3,
                            sblk.ap().rearrange("pp a -> pp a ()").broadcast_to((128, 96, 32)),
                            Alu.mult)
                        v.drain()
                        p.inc(ins, sDq, 1)
                ph.eng("vector", f_v)

                def f_t(t):
                    p.wait(t, "ldc")
                    for c in range(NCH):
                        p.wait(t, sDq, c + 1)
                        for g8 in range(3):
                            p.wait(t, sWs, (c * 3 + g8) * 8)
                            for k8 in range(8):
                                k = g8 * 8 + k8
                                p.inc(t.transpose(
                                    ptr[:, k8 * 128:(k8 + 1) * 128],
                                    wqb[c % 2][:, k * 128:(k + 1) * 128],
                                    self.identb[:]), sTr, 1)
                ph.eng("tensor", f_t)

                def f_sc(s):
                    for c in range(NCH):
                        for g8 in range(3):
                            p.wait(s, sTr, (c * 3 + g8) * 8 + 8)
                            for k8 in range(8):
                                k = g8 * 8 + k8
                                p.inc(s.copy(wdq[c][:, k * 128:(k + 1) * 128],
                                             ptr[:, k8 * 128:(k8 + 1) * 128]),
                                      sWs, 1)
                ph.eng("scalar", f_sc)

            with ExitStack() as sc, p.phase("fD") as ph:
                ot = [p.sb(f"fDo{j}", [128, TTS], F32, sc) for j in range(2)]
                obf = [p.sb(f"fDb{k}", [128, TTS], BF16, sc) for k in range(KCH)]
                u2 = p.sb("fDu2", [128, TTS], F32, sc)
                ui2 = p.sb("fDi2", [128, TTS], I32, sc)
                tres = [p.sb(f"fDt{j}", [128, TTS], F32, sc) for j in range(2)]
                pdo = [p.ps(f"fDp{j}", [128, TTS], F32, sc) for j in range(8)]
                sL = [f"fDL{j}" for j in range(2)]
                sQ, sMM, sEv, sX = "fDQ", "fDM", "fDE", "fDX"

                def f_sync(s):
                    for n in range(NT):
                        for k in range(KCH):
                            i = n * KCH + k
                            if i >= 2:
                                p.wait(s, sQ, i - 1)
                            p.inc(s.dma_start(
                                out=ot[i % 2][:],
                                in_=self.scr["P"][k * 128:(k + 1) * 128,
                                                  n * TTS:(n + 1) * TTS]),
                                sL[i % 2], 16)
                ph.eng("sync", f_sync)

                def f_v(v):
                    for n in range(NT):
                        for k in range(KCH):
                            i = n * KCH + k
                            p.wait(v, sL[i % 2], 16 * (i // 2 + 1))
                            if n >= 1 and k == 0:
                                p.wait(v, sMM, 8 * n)
                            v.tensor_scalar_mul(u2[:], ot[i % 2][:], koc[:, 0:1])
                            v.drain()
                            v.tensor_copy(ui2[:], u2[:])
                            v.drain()
                            ins = v.tensor_scalar(obf[k][:], ui2[:],
                                                  -32768.0, 32767.0,
                                                  Alu.max, Alu.min)
                            v.drain()
                            p.inc(ins, sQ, 1)
                        for c in range(NCH):
                            p.wait(v, sEv, n * NCH + c + 1)
                            ins = v.tensor_tensor(
                                self.xres[c][:, n * TTS:(n + 1) * TTS],
                                self.xres[c][:, n * TTS:(n + 1) * TTS],
                                tres[c % 2][:], Alu.add)
                            v.drain()
                            p.inc(ins, sX, 1)
                ph.eng("vector", f_v)

                def f_t(t):
                    for n in range(NT):
                        p.wait(t, sQ, (n + 1) * KCH)
                        for c in range(NCH):
                            if n >= 1:
                                p.wait(t, sEv, (n - 1) * NCH + c + 1)
                            for k in range(KCH):
                                ins = t.matmul(pdo[c][:],
                                               wdq[c][:, k * 128:(k + 1) * 128],
                                               obf[k][:],
                                               start=(k == 0), stop=(k == KCH - 1))
                            p.inc(ins, sMM, 1)
                ph.eng("tensor", f_t)

                def f_sc(s):
                    for n in range(NT):
                        for c in range(NCH):
                            p.wait(s, sMM, n * NCH + c + 1)
                            if n * NCH + c >= 2:
                                p.wait(s, sX, n * NCH + c - 1)
                            p.inc(s.activation(tres[c % 2][:], pdo[c][:],
                                               Act.Identity,
                                               scale=gbso[:, c:c + 1],
                                               bias=self.zcol[:, 0:1]), sEv, 1)
                ph.eng("scalar", f_sc)


    def l1_attn(self, hb, sc0):
        p, dI = self.p, self.dI
        with ExitStack() as sc, p.phase("l1qk") as ph:
            qw = [p.sb(f"l1qw{c}", [128, D], BF16, sc) for c in range(NCH)]
            kw = [p.sb(f"l1kw{c}", [128, D], BF16, sc) for c in range(NCH)]
            rcos = p.sb("l1rc", [128, T], F32, sc)
            rsin = p.sb("l1rs", [128, T], F32, sc)
            kb8 = p.sb("l1kb8", [128, 8], F32, sc)
            qt = p.sb("l1qt", [128, TTS], F32, sc)
            kt = p.sb("l1kt", [128, TTS], F32, sc)
            sh = p.sb("l1sh", [128, TTS], F32, sc)
            t1 = p.sb("l1t1", [128, TTS], F32, sc)
            qtl = [p.sb(f"l1qtl{j}", [128, TTS], BF16, sc) for j in range(2)]
            ktl = [p.sb(f"l1ktl{j}", [128, TTS], BF16, sc) for j in range(2)]
            ppq = p.ps("l1ppq", [128, TTS], F32, sc)
            ppk = p.ps("l1ppk", [128, TTS], F32, sc)
            sW, sM, sE, sV, sB8 = "l1qkW", "l1qkM", "l1qkE", "l1qkV", "l1qkB"
            sQs = [f"l1qkQ{j}" for j in range(2)]
            sKs = [f"l1qkK{j}" for j in range(2)]

            def f_g(g):
                for c in range(NCH):
                    p.inc(g.dma_start(out=qw[c][:],
                                      in_=dI["q_w"][c * 128:(c + 1) * 128, :]), sW, 16)
                    p.inc(g.dma_start(out=kw[c][:],
                                      in_=dI["k_w"][c * 128:(c + 1) * 128, :]), sW, 16)
                p.inc(g.dma_start(out=rcos[:], in_=dI["rot_cos"]), sW, 16)
                p.inc(g.dma_start(out=rsin[:], in_=dI["rot_sin"]), sW, 16)
            ph.eng("gpsimd", f_g)

            def f_v(v):
                p.wait(v, "ldc")
                v.tensor_scalar_mul(kb8[:], self.cols["k_b"][:], 0.125)
                p.inc(v.drain(), sB8, 1)
                for i in range(NCH * NT):
                    c, n = divmod(i, NT)
                    tl = slice(n * TTS, (n + 1) * TTS)
                    p.wait(v, sE, 2 * i + 2)
                    if i >= 2:
                        p.wait(v, sQs[i % 2], 16 * ((i - 2) // 2 + 1))
                        p.wait(v, sKs[i % 2], 16 * ((i - 2) // 2 + 1))
                    for src_, dst in ((qt, qtl), (kt, ktl)):
                        v.stream_shuffle(sh[:], src_[:], [j ^ 1 for j in range(32)])
                        v.drain()
                        v.tensor_tensor(t1[:], src_[:], rcos[:, tl], Alu.mult)
                        v.drain()
                        v.tensor_tensor(sh[:], sh[:], rsin[:, tl], Alu.mult)
                        v.drain()
                        v.tensor_tensor(dst[i % 2][:], t1[:], sh[:], Alu.add)
                        v.drain()
                    p.inc(v.drain(), sV, 1)
            ph.eng("vector", f_v)

            def f_sync(s):
                for i in range(NCH * NT):
                    c, n = divmod(i, NT)
                    tl = slice(n * TTS, (n + 1) * TTS)
                    p.wait(s, sV, i + 1)
                    p.inc(s.dma_start(out=self.scr["kq"][c * 128:(c + 1) * 128, tl],
                                      in_=qtl[i % 2][:]), sQs[i % 2], 16)
                    p.inc(s.dma_start(
                        out=self.scr["kq"][D + c * 128:D + (c + 1) * 128, tl],
                        in_=ktl[i % 2][:]), sKs[i % 2], 16)
                for j in range(2):
                    p.wait(s, sQs[j])
                    p.wait(s, sKs[j])
            ph.eng("sync", f_sync)

            def f_t(t):
                p.wait(t, sW)
                for i in range(NCH * NT):
                    c, n = divmod(i, NT)
                    if i >= 1:
                        p.wait(t, sE, 2 * i)
                    for k in range(NCH):
                        ins = t.matmul(ppq[:], qw[k][:, c * 128:(c + 1) * 128],
                                       hb[k][:, n * TTS:(n + 1) * TTS],
                                       start=(k == 0), stop=(k == NCH - 1))
                    p.inc(ins, sM, 1)
                    for k in range(NCH):
                        ins = t.matmul(ppk[:], kw[k][:, c * 128:(c + 1) * 128],
                                       hb[k][:, n * TTS:(n + 1) * TTS],
                                       start=(k == 0), stop=(k == NCH - 1))
                    p.inc(ins, sM, 1)
            ph.eng("tensor", f_t)

            def f_sc(s):
                p.wait(s, sB8, 1)
                for i in range(NCH * NT):
                    c, n = divmod(i, NT)
                    p.wait(s, sM, 2 * i + 1)
                    if i >= 1:
                        p.wait(s, sV, i)
                    s.activation(qt[:], ppq[:], Act.Identity,
                                 bias=self.cols["q_b"][:, c:c + 1])
                    p.inc(s.drain(), sE, 1)
                    p.wait(s, sM, 2 * i + 2)
                    s.activation(kt[:], ppk[:], Act.Identity, scale=0.125,
                                 bias=kb8[:, c:c + 1])
                    p.inc(s.drain(), sE, 1)
            ph.eng("scalar", f_sc)

        with ExitStack() as sc, p.phase("l1sc") as ph:
            vw = [p.sb(f"l1vw{c}", [128, 2 * D], BF16, sc) for c in range(NCH)]
            dlt = p.sb("l1dlt", [128, NH * CRET], F32, sc)
            alph = p.sb("l1al", [128, NH], F32, sc)
            beta = p.sb("l1be", [128, NH * CRET], F32, sc)
            vbr = p.sb("l1vbr", [1, 2 * D], F32, sc)
            sall = p.sb("l1S", [128, (NH // 2) * CRET], F32, sc)
            zall = p.sb("l1z", [128, NH // 2], F32, sc)
            vtm = [p.sb(f"l1vt{j}", [128, 2 * D], BF16, sc) for j in range(2)]
            qrc = [p.sb(f"l1qrc{j}", [128, D], BF16, sc) for j in range(2)]
            krc = [p.sb(f"l1krc{j}", [128, D], BF16, sc) for j in range(2)]
            smk = p.sb("l1smk", [128, CRET], BF16, sc)
            qrsf = p.sb("l1qrsf", [128, CRET], F32, sc)
            ktm = p.sb("l1ktm", [128, 64], BF16, sc)
            outsb = p.sb("l1out", [128, CRET], F32, sc)
            zst = p.sb("l1zst", [1, NH * CRET], F32, sc)
            ppv = [p.ps("l1pv0", [128, TTS], F32, sc)]
            psT = p.ps("l1psT", [128, CRET], F32, sc)
            psK = p.ps("l1psK", [128, 64], BF16, sc)
            psA = p.ps("l1psA", [128, CRET], F32, sc)
            psB = p.ps("l1psB", [128, CRET], F32, sc)
            psD = p.ps("l1psD", [128, 136], F32, sc)
            psZ1 = p.ps("l1psZ1", [1, CRET], F32, sc)
            psZ2 = p.ps("l1psZ2", [1, CRET], F32, sc)
            (sW, sVM, sVE, sST, sMK, sKT, sKE, sQS, sOA, sOC, sDm, sSU, sZm,
             sZC, sSp, sZD, sZG) = ("l1s" + x for x in
                                    ("W", "VM", "VE", "ST", "MK", "KT", "KE",
                                     "QS", "OA", "OC", "Dm", "SU", "Zm", "ZC",
                                     "Sp", "ZD", "ZG"))
            sQL = [f"l1sQL{j}" for j in range(2)]
            sAE, sZE = "l1sAE", "l1sZE"

            def f_g(g):
                for c in range(NCH):
                    p.inc(g.dma_start(out=vw[c][:],
                                      in_=dI["v_w"][c * 128:(c + 1) * 128, :]), sW, 16)
                p.inc(g.dma_start(out=dlt[:], in_=dI["dlt"]), sW, 16)
                p.inc(g.dma_start(out=alph[:], in_=dI["alpha"]), sW, 16)
                p.inc(g.dma_start(out=beta[:], in_=dI["beta_bc"]), sW, 16)
                p.inc(g.dma_start(out=vbr[:], in_=dI["v_b_row"]), sW, 16)
                g.memset(sall[:], 0.0)
                g.memset(zall[:], 0.0)
                p.inc(g.drain(), sW + "z", 1)
            ph.eng("gpsimd", f_g)

            def f_t(t):
                p.wait(t, sW)
                p.wait(t, sW + "z", 1)
                for i in range(NH):
                    ci = slice(i * CRET, (i + 1) * CRET)
                    p.wait(t, sQL[i % 2], 16 * 16 * (i // 2 + 1))
                    for e in range(4):
                        ie = 4 * i + e
                        if ie >= 1:
                            p.wait(t, sVE, ie)
                        for k in range(NCH):
                            ins = t.matmul(ppv[0][:], hb[k][:, ci],
                                           vw[k][:, e * TTS:(e + 1) * TTS],
                                           start=(k == 0), stop=False)
                        ins = t.matmul(ppv[0][:], self.onesr[:],
                                       vbr[0:1, e * TTS:(e + 1) * TTS],
                                       start=False, stop=True)
                        p.inc(ins, sVM, 1)
                    for h in range(NH):
                        j = i * NH + h
                        hrow = slice((h % 2) * 64, (h % 2) * 64 + 64)
                        hcol = slice(h * CRET, (h + 1) * CRET)
                        if j >= 1:
                            p.wait(t, sMK, j)
                        ins = t.matmul(psT[:], krc[i % 2][hrow, (h // 2) * 128:(h // 2) * 128 + 128],
                                       qrc[i % 2][hrow, (h // 2) * 128:(h // 2) * 128 + 128],
                                       start=True, stop=True)
                        p.inc(ins, sST, 1)
                        if j >= 1:
                            p.wait(t, sKE, j)
                        idsl = self.identb[hrow, hrow]
                        ins = t.transpose(psK[:, 0:64],
                                          krc[i % 2][hrow, (h // 2) * 128:(h // 2) * 128 + 128],
                                          idsl)
                        p.inc(ins, sKT, 1)
                        p.wait(t, sMK, j + 1)
                        p.wait(t, sVE, 4 * i + 4)
                        if j >= 1:
                            p.wait(t, sOC, j)
                        ins = t.matmul(psA[:], vtm[i % 2][:, hcol], smk[:],
                                       start=True, stop=True)
                        p.inc(ins, sOA, 1)
                        p.wait(t, sQS, j + 1)
                        if i >= 1:
                            p.wait(t, sSU, (i - 1) * NH + h + 1)
                        scol = slice((h // 2) * CRET, (h // 2 + 1) * CRET)
                        ins = t.matmul(psB[:], sall[hrow, scol], qrsf[hrow, :],
                                       start=True, stop=True)
                        p.inc(ins, sOA, 1)
                        if j >= 1:
                            p.wait(t, sZC, j)
                        ins = t.matmul(psZ1[0:1, :], self.onesb[:], smk[:],
                                       start=True, stop=True)
                        ins = t.matmul(psZ2[0:1, :],
                                       zall[hrow, h // 2:h // 2 + 1],
                                       qrsf[hrow, :], start=True, stop=True)
                        p.inc(ins, sZm, 1)
                        p.wait(t, sKE, j + 1)
                        if j >= 1:
                            p.wait(t, sSU, j)
                        prow = slice((h % 2) * 64, (h % 2) * 64 + 64)
                        ins = t.matmul(psD[prow, 0:128], ktm[:, 0:64],
                                       vtm[i % 2][:, hcol], start=True, stop=True)
                        ins = t.matmul(psD[prow, 128:129], ktm[:, 0:64],
                                       self.onesb[:], start=True, stop=True)
                        p.inc(ins, sDm, 1)
            ph.eng("tensor", f_t)

            def f_sc(s):
                for i in range(NH):
                    for e in range(4):
                        p.wait(s, sVM, 4 * i + e + 1)
                        if i >= 2 and e == 0:
                            p.wait(s, sDm, (i - 1) * NH)
                        p.inc(s.copy(vtm[i % 2][:, e * TTS:(e + 1) * TTS],
                                     ppv[0][:]), sVE, 1)
                    for h in range(NH):
                        j = i * NH + h
                        hcol2 = slice(h * CRET, (h + 1) * CRET)
                        p.wait(s, sKT, j + 1)
                        p.inc(s.activation(ktm[:, 0:64], psK[:, 0:64], Act.Copy,
                                           scale=alph[:, h:h + 1]), sKE, 1)
                        p.wait(s, sOA, 2 * j + 1)
                        if j >= 1:
                            p.wait(s, sSp, 16 * j)
                        p.inc(s.copy(outsb[:], psA[:]), sAE, 1)
                        p.wait(s, sZm, j + 1)
                        if i >= 1:
                            p.wait(s, sZG, 16 * i)
                        p.inc(s.copy(zst[0:1, hcol2], psZ1[0:1, :]), sZE, 1)
            ph.eng("scalar", f_sc)

            def f_v(v):
                for i in range(NH):
                    ci = slice(i * CRET, (i + 1) * CRET)
                    p.wait(v, sQL[i % 2], 16 * 16 * (i // 2 + 1))
                    for h in range(NH):
                        j = i * NH + h
                        hrow = slice((h % 2) * 64, (h % 2) * 64 + 64)
                        hcol = slice(h * CRET, (h + 1) * CRET)
                        p.wait(v, sST, j + 1)
                        ins = v.tensor_tensor(smk[:], psT[:], dlt[:, hcol],
                                              Alu.mult)
                        v.drain()
                        p.inc(ins, sMK, 1)
                        ins = v.tensor_tensor(
                            qrsf[hrow, :],
                            qrc[i % 2][hrow, (h // 2) * 128:(h // 2) * 128 + 128],
                            beta[hrow, hcol], Alu.mult)
                        v.drain()
                        p.inc(ins, sQS, 1)
                        p.wait(v, sOA, 2 * j + 2)
                        p.wait(v, sAE, j + 1)
                        ins = v.tensor_tensor(outsb[:], outsb[:], psB[:], Alu.add)
                        v.drain()
                        p.inc(ins, sOC, 1)
                        p.wait(v, sZm, j + 1)
                        p.wait(v, sZE, j + 1)
                        ins = v.tensor_tensor(zst[0:1, hcol],
                                              zst[0:1, hcol], psZ2[0:1, :], Alu.add)
                        v.drain()
                        p.inc(ins, sZC, 1)
                        p.wait(v, sDm, j + 1)
                        lam = float(CONSTS["lam_pc"][h])
                        scol = slice((h // 2) * CRET, (h // 2 + 1) * CRET)
                        zcl = slice(h // 2, h // 2 + 1)
                        prow = slice((h % 2) * 64, (h % 2) * 64 + 64)
                        v.tensor_scalar_mul(sall[hrow, scol], sall[hrow, scol], lam)
                        v.tensor_scalar_mul(zall[hrow, zcl], zall[hrow, zcl], lam)
                        v.drain()
                        v.tensor_tensor(sall[hrow, scol], sall[hrow, scol],
                                        psD[prow, 0:128], Alu.add)
                        ins = v.tensor_tensor(zall[hrow, zcl], zall[hrow, zcl],
                                              psD[prow, 128:129], Alu.add)
                        v.drain()
                        p.inc(ins, sSU, 1)
                    p.inc(v.drain(), sZD, 1)
            ph.eng("vector", f_v)

            def f_sync(s):
                def loads(i):
                    ci2 = slice(i * CRET, (i + 1) * CRET)
                    for k in range(NCH):
                        p.inc(s.dma_start(out=qrc[i % 2][:, k * 128:(k + 1) * 128],
                                          in_=self.scr["kq"][k * 128:(k + 1) * 128, ci2]),
                              sQL[i % 2], 16)
                        p.inc(s.dma_start(out=krc[i % 2][:, k * 128:(k + 1) * 128],
                                          in_=self.scr["kq"][D + k * 128:D + (k + 1) * 128, ci2]),
                              sQL[i % 2], 16)
                loads(0)
                loads(1)
                for i in range(NH):
                    for h in range(NH):
                        j = i * NH + h
                        p.wait(s, sOC, j + 1)
                        p.inc(s.dma_start(
                            out=self.scr["O"][h * CRET:(h + 1) * CRET,
                                              i * CRET:(i + 1) * CRET],
                            in_=outsb[:]), sSp, 16)
                    p.wait(s, sZD, i + 1)
                    p.inc(s.dma_start(out=self.scr["row"][40 + i:41 + i, :],
                                      in_=zst[:]), sZG, 16)
                    if i + 2 < NH:
                        p.wait(s, sDm, (i + 1) * NH)
                        loads(i + 2)
                p.wait(s, sSp, 16 * NH * NH)
                p.wait(s, sZG, 16 * NH)
            ph.eng("sync", f_sync)

        sc_st = sc0.enter_context(ExitStack())
        srows = p.sb("l1srows", [NH, T], F32, sc_st)
        qrows = p.sb("l1qrows", [NH, T], F32, sc_st)
        zrows = p.sb("l1zrows", [NH, T], F32, sc_st)

        with p.phase("l1zg") as ph:
            def f_sync(s):
                src = bass.AP(self.scr["row"].tensor, 40 * T,
                              [[128, NH], [T, NH], [1, CRET]])
                p.inc(s.dma_start(out=zrows.ap().rearrange(
                    "hh (ii r) -> hh ii r", r=CRET), in_=src), "l1zgD", 16)
                p.wait(s, "l1zgD", 16)
            ph.eng("sync", f_sync)

        with ExitStack() as sc, p.phase("l1st") as ph:
            ot = [p.sb(f"l1ot{j}", [128, TTS], F32, sc) for j in range(2)]
            sqt = p.sb("l1sqt", [128, TTS], F32, sc)
            ehot = p.sb("l1eh", [128, NH * NH], F32, sc)
            psS = p.ps("l1psS", [NH, TTS], F32, sc)
            psQ = p.ps("l1psQ", [NH, TTS], F32, sc)
            sW, sQ2, sMS, sEV = "l1tW", "l1tQ", "l1tM", "l1tE"
            sLs = [f"l1tL{j}" for j in range(2)]

            def f_g(g):
                p.inc(g.dma_start(out=ehot[:], in_=dI["ehot"]), sW, 16)
            ph.eng("gpsimd", f_g)

            def f_sync(s):
                for n in range(NT):
                    for h in range(NH):
                        i = n * NH + h
                        if i >= 2:
                            p.wait(s, sQ2, i - 1)
                            p.wait(s, sMS, i - 1)
                        p.inc(s.dma_start(
                            out=ot[i % 2][:],
                            in_=self.scr["O"][h * CRET:(h + 1) * CRET,
                                              n * TTS:(n + 1) * TTS]),
                            sLs[i % 2], 16)
            ph.eng("sync", f_sync)

            def f_sc(s):
                for n in range(NT):
                    for h in range(NH):
                        i = n * NH + h
                        p.wait(s, sLs[i % 2], 16 * (i // 2 + 1))
                        if i >= 1:
                            p.wait(s, sMS, i)
                        p.inc(s.activation(sqt[:], ot[i % 2][:], Act.Square),
                              sQ2, 1)
            ph.eng("scalar", f_sc)

            def f_t(t):
                p.wait(t, sW)
                for n in range(NT):
                    if n >= 1:
                        p.wait(t, sEV, n)
                    for h in range(NH):
                        i = n * NH + h
                        p.wait(t, sQ2, i + 1)
                        ins = t.matmul(psS[:], ehot[:, h * NH:(h + 1) * NH],
                                       ot[i % 2][:], start=(h == 0),
                                       stop=(h == NH - 1))
                        ins = t.matmul(psQ[:], ehot[:, h * NH:(h + 1) * NH],
                                       sqt[:], start=(h == 0), stop=(h == NH - 1))
                        p.inc(ins, sMS, 1)
            ph.eng("tensor", f_t)

            def f_v(v):
                for n in range(NT):
                    p.wait(v, sMS, (n + 1) * NH)
                    tl = slice(n * TTS, (n + 1) * TTS)
                    v.tensor_copy(srows[:, tl], psS[:])
                    ins = v.tensor_copy(qrows[:, tl], psQ[:])
                    v.drain()
                    p.inc(ins, sEV, 1)
            ph.eng("vector", f_v)

        with ExitStack() as sc, p.phase("l1rw") as ph:
            rnorm = p.sb("l1rn", [NH, T], F32, sc)
            dd = p.sb("l1dd", [NH, T], F32, sc)
            sig = p.sb("l1sig", [NH, T], F32, sc)
            mu = p.sb("l1mu", [NH, T], F32, sc)
            var = p.sb("l1var", [NH, T], F32, sc)
            aba = p.sb("l1abra", [NH, T], F32, sc)
            abb = p.sb("l1abrb", [NH, T], F32, sc)
            sW, sV1, sS1, sD1 = "l1rW", "l1rV", "l1rS", "l1rD"

            def f_g(g):
                p.inc(g.dma_start(out=rnorm[:], in_=dI["rnorm"]), sW, 16)
            ph.eng("gpsimd", f_g)

            def f_v(v):
                p.wait(v, sW)
                v.tensor_tensor(dd[:], zrows[0:NH, :], rnorm[:], Alu.mult)
                v.drain()
                v.tensor_scalar_mul(sig[:], dd[:], -1.0)
                v.drain()
                v.tensor_tensor(dd[:], dd[:], sig[:], Alu.max)
                v.drain()
                v.tensor_scalar_max(dd[:], dd[:], 1.0)
                v.drain()
                v.reciprocal(dd[:], dd[:])
                v.drain()
                v.tensor_tensor(sig[:], rnorm[:], dd[:], Alu.mult)
                v.drain()
                v.tensor_scalar_mul(mu[:], srows[:], 1.0 / DVH)
                v.tensor_scalar_mul(var[:], qrows[:], 1.0 / DVH)
                v.drain()
                v.tensor_tensor(dd[:], mu[:], mu[:], Alu.mult)
                v.drain()
                v.tensor_tensor(var[:], var[:], dd[:], Alu.subtract)
                v.drain()
                v.tensor_tensor(dd[:], sig[:], sig[:], Alu.mult)
                v.drain()
                v.tensor_tensor(var[:], var[:], dd[:], Alu.mult)
                v.drain()
                v.tensor_scalar_add(var[:], var[:], EPS)
                v.drain()
                v.reciprocal(var[:], var[:])
                p.inc(v.drain(), sV1, 1)
                p.wait(v, sS1, 1)
                v.tensor_tensor(aba[:], sig[:], var[:], Alu.mult)
                v.drain()
                v.tensor_tensor(abb[:], mu[:], aba[:], Alu.mult)
                v.drain()
                v.tensor_scalar_mul(abb[:], abb[:], -1.0)
                v.drain()
                p.inc(v.drain(), sD1, 1)
            ph.eng("vector", f_v)

            def f_sc(s):
                p.wait(s, sV1, 1)
                p.inc(s.activation(var[:], var[:], Act.Sqrt), sS1, 1)
            ph.eng("scalar", f_sc)

            def f_sync(s):
                p.wait(s, sD1, 1)
                p.inc(s.dma_start(out=self.scr["row"][0:NH, :], in_=aba[:]),
                      "l1rSp", 16)
                p.inc(s.dma_start(out=self.scr["row"][NH:2 * NH, :], in_=abb[:]),
                      "l1rSp2", 16)
                p.wait(s, "l1rSp", 16)
                p.wait(s, "l1rSp2", 16)
            ph.eng("sync", f_sync)

        sc_st.close()

        with ExitStack() as sc, p.phase("l1f") as ph:
            gw = [p.sb(f"l1gw{c}", [128, 2 * D], BF16, sc) for c in range(NCH)]
            ow = [p.sb(f"l1ow{c}", [128, D], BF16, sc) for c in range(2 * NCH)]
            gb16 = p.sb("l1gb16", [128, 16], F32, sc)
            gaob = p.sb("l1gaob", [128, 8], F32, sc)
            gsil = p.sb("l1gs", [128, NH * TTS], BF16, sc)
            ot = [p.sb(f"l1fo{j}", [128, TTS], F32, sc) for j in range(2)]
            abca = [p.sb(f"l1aba{j}", [128, TTS], F32, sc) for j in range(2)]
            abcb = [p.sb(f"l1abb{j}", [128, TTS], F32, sc) for j in range(2)]
            fus = p.sb("l1fu", [128, TTS], BF16, sc)
            tre = [p.sb(f"l1tr{j}", [128, TTS], F32, sc) for j in range(2)]
            psO = [p.ps(f"l1pO{c}", [128, TTS], F32, sc) for c in range(8)]
            (sW, sGM, sGE, sFu, sOM, sOE, sXm, sV0) = (
                "l1f" + x for x in ("W", "GM", "GE", "Fu", "OM", "OE", "Xm", "V"))
            sLo = [f"l1fLo{j}" for j in range(2)]
            sLa = [f"l1fLa{j}" for j in range(2)]
            sLb = [f"l1fLb{j}" for j in range(2)]

            def f_g(g):
                for c in range(NCH):
                    p.inc(g.dma_start(out=gw[c][:],
                                      in_=dI["g_w"][c * 128:(c + 1) * 128, :]), sW, 16)
                for c in range(2 * NCH):
                    p.inc(g.dma_start(out=ow[c][:],
                                      in_=dI["o_w"][c * 128:(c + 1) * 128, :]), sW, 16)
                p.inc(g.dma_start(out=gb16[:], in_=dI["g_b"]), sW, 16)
            ph.eng("gpsimd", f_g)

            def f_sync(s):
                for n in range(NT):
                    for h in range(NH):
                        i = n * NH + h
                        if i >= 2:
                            p.wait(s, sFu, i - 1)
                        p.inc(s.dma_start(
                            out=ot[i % 2][:],
                            in_=self.scr["O"][h * CRET:(h + 1) * CRET,
                                              n * TTS:(n + 1) * TTS]),
                            sLo[i % 2], 16)
                        p.inc(s.dma_start(
                            out=abca[i % 2][:],
                            in_=self.scr["row"][h:h + 1, n * TTS:(n + 1) * TTS]
                            .partition_broadcast(128)), sLa[i % 2], 16)
                        p.inc(s.dma_start(
                            out=abcb[i % 2][:],
                            in_=self.scr["row"][NH + h:NH + h + 1,
                                                n * TTS:(n + 1) * TTS]
                            .partition_broadcast(128)), sLb[i % 2], 16)
            ph.eng("sync", f_sync)

            def f_v(v):
                p.wait(v, "ldc")
                v.tensor_tensor(gaob[:], self.modfm[1][:, 16:24],
                                self.cols["o_b"][:], Alu.mult)
                p.inc(v.drain(), sV0, 1)
                for n in range(NT):
                    for h in range(NH):
                        i = n * NH + h
                        p.wait(v, sLo[i % 2], 16 * (i // 2 + 1))
                        p.wait(v, sLa[i % 2], 16 * (i // 2 + 1))
                        p.wait(v, sLb[i % 2], 16 * (i // 2 + 1))
                        p.wait(v, sGE, i + 1)
                        v.tensor_tensor(ot[i % 2][:], ot[i % 2][:],
                                        abca[i % 2][:], Alu.mult)
                        v.drain()
                        v.tensor_tensor(ot[i % 2][:], ot[i % 2][:],
                                        abcb[i % 2][:], Alu.add)
                        v.drain()
                        if i >= 1:
                            p.wait(v, sOM, i)
                        ins = v.tensor_tensor(fus[:], ot[i % 2][:],
                                              gsil[:, h * TTS:(h + 1) * TTS],
                                              Alu.mult)
                        v.drain()
                        p.inc(ins, sFu, 1)
                    for c in range(NCH):
                        p.wait(v, sOE, n * NCH + c + 1)
                        ins = v.tensor_tensor(
                            self.xres[c][:, n * TTS:(n + 1) * TTS],
                            self.xres[c][:, n * TTS:(n + 1) * TTS],
                            tre[c % 2][:], Alu.add)
                        v.drain()
                        p.inc(ins, sXm, 1)
            ph.eng("vector", f_v)

            def f_t(t):
                p.wait(t, sW)
                for n in range(NT):
                    tl = slice(n * TTS, (n + 1) * TTS)
                    for h in range(NH):
                        i = n * NH + h
                        if i >= 2:
                            p.wait(t, sGE, i - 1)
                        if n >= 1 and h < 2:
                            p.wait(t, sOE, (n - 1) * NCH + h + 1)
                        for k in range(NCH):
                            ins = t.matmul(psO[h % 2][:],
                                           gw[k][:, h * 128:(h + 1) * 128],
                                           hb[k][:, tl],
                                           start=(k == 0), stop=(k == NCH - 1))
                        p.inc(ins, sGM, 1)
                    p.wait(t, sGE, (n + 1) * NH)
                    for h in range(NH):
                        i = n * NH + h
                        p.wait(t, sFu, i + 1)
                        for c in range(NCH):
                            if n >= 1 and h == 0:
                                p.wait(t, sOE, (n - 1) * NCH + c + 1)
                            ins = t.matmul(psO[c][:],
                                           ow[h][:, c * 128:(c + 1) * 128],
                                           fus[:], start=(h == 0),
                                           stop=(h == NH - 1))
                        p.inc(ins, sOM, 1)
            ph.eng("tensor", f_t)

            def f_sc(s):
                p.wait(s, sV0, 1)
                for n in range(NT):
                    for h in range(NH):
                        i = n * NH + h
                        p.wait(s, sGM, i + 1)
                        if n >= 1:
                            p.wait(s, sFu, (n - 1) * NH + h + 1)
                        p.inc(s.activation(gsil[:, h * TTS:(h + 1) * TTS],
                                           psO[h % 2][:], Act.Silu,
                                           bias=gb16[:, h:h + 1]), sGE, 1)
                    for c in range(NCH):
                        p.wait(s, sOM, (n + 1) * NH)
                        if n * NCH + c >= 2:
                            p.wait(s, sXm, n * NCH + c - 1)
                        p.inc(s.activation(tre[c % 2][:], psO[c][:], Act.Identity,
                                           scale=self.modfm[1][:, 16 + c:17 + c],
                                           bias=gaob[:, c:c + 1]), sOE, 1)
            ph.eng("scalar", f_sc)


    def l1_moe(self, hb, scm):
        p, dI = self.p, self.dI
        MATS = ("1g", "1l", "3g", "3l")
        NEH = 18
        TH = T // 2
        routed_upto = [sum(1 for x in range(eh + 1) if x % 9 != 8)
                       for eh in range(NEH)]
        with ExitStack() as sc:
            he = [p.sb(f"mxhe{k}", [128, TH], BF16, sc) for k in range(16)]
            wsl = {m: [p.sb(f"mxw{m}{j}", [128, 1024], BF16, sc)
                       for j in range(2)] for m in MATS}
            bcol = {m: [p.sb(f"mxb{m}{j}", [128, 16], F32, sc)
                        for j in range(2)] for m in MATS}
            b2c = [p.sb(f"mxb2{j}", [128, NCH], F32, sc) for j in range(2)]
            w2b = [p.sb(f"mxu{j}", [128, D], BF16, sc) for j in range(4)]
            tb = {nm: [p.sb(f"mxt{nm}{j}", [128, TTS], F32, sc)
                       for j in range(2)] for nm in ("1", "2", "3", "4")}
            cbb = [p.sb(f"mxc{j}", [128, TTS], F32, sc) for j in range(2)]
            dot = [p.sb(f"mxd{j}", [128, TTS], F32, sc) for j in range(4)]
            gwt = p.sb("mxgw", [128, NCH * NE], F32, sc)
            gbb = p.sb("mxgb", [128, NE], F32, sc)
            hfc = [p.sb(f"mxh{j}", [128, 1024], F32, sc) for j in range(2)]
            sco = p.sb("mxsc", [128, NE], F32, sc)
            bia = p.sb("mxbi", [128, NE], F32, sc)
            m1 = p.sb("mxm1", [128, 1], F32, sc)
            cmb = p.sb("mxcm", [128, NE], F32, sc)
            cfm = p.sb("mxcf", [NE, T], F32, sc)
            psA = [p.ps(f"mxp{j}", [128, TTS], F32, sc) for j in range(8)]
            acc_d = self.scr["x"]

            (gW, gK, gVe, gVx, gV, gTr, gTe, gSp) = (
                "mx" + x for x in ("gW", "gK", "gVe", "gVx", "gV", "gTr",
                                   "gTe", "gSp"))
            gHf = ["mxgH0", "mxgH1"]
            (sWb, sMM, sEv, sG, sS, sSw, sDM, sDE, sDC) = (
                "mx" + x for x in ("Wb", "MM", "Ev", "G", "S", "Hw", "DM",
                                   "DE", "DC"))
            sAc = ["mxAc0", "mxAc1", "mxAc2", "mxAc3"]
            sWl = ["mxWl0", "mxWl1"]
            sW2 = ["mxW20", "mxW21", "mxW22", "mxW23"]
            sCb = ["mxCb0", "mxCb1"]

            def HLs(hh, n2):
                return slice((hh * 2 + n2) * TTS, (hh * 2 + n2 + 1) * TTS)

            def LLs(n2):
                return slice(n2 * TTS, (n2 + 1) * TTS)

            def wsrc(shared, e, m, i):
                nm = ("sw" if shared else "w") + m[0] + "_" + m[1]
                r0 = ((0 if shared else e * 16) + i) * 128
                return dI[nm][r0:r0 + 128, :]

            with p.phase("mx") as ph:
                def wsl_load(g, eh, i):
                    e_ = eh % 9
                    sh = e_ == NE
                    gi = eh * 16 + i
                    if gi >= 2:
                        p.wait(g, sMM, 8 * (gi - 1))
                    for m in MATS:
                        p.inc(g.dma_start(out=wsl[m][gi % 2][:],
                                          in_=wsrc(sh, e_, m, i)),
                              sWl[gi % 2], 16)

                def f_g(g):
                    p.inc(g.dma_start(out=gwt[:], in_=dI["gate_wp"]), gW, 16)
                    p.inc(g.dma_start(out=gbb[:], in_=dI["gate_bbc"]), gW, 16)

                    def biases(eh):
                        e_ = eh % 9
                        sh = e_ == NE
                        if eh >= 2:
                            p.wait(g, sEv, 128 * (eh - 1))
                            p.wait(g, sDE, 16 * (eh - 1))
                        for m in MATS:
                            bn = (("sb" if sh else "b") + m[0] + "_"
                                  + m[1] + "_c")
                            src = (dI[bn] if sh
                                   else dI[bn][e_ * 128:(e_ + 1) * 128, :])
                            p.inc(g.dma_start(out=bcol[m][eh % 2][:], in_=src),
                                  sWb, 16)
                        if not sh:
                            p.inc(g.dma_start(
                                out=b2c[eh % 2][:],
                                in_=dI["b2_c"][e_ * 128:(e_ + 1) * 128, :]),
                                sWb, 16)

                    def w2load(eh, ks):
                        e_ = eh % 9
                        sh = e_ == NE
                        for kk in ks:
                            gik = eh * 32 + kk
                            if gik >= 4:
                                p.wait(g, sDM, gik - 3)
                            r0 = (0 if sh else e_ * MI) + (kk % 16) * 128
                            nm = "sw2" if sh else "w2"
                            p.inc(g.dma_start(
                                out=w2b[gik % 4][:],
                                in_=dI[nm][r0:r0 + 128, :]),
                                sW2[gik % 4], 16)

                    biases(0)
                    wsl_load(g, 0, 0)
                    wsl_load(g, 0, 1)
                    for eh in range(NEH):
                        w2load(eh, range(0, 4))
                        for i in range(2, 16):
                            wsl_load(g, eh, i)
                        w2load(eh, range(4, 16))
                        self._moe_accum(g, eh, 0, acc_d, dot, sDC, sAc, HLs)
                        if eh + 1 < NEH:
                            biases(eh + 1)
                            wsl_load(g, eh + 1, 0)
                            wsl_load(g, eh + 1, 1)
                        w2load(eh, range(16, 32))
                        self._moe_accum(g, eh, 1, acc_d, dot, sDC, sAc, HLs)
                ph.eng("gpsimd", f_g)

                def f_sy(s):
                    for q in range(16):
                        if q >= 2:
                            p.wait(s, gK, q - 1)
                        k, hf = q // 2, q % 2
                        p.inc(s.dma_start(
                            out=hfc[q % 2][:],
                            in_=self.scr["P"][k * 128:(k + 1) * 128,
                                              hf * 1024:(hf + 1) * 1024]),
                            gHf[q % 2], 16)
                    p.wait(s, gTe, 16)
                    p.inc(s.dma_start(out=self.scr["comb"], in_=cfm[:]),
                          gSp, 16)
                    for eh in range(NEH):
                        hh, e = divmod(eh, 9)
                        if e != NE:
                            for n2 in range(2):
                                if eh == 0:
                                    p.wait(s, gSp, 16)
                                else:
                                    p.wait(s, sDC,
                                           (eh - 1) * 16 + n2 * 8 + 8)
                                p.inc(s.dma_start(
                                    out=cbb[n2][:],
                                    in_=self.scr["comb"][e:e + 1, HLs(hh, n2)]
                                    .partition_broadcast(128)), sCb[n2], 16)
                ph.eng("sync", f_sy)

                def f_t(t):
                    p.wait(t, gW, 32)
                    for q in range(16):
                        k, hf = q // 2, q % 2
                        p.wait(t, gHf[q % 2], 16 * (q // 2 + 1))
                        for it8 in range(8):
                            ig = hf * 8 + it8
                            ins = t.matmul(psA[0][:, ig * 8:(ig + 1) * 8],
                                           hfc[q % 2][:, it8 * 128:(it8 + 1) * 128],
                                           gwt[:, k * 8:(k + 1) * 8],
                                           start=(q == 0 and it8 == 0),
                                           stop=(q == 15 and it8 == 7))
                        p.inc(ins, gK, 1)
                    for ig in range(16):
                        p.wait(t, gV, ig + 1)
                        if ig >= 1:
                            p.wait(t, gTe, ig)
                        p.inc(t.transpose(psA[1][0:NE, 0:128], cmb[:],
                                          self.identf[:]), gTr, 1)
                    for eh in range(NEH):
                        hh, e = divmod(eh, 9)
                        for i in range(16):
                            gi = eh * 16 + i
                            p.wait(t, sWl[gi % 2], 64 * (gi // 2 + 1))
                            for n2 in range(2):
                                it = i * 2 + n2
                                if it <= 1:
                                    if eh == 0:
                                        if it == 0:
                                            p.wait(t, gTe, 16)
                                    else:
                                        p.wait(t, sDE, 16 * eh)
                                else:
                                    p.wait(t, sEv, eh * 128 + 4 * (it - 1))
                                for mi, m in enumerate(MATS):
                                    bank = (it % 2) * 4 + mi
                                    for k in range(NCH):
                                        ins = t.matmul(
                                            psA[bank][:],
                                            wsl[m][gi % 2][:, k * 128:(k + 1) * 128],
                                            hb[k][:, HLs(hh, n2)],
                                            start=(k == 0), stop=(k == 7))
                                    p.inc(ins, sMM, 1)
                        for n2 in range(2):
                            if n2 == 0:
                                p.wait(t, sEv, 128 * (eh + 1))
                            else:
                                p.wait(t, sDE, eh * 16 + 8)
                            for k in range(16):
                                gik = eh * 32 + n2 * 16 + k
                                p.wait(t, sW2[gik % 4], 16 * (gik // 4 + 1))
                                p.wait(t, sSw, eh * 32 + k * 2 + n2 + 1)
                                for c in range(NCH):
                                    ins = t.matmul(
                                        psA[c][:],
                                        w2b[gik % 4][:, c * 128:(c + 1) * 128],
                                        he[k][:, LLs(n2)],
                                        start=(k == 0), stop=(k == 15))
                                p.inc(ins, sDM, 1)
                ph.eng("tensor", f_t)

                def f_sc(s):
                    for ig in range(16):
                        p.wait(s, gVe, ig + 1)
                        p.inc(s.activation(sco[:], sco[:], Act.Exp), gVx, 1)
                    for eh in range(NEH):
                        hh, e = divmod(eh, 9)
                        shared = e == NE
                        for it in range(32):
                            git = eh * 32 + it
                            i = it // 2
                            j = git % 2
                            if it == 0:
                                p.wait(s, sWb,
                                       16 * sum(4 if x % 9 == NE else 5
                                                for x in range(eh + 1)))
                            p.wait(s, sMM, eh * 128 + 4 * (it + 1))
                            if git >= 2:
                                p.wait(s, sSw, git - 1)
                            bank = (it % 2) * 4
                            p.inc(s.activation(tb["1"][j][:], psA[bank][:],
                                               Act.Identity,
                                               bias=bcol["1g"][eh % 2][:, i:i + 1]),
                                  sEv, 1)
                            p.inc(s.activation(tb["2"][j][:], psA[bank + 2][:],
                                               Act.Identity,
                                               bias=bcol["3g"][eh % 2][:, i:i + 1]),
                                  sEv, 1)
                            p.inc(s.activation(tb["3"][j][:], psA[bank + 1][:],
                                               Act.Identity,
                                               bias=bcol["1l"][eh % 2][:, i:i + 1]),
                                  sEv, 1)
                            p.inc(s.activation(tb["4"][j][:], psA[bank + 3][:],
                                               Act.Identity,
                                               bias=bcol["3l"][eh % 2][:, i:i + 1]),
                                  sEv, 1)
                            p.wait(s, sG, git + 1)
                            p.inc(s.activation(tb["1"][j][:], tb["1"][j][:],
                                               Act.Silu, scale=1.702), sS, 1)
                        for n2 in range(2):
                            for c in range(NCH):
                                gdj = eh * 16 + n2 * 8 + c
                                p.wait(s, sDM, eh * 32 + 16 * (n2 + 1))
                                if gdj >= 4:
                                    p.wait(s, sAc[gdj % 4],
                                           16 * (gdj // 4))
                                bc = (self.cols["sb2"] if shared
                                      else b2c[eh % 2])
                                p.inc(s.activation(dot[gdj % 4][:], psA[c][:],
                                                   Act.Identity,
                                                   bias=bc[:, c:c + 1]),
                                      sDE, 1)
                ph.eng("scalar", f_sc)

                def f_v(v):
                    p.wait(v, gK, 16)
                    for ig in range(16):
                        gsl = psA[0][:, ig * 8:(ig + 1) * 8]
                        v.tensor_reduce(m1[:], gsl, AX.X, Alu.max)
                        v.drain()
                        v.tensor_scalar(sco[:], gsl, m1[:, 0:1], None,
                                        Alu.subtract)
                        v.drain()
                        p.inc(v.drain(), gVe, 1)
                        p.wait(v, gVx, ig + 1)
                        v.tensor_reduce(m1[:], sco[:], AX.X, Alu.add)
                        v.drain()
                        v.reciprocal(m1[:], m1[:])
                        v.drain()
                        v.tensor_scalar(sco[:], sco[:], m1[:, 0:1], None,
                                        Alu.mult)
                        v.drain()
                        v.tensor_tensor(bia[:], sco[:], gbb[:], Alu.add)
                        v.drain()
                        v.tensor_reduce(m1[:], bia[:], AX.X, Alu.max)
                        v.drain()
                        v.tensor_scalar(cmb[:], bia[:], m1[:, 0:1], None,
                                        Alu.is_ge)
                        v.drain()
                        v.scalar_tensor_tensor(bia[:], cmb[:], -1e30, bia[:],
                                               Alu.mult, Alu.add)
                        v.drain()
                        v.tensor_reduce(m1[:], bia[:], AX.X, Alu.max)
                        v.drain()
                        v.tensor_scalar(bia[:], bia[:], m1[:, 0:1], None,
                                        Alu.is_ge)
                        v.drain()
                        v.tensor_tensor(cmb[:], cmb[:], bia[:], Alu.add)
                        v.drain()
                        ins = v.tensor_tensor(cmb[:], cmb[:], sco[:], Alu.mult)
                        v.drain()
                        p.inc(ins, gV, 1)
                        p.wait(v, gTr, ig + 1)
                        ins = v.tensor_copy(cfm[0:NE, ig * 128:(ig + 1) * 128],
                                            psA[1][0:NE, 0:128])
                        v.drain()
                        p.inc(ins, gTe, 1)
                    for eh in range(NEH):
                        hh, e = divmod(eh, 9)
                        shared = e == NE
                        for it in range(32):
                            git = eh * 32 + it
                            i, n2 = it // 2, it % 2
                            j = git % 2
                            p.wait(v, sEv, eh * 128 + 4 * (it + 1))
                            v.tensor_tensor(tb["1"][j][:], tb["1"][j][:],
                                            tb["2"][j][:], Alu.mult)
                            v.drain()
                            ins = v.tensor_scalar(tb["1"][j][:], tb["1"][j][:],
                                                  7.0, None, Alu.min)
                            v.drain()
                            p.inc(ins, sG, 1)
                            v.tensor_tensor(tb["3"][j][:], tb["3"][j][:],
                                            tb["4"][j][:], Alu.mult)
                            v.drain()
                            v.tensor_scalar(tb["3"][j][:], tb["3"][j][:],
                                            7.0, -7.0, Alu.min, Alu.max)
                            v.drain()
                            v.tensor_scalar(tb["3"][j][:], tb["3"][j][:],
                                            1.0 / 1.702, 1.0 / 1.702,
                                            Alu.mult, Alu.add)
                            v.drain()
                            p.wait(v, sS, git + 1)
                            if eh >= 1:
                                p.wait(v, sDM, (eh - 1) * 32 + 16 + i + 1)
                            ins = v.tensor_tensor(he[i][:, LLs(n2)],
                                                  tb["1"][j][:], tb["3"][j][:],
                                                  Alu.mult)
                            v.drain()
                            p.inc(ins, sSw, 1)
                        for n2 in range(2):
                            for c in range(NCH):
                                gdj = eh * 16 + n2 * 8 + c
                                p.wait(v, sDE, gdj + 1)
                                if not shared:
                                    p.wait(v, sCb[n2], 16 * routed_upto[eh])
                                    v.tensor_tensor(dot[gdj % 4][:],
                                                    dot[gdj % 4][:],
                                                    cbb[n2][:], Alu.mult)
                                    v.drain()
                                p.inc(v.drain(), sDC, 1)
                ph.eng("vector", f_v)

        with ExitStack() as sc2, p.phase("mxmg") as ph:
            at = [p.sb(f"mxmg{j}", [128, TH], F32, sc2) for j in range(2)]
            ut = p.sb("mxmgu", [128, TH], F32, sc2)
            sL = ["mxmgL0", "mxmgL1"]
            sV = "mxmgV"

            def f_sy(s):
                for j in range(16):
                    c, hhh = j // 2, j % 2
                    if j >= 2:
                        p.wait(s, sV, j - 1)
                    p.inc(s.dma_start(
                        out=at[j % 2][:],
                        in_=self.scr["x"][c * 128:(c + 1) * 128,
                                          hhh * TH:(hhh + 1) * TH]),
                        sL[j % 2], 16)
            ph.eng("sync", f_sy)

            def f_v(v):
                for j in range(16):
                    c, hhh = j // 2, j % 2
                    hl = slice(hhh * TH, (hhh + 1) * TH)
                    p.wait(v, sL[j % 2], 16 * (j // 2 + 1))
                    v.tensor_scalar(ut[:], at[j % 2][:],
                                    self.modfm[1][:, 40 + c:41 + c], None,
                                    Alu.mult)
                    v.drain()
                    ins = v.tensor_tensor(self.xres[c][:, hl],
                                          self.xres[c][:, hl],
                                          ut[:], Alu.add)
                    v.drain()
                    p.inc(ins, sV, 1)
            ph.eng("vector", f_v)
        self.tap("final", [xc.ap() for xc in self.xres])

    def _moe_accum(self, g, eh, n2, acc_d, dot, sDC, sAc, HLs):
        p = self.p
        hh, e = divmod(eh, 9)
        for c in range(NCH):
            gdj = eh * 16 + n2 * 8 + c
            p.wait(g, sDC, gdj + 1)
            if gdj >= 4:
                p.wait(g, sAc[gdj % 4], 16 * (gdj // 4))
            p.inc(g.dma_start(
                out=acc_d[c * 128:(c + 1) * 128, HLs(hh, n2)],
                in_=dot[gdj % 4][:],
                accum_op=(Alu.bypass if e == 0 else Alu.add)),
                sAc[gdj % 4], 16)


def pack_inputs(inputs):
    global CONSTS
    if CONSTS is None:
        CONSTS = host_constants()
    C = CONSTS
    g = lambda k: np.asarray(inputs[k])
    com = {}
    tr = lambda a: np.ascontiguousarray(np.asarray(a).T.astype(np.float32))
    com["wk_t"] = tr(g("L0_attn_Wk")); com["wv_t"] = tr(g("L0_attn_Wv"))
    com["wq_t"] = tr(g("L0_attn_Wq")); com["wkp_t"] = tr(g("L0_attn_Wkp"))
    com["wqp_t"] = tr(g("L0_attn_Wqp")); com["wo_t"] = tr(g("L0_attn_Wo"))
    com["bk"] = g("L0_attn_bk"); com["bv"] = g("L0_attn_bv"); com["bq"] = g("L0_attn_bq")
    com["bkp"] = g("L0_attn_bkp"); com["bqp"] = g("L0_attn_bqp"); com["bo"] = g("L0_attn_bo")
    com["ps_col"] = g("L0_attn_phase_scale")
    com["ln_g"] = g("L0_attn_ln_g"); com["ln_b"] = g("L0_attn_ln_b")
    com["w_up"] = g("L0_ffn_w_up"); com["w_gate"] = g("L0_ffn_w_gate")
    com["w_down"] = g("L0_ffn_w_down")
    com["wmod0"] = g("L0_wmod"); com["wmod1"] = g("L1_wmod")
    com["q_w"] = g("L1_attn_q_w"); com["k_w"] = g("L1_attn_k_w")
    com["v_w"] = g("L1_attn_v_w"); com["g_w"] = g("L1_attn_g_w")
    com["o_w"] = g("L1_attn_o_w")
    com["q_b"] = g("L1_attn_q_b"); com["k_b"] = g("L1_attn_k_b")
    com["v_b"] = g("L1_attn_v_b"); com["g_b"] = g("L1_attn_g_b")
    com["o_b"] = g("L1_attn_o_b")
    gwT = np.asarray(g("L1_ffn_gate_w")).T
    com["gate_wp"] = gwT.reshape(NCH, 128, NE).transpose(1, 0, 2).reshape(128, NCH * NE)
    com["gate_bbc"] = np.broadcast_to(np.asarray(g("L1_ffn_gate_b"))[None, :],
                                      (128, NE))
    w1 = g("L1_ffn_w1"); w3 = g("L1_ffn_w3")
    wpk = lambda a: np.ascontiguousarray(
        np.asarray(a).reshape(NE, NCH, 128, 16, 128)
        .transpose(0, 3, 2, 1, 4).reshape(NE * MI, D))
    com["w1_g"] = wpk(w1[:, :, 0::2]); com["w1_l"] = wpk(w1[:, :, 1::2])
    com["w3_g"] = wpk(w3[:, :, 0::2]); com["w3_l"] = wpk(w3[:, :, 1::2])
    com["w2"] = np.asarray(g("L1_ffn_w2")).reshape(NE * MI, D)
    b1 = g("L1_ffn_b1"); b3 = g("L1_ffn_b3")
    bc = lambda a: np.asarray(a).reshape(NE, 16, 128).transpose(0, 2, 1).reshape(NE * 128, 16)
    com["b1_g_c"] = bc(b1[:, 0::2]); com["b1_l_c"] = bc(b1[:, 1::2])
    com["b3_g_c"] = bc(b3[:, 0::2]); com["b3_l_c"] = bc(b3[:, 1::2])
    com["b2_c"] = np.asarray(g("L1_ffn_b2")).reshape(NE, NCH, 128).transpose(0, 2, 1).reshape(NE * 128, NCH)
    sw1 = g("L1_ffn_sw1"); sw3 = g("L1_ffn_sw3")
    spk = lambda a: np.ascontiguousarray(
        np.asarray(a).reshape(NCH, 128, 16, 128)
        .transpose(2, 1, 0, 3).reshape(MI, D))
    com["sw1_g"] = spk(sw1[:, 0::2]); com["sw1_l"] = spk(sw1[:, 1::2])
    com["sw3_g"] = spk(sw3[:, 0::2]); com["sw3_l"] = spk(sw3[:, 1::2])
    com["sw2"] = g("L1_ffn_sw2")
    sb1 = g("L1_ffn_sb1"); sb3 = g("L1_ffn_sb3")
    sbc = lambda a: np.asarray(a).reshape(16, 128).T
    com["sb1_g_c"] = sbc(sb1[0::2]); com["sb1_l_c"] = sbc(sb1[1::2])
    com["sb3_g_c"] = sbc(sb3[0::2]); com["sb3_l_c"] = sbc(sb3[1::2])
    com["sb2"] = g("L1_ffn_sb2")
    for k in ("ident_f32", "ones_f32", "ones_row", "zero_col", "halfpi_col",
              "rot_cos", "rot_sin", "dlt", "alpha", "beta_bc", "rnorm", "ehot"):
        com[k] = C[k]
    for v in ("bk", "bv", "bq", "bkp", "bqp", "bo", "ps_col", "ln_g", "ln_b",
              "q_b", "k_b", "o_b", "sb2"):
        com[v] = np.asarray(com[v]).reshape(NCH, 128).T
    com["v_b_row"] = np.asarray(com["v_b"]).reshape(1, 2 * D); del com["v_b"]
    com["g_b"] = np.asarray(com["g_b"]).reshape(16, 128).T
    com = {k: np.ascontiguousarray(np.asarray(v, dtype=np.float32)) for k, v in com.items()}
    import ml_dtypes
    com["ident_bf"] = np.eye(128).astype(ml_dtypes.bfloat16)

    x = np.asarray(inputs["L0_attn_x"])
    c0 = np.asarray(inputs["L0_c"]); c1 = np.asarray(inputs["L1_c"])
    maps = []
    for b in range(B):
        m = dict(com)
        m["xT"] = np.ascontiguousarray(x[b].T.astype(np.float32))
        m["c0"] = np.ascontiguousarray(c0[b].reshape(NCH, 128).T.astype(np.float32))
        m["c1"] = np.ascontiguousarray(c1[b].reshape(NCH, 128).T.astype(np.float32))
        maps.append(m)
    return maps


def kernel(**inputs):
    maps = pack_inputs(inputs)
    nc = build_program()
    res = run_bass_kernel_spmd(nc, maps, core_ids=list(range(B)))
    outs = [np.asarray(res.results[b]["out_fm"]).T for b in range(B)]
    return np.ascontiguousarray(np.stack(outs)).astype(np.float32)


# version 13
# speedup vs baseline: 1.1586x; 1.0176x over previous
import math
from contextlib import ExitStack

import numpy as np
import concourse.bass as bass
import concourse.mybir as mybir
from concourse.bass_utils import run_bass_kernel_spmd

F32 = mybir.dt.float32
BF16 = mybir.dt.bfloat16
I32 = mybir.dt.int32
Alu = mybir.AluOpType
Act = mybir.ActivationFunctionType
AX = mybir.AxisListType

B, T, D = 8, 2048, 1024
NCH = 8
NT = 4
TTS = 512
NH, DKH, DVH = 16, 64, 128
CRET = 128
EPS = 1e-5
DI = 3072
MI = 2048
NE = 8


class Prog:
    ENGINES = ("sync", "vector", "scalar", "tensor", "gpsimd")
    PERSIST = ("bar", "st", "ldx", "ldc", "ldw0", "ldw0b", "ldw1", "pcast",
               "l0aSp00", "l0aSp01", "l0aSp10", "l0aSp11", "l0aSp20", "l0aSp21")

    def __init__(self, nc, ctx, taps=()):
        self.nc = nc
        self.ctx = ctx
        self.taps = list(taps)
        self.sems = {}
        self.base = {}
        self.cnt = {}
        self.pcnt = {}
        allp = [ctx.enter_context(nc.semaphore(f"pool{i}")) for i in range(87)]
        self.pool = allp[:52]
        self.gpool = allp[52:]
        self.pool_i = 0
        self.gpool_i = 0
        self.kind = {}
        self._persist = {n: ctx.enter_context(nc.semaphore("s_" + n))
                         for n in self.PERSIST}
        for h in allp + list(self._persist.values()):
            self.pcnt[h.name] = 0
        self.phase_names = set()
        self.bar = self.sem("bar")
        self.nbar = 0
        self.sem("st")

    def sem(self, name, gp=False):
        if name not in self.sems:
            if name in self._persist:
                h = self._persist[name]
            elif gp:
                h = self.gpool[self.gpool_i % len(self.gpool)]
                self.gpool_i += 1
                self.kind[name] = "gp"
            else:
                h = self.pool[self.pool_i % len(self.pool)]
                self.pool_i += 1
                self.kind[name] = "hw"
            self.sems[name] = h
            self.base[name] = self.pcnt[h.name]
            self.cnt[name] = 0
        return self.sems[name]

    def inc(self, ins, name, n=1, quiesce=True):
        gp = "qPoolDynamic" in ins.concise()
        if name not in self.sems:
            h = self.sem(name, gp=gp)
        else:
            h = self.sems[name]
            if gp and self.kind.get(name) == "hw" and name not in self._persist:
                raise RuntimeError(f"sem {name}: gpsimd DMA inc on hw-pool sem "
                                   "(bind order bug - inc before wait)")
        ins.then_inc(h, n)
        self.cnt[name] += n
        self.pcnt[h.name] += n
        if quiesce:
            self.phase_names.add(name)
        return ins

    def wait(self, e, name, val=None):
        h = self.sem(name)
        e.wait_ge(h, self.base[name] + (self.cnt[name] if val is None else val))

    def val(self, name):
        self.sem(name)
        return self.cnt[name]

    def sb(self, name, shape, dtype, stack=None):
        return (stack or self.ctx).enter_context(self.nc.sbuf_tensor(name, shape, dtype))

    def ps(self, name, shape, dtype=F32, stack=None):
        return (stack or self.ctx).enter_context(self.nc.psum_tensor(name, shape, dtype))

    def phase(self, name):
        return _Phase(self, name)


class _Phase:
    def __init__(self, prog, name):
        self.p = prog
        self.name = name

    def __enter__(self):
        self.block = self.p.nc.Block(self.name)
        self.block.__enter__()
        return self

    def eng(self, which, f):
        p = self.p

        def wrapped(e):
            if p.nbar:
                e.wait_ge(p.bar, 5 * p.nbar)
            f(e)
        getattr(self.block, which)(wrapped)

    def __exit__(self, *exc):
        if exc[0] is None:
            p = self.p
            qn = [(n, p.base[n] + p.cnt[n]) for n in sorted(p.phase_names)
                  if n != "bar"]
            p.phase_names = set()
            if qn:
                def quiesce(e):
                    if p.nbar:
                        e.wait_ge(p.bar, 5 * p.nbar)
                    for n, v in qn:
                        e.wait_ge(p.sems[n], v)
                self.block.sync(quiesce)
            for which in Prog.ENGINES:
                def endf(e):
                    if p.nbar:
                        e.wait_ge(p.bar, 5 * p.nbar)
                    e.drain().then_inc(p.bar, 1)
                    p.cnt["bar"] += 1
                    p.pcnt[p.bar.name] += 1
                getattr(self.block, which)(endf)
            p.nbar += 1
        return self.block.__exit__(*exc)


def host_constants():
    c = {}
    c["ident_f32"] = np.eye(128, dtype=np.float32)
    c["ones_f32"] = np.ones((128, 1), np.float32)
    c["ones_row"] = np.ones((1, 128), np.float32)
    c["zero_col"] = np.zeros((128, 1), np.float32)
    c["halfpi_col"] = np.full((128, 1), math.pi / 2, np.float32)
    half = 1.0 / np.power(10000.0, np.linspace(0.0, 1.0, DKH // 2))
    angle = np.repeat(half, 2)
    idx = np.arange(T, dtype=np.float64)
    sin = np.sin(idx[:, None] * angle[None, :])
    cos = np.cos(idx[:, None] * angle[None, :])
    c["rot_cos"] = np.tile(cos.T, (2, 1)).astype(np.float32)
    sign = np.where(np.arange(DKH) % 2 == 0, -1.0, 1.0)
    c["rot_sin"] = np.tile((sin * sign[None, :]).T, (2, 1)).astype(np.float32)
    decay = np.log(1.0 - 2.0 ** (-5.0 - np.arange(NH, dtype=np.float64)))
    r = np.arange(CRET)
    dl = np.exp((r[None, :] - r[:, None])[None] * decay[:, None, None])
    dl = np.where(r[None, :] >= r[:, None], dl, 0.0)
    c["dlt"] = np.ascontiguousarray(
        dl.transpose(1, 0, 2).reshape(CRET, NH * CRET)).astype(np.float32)
    c["alpha"] = np.exp((CRET - 1 - r)[:, None] * decay[None, :]).astype(np.float32)
    beta = np.exp((r + 1)[None, :] * decay[:, None])
    c["beta_bc"] = np.broadcast_to(
        beta.reshape(1, NH * CRET), (128, NH * CRET)).astype(np.float32).copy()
    lam = np.exp(decay)
    n = np.arange(T, dtype=np.float64)
    rowsum = (1.0 - lam[:, None] ** (n[None, :] + 1.0)) / (1.0 - lam[:, None])
    c["rnorm"] = (1.0 / np.sqrt(rowsum)).astype(np.float32)
    eh = np.zeros((128, NH * NH), np.float32)
    for h_ in range(NH):
        eh[:, h_ * NH + h_] = 1.0
    c["ehot"] = eh
    c["lam_pc"] = [float(v) for v in lam ** CRET]
    return c


CONSTS = None


def build_program(taps=(), upto=None, start=None):
    global CONSTS
    if CONSTS is None:
        CONSTS = host_constants()
    nc = bass.Bass("TRN2", target_bir_lowering=False, debug=False)
    dI = {}

    def din(name, shape, dtype=F32):
        dI[name] = nc.dram_tensor(name, list(shape), dtype, kind="ExternalInput").ap()

    din("xT", (D, T))
    for w in ("wk_t", "wv_t", "wq_t", "wkp_t", "wqp_t", "wo_t"):
        din(w, (D, D))
    for v in ("bk", "bv", "bq", "bkp", "bqp", "bo", "ps_col", "ln_g", "ln_b",
              "c0", "c1", "q_b", "k_b", "o_b", "sb2"):
        din(v, (128, NCH))
    din("w_up", (DI, D)); din("w_gate", (DI, D)); din("w_down", (D, DI))
    din("wmod0", (D, 6 * D)); din("wmod1", (D, 6 * D))
    din("q_w", (D, D)); din("k_w", (D, D)); din("v_w", (D, 2 * D)); din("g_w", (D, 2 * D))
    din("o_w", (2 * D, D))
    din("v_b_row", (1, 2 * D)); din("g_b", (128, 16))
    din("ehot", (128, NH * NH))
    din("gate_wp", (128, NCH * NE)); din("gate_bbc", (128, NE))
    for nm in ("w1_g", "w1_l", "w3_g", "w3_l"):
        din(nm, (NE * MI, D))
    din("w2", (NE * MI, D))
    for nm in ("b1_g", "b1_l", "b3_g", "b3_l"):
        din(nm + "_c", (NE * 128, 16))
    din("b2_c", (NE * 128, NCH))
    for nm in ("sw1_g", "sw1_l", "sw3_g", "sw3_l"):
        din(nm, (MI, D))
    din("sw2", (MI, D))
    for nm in ("sb1_g", "sb1_l", "sb3_g", "sb3_l"):
        din(nm + "_c", (128, 16))
    din("ident_f32", (128, 128))
    din("ones_f32", (128, 1)); din("ones_row", (1, 128))
    din("zero_col", (128, 1)); din("halfpi_col", (128, 1))
    din("ident_bf", (128, 128), BF16)
    din("rot_cos", (128, T)); din("rot_sin", (128, T))
    din("dlt", (CRET, NH * CRET)); din("alpha", (CRET, NH))
    din("beta_bc", (128, NH * CRET)); din("rnorm", (NH, T))

    out = nc.dram_tensor("out_fm", [D, T], F32, kind="ExternalOutput").ap()

    scr = {
        "mod": nc.dram_tensor("scr_mod", [2, 6 * D], F32, kind="Internal").ap(),
        "row": nc.dram_tensor("scr_row", [64, T], F32, kind="Internal").ap(),
        "P": nc.dram_tensor("scr_P", [DI, T], F32, kind="Internal").ap(),
        "G": nc.dram_tensor("scr_G", [DI, T], F32, kind="Internal").ap(),
        "O": nc.dram_tensor("scr_O", [2 * D, T], F32, kind="Internal").ap(),
        "x": nc.dram_tensor("scr_x", [D, T], F32, kind="Internal").ap(),
        "comb": nc.dram_tensor("scr_comb", [NE, T], F32, kind="Internal").ap(),
        "kq": nc.dram_tensor("scr_kq", [2 * D, T], BF16, kind="Internal").ap(),
        "UP": nc.dram_tensor("scr_UP", [DI, T], mybir.dt.int16, kind="Internal").ap(),
    }
    for nm in ("w1_g", "w1_l", "w3_g", "w3_l"):
        scr["c" + nm] = nc.dram_tensor("scr_c" + nm, [NE * MI, D], BF16,
                                       kind="Internal").ap()
        scr["cs" + nm[1:]] = nc.dram_tensor("scr_cs" + nm[1:], [MI, D], BF16,
                                            kind="Internal").ap()
    scr["cw2"] = nc.dram_tensor("scr_cw2", [NE * MI, D], BF16,
                                kind="Internal").ap()
    scr["csw2"] = nc.dram_tensor("scr_csw2", [MI, D], BF16,
                                 kind="Internal").ap()
    cc = [(nc.dram_tensor(f"cc_in{i}", [1, 8], F32, kind="Internal").ap(),
           nc.dram_tensor(f"cc_out{i}", [1, 8], F32, kind="Internal").ap())
          for i in range(4)]

    with ExitStack() as ctx:
        p = Prog(nc, ctx, taps)
        em = Emit(p, dI, out, scr, cc, taps, upto)
        em.start = start
        em.run()
    return nc


class Emit:
    def __init__(self, p, dI, out, scr, cc, taps, upto):
        self.p = p
        self.dI = dI
        self.out = out
        self.scr = scr
        self.cc = cc
        self.taps = list(taps)
        self.upto = upto

    def tap(self, name, src_list):
        if name not in self.taps:
            return
        p = self.p
        ap0 = src_list[0]
        n = len(src_list)
        t = p.nc.dram_tensor("tap_" + name, [n * ap0.shape[0]] + list(ap0.shape[1:]),
                             ap0.dtype, kind="ExternalOutput").ap()
        with p.phase("tap_" + name) as ph:
            def f(s):
                for i, a in enumerate(src_list):
                    p.inc(s.dma_start(
                        out=t[i * ap0.shape[0]:(i + 1) * ap0.shape[0]], in_=a), "st", 16)
            ph.eng("sync", f)

    def tap_dram(self, name, dram_ap, nrows):
        if name not in self.taps:
            return
        p = self.p
        t = p.nc.dram_tensor("tap_" + name, [nrows, T], dram_ap.dtype,
                             kind="ExternalOutput").ap()
        with p.phase("tapd_" + name) as ph:
            def f(s):
                for c in range(nrows // 128):
                    p.inc(s.dma_start(out=t[c * 128:(c + 1) * 128, :],
                                      in_=dram_ap[c * 128:(c + 1) * 128, :]),
                          "st", 16)
            ph.eng("sync", f)

    def stop(self, name):
        return self.upto is not None and name == self.upto

    def run(self):
        p = self.p
        self.alloc_persistent()
        self.p0_loads()
        if getattr(self, "start", None) != "l1":
            self.mod_layer(0)
            self.tap("mod0", [self.modfm[0].ap()])
            if self.stop("mod0"):
                return self.finish()
            r = self.block_l0()
            if r:
                return self.finish()
        r = self.block_l1()
        if r:
            return self.finish()
        self.finish(final=True)

    def block_l0(self):
        p = self.p
        with ExitStack() as sc0:
            hb = [p.sb(f"hb0_{c}", [128, T], BF16, sc0) for c in range(NCH)]
            with ExitStack() as sc1:
                wk = [p.sb(f"wkc{c}", [128, D], BF16, sc1) for c in range(NCH)]
                wq = [p.sb(f"wqc{c}", [128, D], BF16, sc1) for c in range(NCH)]
                wv = [p.sb(f"wvc{c}", [128, D], BF16, sc1) for c in range(NCH)]
                with p.phase("l0w1") as ph:
                    def f_g(g):
                        for c in range(NCH):
                            for dst, s_ in ((wk, "wk_t"), (wq, "wq_t"), (wv, "wv_t")):
                                p.inc(g.dma_start(
                                    out=dst[c][:],
                                    in_=self.dI[s_][c * 128:(c + 1) * 128, :]),
                                    "ldw0", 16)
                    ph.eng("gpsimd", f_g)
                self.ln_mod(dst=hb, ca=self.modfm[0][:, 8:16],
                            sa=self.modfm[0][:, 0:8], tag="ln1",
                            resid_g=self.modfm[0][:, 16:24])
                self.tap("ln1", [h.ap() for h in hb])
                if self.stop("ln1"):
                    return True
                self.l0_attn_a(hb, wk, wq, wv)
            self.l0_attn_b()
        self.tap("l0attn", [xc.ap() for xc in self.xres])
        if self.stop("l0attn"):
            return True
        with ExitStack() as scf:
            if self.l0_ffn(scf):
                return True
        self.tap("l0ffn", [xc.ap() for xc in self.xres])
        return self.stop("l0ffn")

    def block_l1(self):
        p = self.p
        self.mod_layer(1)
        self.tap("mod1", [self.modfm[1].ap()])
        if self.stop("mod1"):
            return True
        with ExitStack() as sc0:
            hb = [p.sb(f"hb1_{c}", [128, T], BF16, sc0) for c in range(NCH)]
            self.ln_mod(dst=hb, ca=self.modfm[1][:, 8:16],
                        sa=self.modfm[1][:, 0:8], tag="ln3")
            self.tap("ln3", [h.ap() for h in hb])
            if self.stop("ln3"):
                return True
            self.l1_attn(hb, sc0)
        self.tap("l1attn", [xc.ap() for xc in self.xres])
        if self.stop("l1attn"):
            return True
        with ExitStack() as scm:
            hb = [p.sb(f"hbm_{c}", [128, T], BF16, scm) for c in range(NCH)]
            self.ln_mod(dst=hb, ca=self.modfm[1][:, 32:40],
                        sa=self.modfm[1][:, 24:32], tag="ln4",
                        spill32=self.scr["P"])
            if self.stop("ln4"):
                return True
            if self.l1_moe(hb, scm):
                return True
        return False

    def finish(self, final=False):
        p = self.p
        with p.phase("fin") as ph:
            def f(s):
                if final:
                    for c in range(NCH):
                        p.inc(s.dma_start(out=self.out[c * 128:(c + 1) * 128, :],
                                          in_=self.xres[c][:]), "st", 16)
                p.wait(s, "st")
            ph.eng("sync", f)

    def alloc_persistent(self):
        p = self.p
        self.xres = [p.sb(f"xres{c}", [128, T], F32) for c in range(NCH)]
        self.identf = p.sb("identf", [128, 128], F32)
        self.identb = p.sb("identb", [128, 128], BF16)
        self.onesf = p.sb("onesf", [128, 1], F32)
        self.onesr = p.sb("onesr", [1, 128], F32)
        self.zcol = p.sb("zcol", [128, 1], F32)
        self.hpicol = p.sb("hpicol", [128, 1], F32)
        self.onesb = p.sb("onesb", [128, 1], BF16)
        self.cols = {}
        for v in ("bk", "bv", "bq", "bkp", "bqp", "bo", "ps_col", "ln_g", "ln_b",
                  "c0", "c1", "q_b", "k_b", "o_b", "sb2"):
            self.cols[v] = p.sb("col_" + v, [128, 8], F32)
        self.modfm = [p.sb(f"modfm{l}", [128, 48], F32) for l in range(2)]
        self.negps = p.sb("negps", [128, 8], F32)

    def p0_loads(self):
        p, dI = self.p, self.dI
        with p.phase("p0") as ph:
            def f_sync(s):
                for c in range(NCH):
                    p.inc(s.dma_start(out=self.xres[c][:],
                                      in_=dI["xT"][c * 128:(c + 1) * 128, :]), "ldx", 16)
                p.inc(s.dma_start(out=self.identf[:], in_=dI["ident_f32"]), "ldc", 16)
                p.inc(s.dma_start(out=self.identb[:], in_=dI["ident_bf"]), "ldc", 16)
                p.inc(s.dma_start(out=self.onesf[:], in_=dI["ones_f32"]), "ldc", 16)
                p.inc(s.dma_start(out=self.onesr[:], in_=dI["ones_row"]), "ldc", 16)
                p.inc(s.dma_start(out=self.zcol[:], in_=dI["zero_col"]), "ldc", 16)
                p.inc(s.dma_start(out=self.hpicol[:], in_=dI["halfpi_col"]), "ldc", 16)
                for v in ("bk", "bv", "bq", "bkp", "bqp", "bo", "ps_col", "ln_g",
                          "ln_b", "c0", "c1", "q_b", "k_b", "o_b", "sb2"):
                    p.inc(s.dma_start(out=self.cols[v][:], in_=dI[v]), "ldc", 16)
            ph.eng("sync", f_sync)

            def f_vec(v):
                p.wait(v, "ldc")
                v.tensor_copy(self.onesb[:], self.onesf[:])
                v.tensor_scalar_mul(self.negps[:], self.cols["ps_col"][:], -1.0)
                v.drain()
            ph.eng("vector", f_vec)


    def mod_layer(self, layer):
        p, dI = self.p, self.dI
        cname = "c0" if layer == 0 else "c1"
        wname = "wmod0" if layer == 0 else "wmod1"
        with ExitStack() as sc:
            wmodb = [p.sb(f"wmodb{layer}_{j}", [128, 6 * D], F32, sc) for j in range(2)]
            with p.phase(f"mod{layer}") as ph:
                csil = p.sb(f"modcs{layer}", [128, 8], F32, sc)
                mps = p.ps(f"modps{layer}", [128, 48], F32, sc)
                sS, sV, sM, sF = (f"mod{layer}" + x for x in "SVMF")
                sW = [f"mod{layer}w{j}" for j in range(2)]

                def f_g(g):
                    for c in range(NCH):
                        if c >= 2:
                            p.wait(g, sF, c - 1)
                        p.inc(g.dma_start(out=wmodb[c % 2][:],
                                          in_=dI[wname][c * 128:(c + 1) * 128, :]),
                              sW[c % 2], 16)
                ph.eng("gpsimd", f_g)

                def f_sc(s):
                    p.wait(s, "ldc")
                    s.activation(csil[:], self.cols[cname][:], Act.Silu)
                    p.inc(s.drain(), sS, 1)
                ph.eng("scalar", f_sc)

                def f_v(v):
                    p.wait(v, sF, p.val(sF) + NCH)
                    v.tensor_copy(self.modfm[layer][:], mps[:])
                    v.drain()
                ph.eng("vector", f_v)

                def f_t(t):
                    p.wait(t, sS, 1)
                    for c in range(NCH):
                        p.wait(t, sW[c % 2], 16 * (c // 2 + 1))
                        for j in range(48):
                            ins = t.matmul(mps[:, j:j + 1],
                                           wmodb[c % 2][:, j * 128:(j + 1) * 128],
                                           csil[:, c:c + 1],
                                           start=(c == 0 and j == 0),
                                           stop=(c == NCH - 1 and j == 47))
                        p.inc(ins, sF, 1)
                ph.eng("tensor", f_t)

        with p.phase(f"modg{layer}") as ph:
            def f_v(v):
                with ExitStack() as s2:
                    gtmp = self.p.sb(f"modgt{layer}", [128, 24], F32, s2)
                    for off in (16, 40):
                        g = self.modfm[layer][:, off:off + 8]
                        sgn, mag, neg = gtmp[:, 0:8], gtmp[:, 8:16], gtmp[:, 16:24]
                        v.tensor_scalar(sgn, g, 0.0, None, Alu.is_ge)
                        v.tensor_scalar_mul(neg, g, -1.0)
                        v.drain()
                        v.tensor_scalar(sgn, sgn, 2.0, -1.0, Alu.mult, Alu.add)
                        v.tensor_tensor(mag, g, neg, Alu.max)
                        v.drain()
                        v.tensor_scalar_max(mag, mag, 0.25)
                        v.drain()
                        v.tensor_tensor(g, sgn, mag, Alu.mult)
                        v.drain()
            ph.eng("vector", f_v)

    def ln_mod(self, dst, ca, sa, tag, resid_g=None, spill32=None):
        p = self.p
        with ExitStack() as sc:
            rows = [p.sb(tag + f"_r{i}", [1, T], F32, sc) for i in range(4)]
            arow, brow, rmean, rtmp = (r.ap() for r in rows)
            sc_st = sc.enter_context(ExitStack())
            stat_s = p.ps(tag + "_sts", [1, T], F32, sc_st)
            stat_q = p.ps(tag + "_stq", [1, T], F32, sc_st)
            sq = [p.sb(tag + f"_sq{j}", [128, T], F32, sc)
                  for j in range(2 if resid_g is not None else 1)]

            with p.phase(tag + "a") as ph:
                sQ, sM = tag + "Q", tag + "M"
                mbase, qbase = p.val(sM), p.val(sQ)

                def f_sc(s):
                    p.wait(s, "ldx")
                    for c in range(NCH):
                        if c >= 1:
                            p.wait(s, sM, mbase + 2 * c)
                        p.inc(s.activation(sq[0][:], self.xres[c][:], Act.Square),
                              sQ, 1)
                ph.eng("scalar", f_sc)

                def f_t(t):
                    p.wait(t, "ldx")
                    p.wait(t, "ldc")
                    for c in range(NCH):
                        for n in range(NT):
                            ins = t.matmul(stat_s[0:1, n * TTS:(n + 1) * TTS],
                                           self.onesf[:],
                                           self.xres[c][:, n * TTS:(n + 1) * TTS],
                                           start=(c == 0), stop=(c == NCH - 1))
                        p.inc(ins, sM, 1)
                        p.wait(t, sQ, qbase + c + 1)
                        for n in range(NT):
                            ins = t.matmul(stat_q[0:1, n * TTS:(n + 1) * TTS],
                                           self.onesf[:],
                                           sq[0][:, n * TTS:(n + 1) * TTS],
                                           start=(c == 0), stop=(c == NCH - 1))
                        p.inc(ins, sM, 1)
                ph.eng("tensor", f_t)

            with p.phase(tag + "b") as ph:
                sW, sS, sR = tag + "W", tag + "S", tag + "R"

                def f_v(v):
                    v.tensor_scalar_mul(rmean, stat_s[0:1, :], 1.0 / D)
                    v.tensor_scalar_mul(rtmp, stat_q[0:1, :], 1.0 / D)
                    v.drain()
                    v.tensor_tensor(arow, rmean, rmean, Alu.mult)
                    v.drain()
                    v.tensor_tensor(rtmp, rtmp, arow, Alu.subtract)
                    v.drain()
                    v.tensor_scalar_add(rtmp, rtmp, EPS)
                    v.drain()
                    v.reciprocal(rtmp, rtmp)
                    p.inc(v.drain(), sW, 1)
                    p.wait(v, sS, 1)
                    v.tensor_tensor(brow, rmean, arow, Alu.mult)
                    v.drain()
                    v.tensor_scalar_mul(brow, brow, -1.0)
                    v.drain()
                    p.inc(v.drain(), sR, 1)
                ph.eng("vector", f_v)

                def f_sc(s):
                    p.wait(s, sW, 1)
                    p.inc(s.activation(arow, rtmp, Act.Sqrt), sS, 1)
                ph.eng("scalar", f_sc)

            sc_st.close()
            with p.phase(tag + "c") as ph:
                abps = [p.ps(tag + f"_ab{j}", [128, 512], F32, sc) for j in range(4)]
                ca1 = p.sb(tag + "_ca1", [128, 8], F32, sc)
                sB, sA = tag + "B", tag + "A"
                if spill32 is not None:
                    hfb = [p.sb(tag + f"_hf{j}", [128, TTS], F32, sc)
                           for j in range(2)]
                    sH = tag + "H"
                    sHD = [tag + "HD0", tag + "HD1"]

                    def f_sy(s):
                        for j in range(NT * NCH):
                            n, c = j // NCH, j % NCH
                            p.wait(s, sH, j + 1)
                            p.inc(s.dma_start(
                                out=spill32[c * 128:(c + 1) * 128,
                                            n * TTS:(n + 1) * TTS],
                                in_=hfb[j % 2][:]), sHD[j % 2], 16)
                    ph.eng("sync", f_sy)

                def f_t(t):
                    for n in range(NT):
                        if n >= 2:
                            p.wait(t, sA, NCH * (n - 1))
                        ins = t.matmul(abps[2 * (n % 2)][:], self.onesr[:],
                                       arow[:, n * TTS:(n + 1) * TTS],
                                       start=True, stop=True)
                        ins2 = t.matmul(abps[2 * (n % 2) + 1][:], self.onesr[:],
                                        brow[:, n * TTS:(n + 1) * TTS],
                                        start=True, stop=True)
                        p.inc(ins2, sB, 1)
                ph.eng("tensor", f_t)

                def f_v(v):
                    v.tensor_scalar_add(ca1[:], ca, 1.0)
                    v.drain()
                    for n in range(NT):
                        p.wait(v, sB, n + 1)
                        a = abps[2 * (n % 2)]
                        bb = abps[2 * (n % 2) + 1]
                        tl = slice(n * TTS, (n + 1) * TTS)
                        for c in range(NCH):
                            tmp = sq[0][:, 0:TTS]
                            v.tensor_tensor(tmp, self.xres[c][:, tl], a[:], Alu.mult)
                            v.drain()
                            v.tensor_tensor(tmp, tmp, bb[:], Alu.add)
                            v.drain()
                            if resid_g is None and spill32 is not None:
                                j = n * NCH + c
                                if j >= 2:
                                    p.wait(v, sHD[j % 2], 16 * (j // 2))
                                hf = hfb[j % 2]
                                v.tensor_scalar(hf[:], tmp,
                                                ca1[:, c:c + 1], sa[:, c:c + 1],
                                                Alu.mult, Alu.add)
                                v.drain()
                                p.inc(v.drain(), sH, 1)
                                ins = v.tensor_copy(dst[c][:, tl], hf[:])
                                v.drain()
                            elif resid_g is None:
                                ins = v.tensor_scalar(dst[c][:, tl], tmp,
                                                      ca1[:, c:c + 1], sa[:, c:c + 1],
                                                      Alu.mult, Alu.add)
                                v.drain()
                            else:
                                hf = sq[-1][:, 0:TTS]
                                v.tensor_scalar(hf, tmp, ca1[:, c:c + 1],
                                                sa[:, c:c + 1], Alu.mult, Alu.add)
                                v.drain()
                                v.tensor_copy(dst[c][:, tl], hf)
                                v.tensor_scalar_mul(tmp, hf, resid_g[:, c:c + 1])
                                v.drain()
                                ins = v.tensor_tensor(self.xres[c][:, tl],
                                                      self.xres[c][:, tl], tmp, Alu.add)
                                v.drain()
                            p.inc(ins, sA, 1)
                ph.eng("vector", f_v)

    def l0_attn_a(self, hb, wk, wq, wv):
        p = self.p
        with ExitStack() as sca, p.phase("l0a") as ph:
            ppk = [p.ps(f"l0apk{j}", [128, TTS], F32, sca) for j in range(2)]
            ppq = [p.ps(f"l0apq{j}", [128, TTS], F32, sca) for j in range(2)]
            ppv = [p.ps(f"l0apv{j}", [128, TTS], F32, sca) for j in range(2)]
            ktile = [p.sb(f"l0ak{j}", [128, TTS], BF16, sca) for j in range(2)]
            qtile = [p.sb(f"l0aq{j}", [128, TTS], BF16, sca) for j in range(2)]
            vtile = [p.sb(f"l0av{j}", [128, TTS], F32, sca) for j in range(2)]
            sM, sE = "l0aM", "l0aE"
            sSp = [[f"l0aSp{t_}{j}" for j in range(2)] for t_ in range(3)]

            def f_t(t):
                p.wait(t, "ldw0")
                for i in range(NCH * NT):
                    c, n = divmod(i, NT)
                    if i >= 2:
                        p.wait(t, sE, 3 * (i - 1))
                    for wmat, bank in ((wk, ppk), (wq, ppq), (wv, ppv)):
                        for k in range(NCH):
                            ins = t.matmul(
                                bank[i % 2][:],
                                wmat[k][:, c * 128:(c + 1) * 128],
                                hb[k][:, n * TTS:(n + 1) * TTS],
                                start=(k == 0), stop=(k == NCH - 1))
                        p.inc(ins, sM, 1)
            ph.eng("tensor", f_t)

            def f_sc(s):
                for i in range(NCH * NT):
                    c, n = divmod(i, NT)
                    if i >= 2:
                        for t_ in range(3):
                            p.wait(s, sSp[t_][i % 2], 16 * ((i - 2) // 2 + 1))
                    p.wait(s, sM, 3 * i + 1)
                    p.inc(s.activation(ktile[i % 2][:], ppk[i % 2][:], Act.Identity,
                                       bias=self.cols["bk"][:, c:c + 1]), sE, 1)
                    p.wait(s, sM, 3 * i + 2)
                    p.inc(s.activation(qtile[i % 2][:], ppq[i % 2][:], Act.Identity,
                                       bias=self.cols["bq"][:, c:c + 1]), sE, 1)
                    p.wait(s, sM, 3 * i + 3)
                    p.inc(s.activation(vtile[i % 2][:], ppv[i % 2][:], Act.Identity,
                                       bias=self.cols["bv"][:, c:c + 1]), sE, 1)
            ph.eng("scalar", f_sc)

            def f_sync(s):
                for i in range(NCH * NT):
                    c, n = divmod(i, NT)
                    tl = slice(n * TTS, (n + 1) * TTS)
                    p.wait(s, sE, 3 * i + 3)
                    p.inc(s.dma_start(out=self.scr["kq"][c * 128:(c + 1) * 128, tl],
                                      in_=ktile[i % 2][:]), sSp[0][i % 2], 16)
                    p.inc(s.dma_start(out=self.scr["kq"][D + c * 128:D + (c + 1) * 128, tl],
                                      in_=qtile[i % 2][:]), sSp[1][i % 2], 16)
                    p.inc(s.dma_start(out=self.scr["P"][c * 128:(c + 1) * 128, tl],
                                      in_=vtile[i % 2][:]), sSp[2][i % 2], 16)
                for t_ in range(3):
                    for j in range(2):
                        p.wait(s, sSp[t_][j])
            ph.eng("sync", f_sync)

    def l0_attn_b(self):
        p, dI = self.p, self.dI
        with ExitStack() as scb:
            wkp = [p.sb(f"wkpc{c}", [128, D], BF16, scb) for c in range(NCH)]
            wqp = [p.sb(f"wqpc{c}", [128, D], BF16, scb) for c in range(NCH)]
            wo = [p.sb(f"woc{c}", [128, D], BF16, scb) for c in range(NCH)]
            gabo = p.sb("l0gabo", [128, 8], F32, scb)
            with p.phase("l0wb") as ph:
                def f_g(g):
                    for c in range(NCH):
                        for dst, s_ in ((wkp, "wkp_t"), (wqp, "wqp_t"), (wo, "wo_t")):
                            p.inc(g.dma_start(out=dst[c][:],
                                              in_=dI[s_][c * 128:(c + 1) * 128, :]),
                                  "ldw0b", 16)
                ph.eng("gpsimd", f_g)

                def f_v(v):
                    v.tensor_tensor(gabo[:], self.modfm[0][:, 16:24],
                                    self.cols["bo"][:], Alu.mult)
                    v.drain()
                ph.eng("vector", f_v)

            for n in range(NT):
                self._l0_tile(n, wkp, wqp, wo, gabo)

    def _l0_tile(self, n, wkp, wqp, wo, gabo):
        p = self.p
        tl = slice(n * TTS, (n + 1) * TTS)
        tg = f"l0b{n}"
        C = self.cols
        with ExitStack() as sc, p.phase(tg) as ph:
            keyf = [p.sb(tg + f"keyf{c}", [128, TTS], BF16, sc) for c in range(NCH)]
            qryf = [p.sb(tg + f"qryf{c}", [128, TTS], BF16, sc) for c in range(NCH)]
            kp = p.sb(tg + "kp", [128, TTS], F32, sc)
            skp = p.sb(tg + "skp", [128, TTS], F32, sc)
            ckp = p.sb(tg + "ckp", [128, TTS], F32, sc)
            qp = p.sb(tg + "qp", [128, TTS], F32, sc)
            sqp = p.sb(tg + "sqp", [128, TTS], F32, sc)
            cqp = p.sb(tg + "cqp", [128, TTS], F32, sc)
            val = [p.sb(tg + f"val{j}", [128, TTS], F32, sc) for j in range(2)]
            scn = p.sb(tg + "scn", [128, TTS], F32, sc)
            mr = p.sb(tg + "mr", [128, TTS], F32, sc)
            mi = p.sb(tg + "mi", [128, TTS], F32, sc)
            reti = [p.sb(tg + f"ret{c}", [128, TTS], F32, sc) for c in range(NCH)]
            sqb = [p.sb(tg + f"sqb{j}", [128, TTS], F32, sc) for j in range(2)]
            lno = keyf
            tres = val
            arow, brow = mr[0:1, :], mi[0:1, :]
            rmean, rtmp = scn[0:1, :], ckp[0:1, :]
            ppk = p.ps(tg + "ppk", [128, TTS], F32, sc)
            ppq = p.ps(tg + "ppq", [128, TTS], F32, sc)
            ppo = [p.ps(tg + f"ppo{j}", [128, TTS], F32, sc) for j in range(2)]
            pst_s = p.ps(tg + "psts", [1, TTS], F32, sc)
            pst_q = p.ps(tg + "pstq", [1, TTS], F32, sc)
            pab = [p.ps(tg + f"pab{j}", [128, TTS], F32, sc) for j in range(2)]
            (sKQ, sM, sE, sKP, sQP, sC, sQ2, sMs, sW, sS, sR, sB, sA, sO,
             sF, sX) = (tg + x_ for x_ in
                        ("KQ", "M", "E", "KP", "QP", "C", "Q2", "Ms", "W",
                         "S", "R", "B", "A", "O", "F", "X"))
            sL = [tg + "L0", tg + "L1"]

            def f_sync(s):
                for t_ in range(3):
                    for j in range(2):
                        p.wait(s, f"l0aSp{t_}{j}")
                for c in range(NCH):
                    p.inc(s.dma_start(out=keyf[c][:],
                                      in_=self.scr["kq"][c * 128:(c + 1) * 128, tl]),
                          sKQ, 16)
                    p.inc(s.dma_start(out=qryf[c][:],
                                      in_=self.scr["kq"][D + c * 128:D + (c + 1) * 128, tl]),
                          sKQ, 16)
                for c in range(NCH):
                    if c >= 2:
                        p.wait(s, sC, c - 1)
                    p.inc(s.dma_start(out=val[c % 2][:],
                                      in_=self.scr["P"][c * 128:(c + 1) * 128, tl]),
                          sL[c % 2], 16)
            ph.eng("sync", f_sync)

            def f_t(t):
                p.wait(t, "ldw0b")
                p.wait(t, sKQ, 16 * 2 * NCH)
                for c in range(NCH):
                    if c >= 1:
                        p.wait(t, sE, 2 * c)
                    for k in range(NCH):
                        ins = t.matmul(ppk[:], wkp[k][:, c * 128:(c + 1) * 128],
                                       keyf[k][:], start=(k == 0), stop=(k == NCH - 1))
                    p.inc(ins, sM, 1)
                    for k in range(NCH):
                        ins = t.matmul(ppq[:], wqp[k][:, c * 128:(c + 1) * 128],
                                       qryf[k][:], start=(k == 0), stop=(k == NCH - 1))
                    p.inc(ins, sM, 1)
                for c in range(NCH):
                    p.wait(t, sC, c + 1)
                    ins = t.matmul(pst_s[0:1, :], self.onesf[:], reti[c][:],
                                   start=(c == 0), stop=(c == NCH - 1))
                    p.wait(t, sQ2, c + 1)
                    ins = t.matmul(pst_q[0:1, :], self.onesf[:], sqb[c % 2][:],
                                   start=(c == 0), stop=(c == NCH - 1))
                    p.inc(ins, sMs, 1)
                p.wait(t, sR, 1)
                ins = t.matmul(pab[0][:], self.onesr[:], arow, start=True, stop=True)
                ins2 = t.matmul(pab[1][:], self.onesr[:], brow, start=True, stop=True)
                p.inc(ins2, sB, 1)
                for c in range(NCH):
                    p.wait(t, sA, NCH)
                    if c >= 2:
                        p.wait(t, sF, c - 1)
                    for k in range(NCH):
                        ins = t.matmul(ppo[c % 2][:], wo[k][:, c * 128:(c + 1) * 128],
                                       lno[k][:], start=(k == 0), stop=(k == NCH - 1))
                    p.inc(ins, sO, 1)
            ph.eng("tensor", f_t)

            def f_sc(s):
                for c in range(NCH):
                    p.wait(s, sM, 2 * c + 1)
                    if c >= 1:
                        p.wait(s, sC, c)
                    s.activation(kp[:], ppk[:], Act.Tanh, bias=C["bkp"][:, c:c + 1])
                    p.inc(s.drain(), sE, 1)
                    s.activation(skp[:], kp[:], Act.Sin, scale=C["ps_col"][:, c:c + 1], bias=self.zcol[:, 0:1])
                    s.drain()
                    s.activation(ckp[:], kp[:], Act.Abs)
                    s.drain()
                    s.activation(kp[:], ckp[:], Act.Sin,
                                 scale=self.negps[:, c:c + 1], bias=self.hpicol[:, 0:1])
                    p.inc(s.drain(), sKP, 1)
                    p.wait(s, sM, 2 * c + 2)
                    s.activation(qp[:], ppq[:], Act.Tanh, bias=C["bqp"][:, c:c + 1])
                    p.inc(s.drain(), sE, 1)
                    s.activation(sqp[:], qp[:], Act.Sin, scale=C["ps_col"][:, c:c + 1], bias=self.zcol[:, 0:1])
                    s.drain()
                    s.activation(cqp[:], qp[:], Act.Abs)
                    s.drain()
                    s.activation(qp[:], cqp[:], Act.Sin,
                                 scale=self.negps[:, c:c + 1], bias=self.hpicol[:, 0:1])
                    p.inc(s.drain(), sQP, 1)
                for c in range(NCH):
                    p.wait(s, sC, c + 1)
                    if c >= 2:
                        p.wait(s, sMs, c - 1)
                    p.inc(s.activation(sqb[c % 2][:], reti[c][:], Act.Square), sQ2, 1)
                p.wait(s, sW, 1)
                p.inc(s.activation(arow, rtmp, Act.Sqrt), sS, 1)
                for c in range(NCH):
                    p.wait(s, sO, c + 1)
                    if c >= 2:
                        p.wait(s, sX, c - 1)
                    p.inc(s.activation(tres[c % 2][:], ppo[c % 2][:], Act.Identity,
                                       scale=self.modfm[0][:, 16 + c:17 + c],
                                       bias=gabo[:, c:c + 1]), sF, 1)
            ph.eng("scalar", f_sc)

            def f_v(v):
                for c in range(NCH):
                    p.wait(v, sKP, c + 1)
                    p.wait(v, sL[c % 2], 16 * (c // 2 + 1))
                    v.tensor_tensor(ckp[:], val[c % 2][:], kp[:], Alu.mult)
                    v.drain()
                    v.tensor_tensor(skp[:], val[c % 2][:], skp[:], Alu.mult)
                    v.drain()
                    for src, dstb in ((ckp, mr), (skp, mi)):
                        v.tensor_tensor_scan(scn[:], src[:], src[:], 0.0,
                                             Alu.add, Alu.bypass)
                        v.drain()
                        v.tensor_copy(dstb[:, 0:64], scn[:, 0:64])
                        s3 = scn.ap().rearrange("pp (a b) -> pp a b", b=64)
                        bnd = s3[:, 0:7, 63:64].broadcast_to((128, 7, 64))
                        v.drain()
                        v.tensor_tensor(
                            dstb.ap().rearrange("pp (a b) -> pp a b", b=64)[:, 1:8, :],
                            s3[:, 1:8, :], bnd, Alu.subtract)
                        v.drain()
                    p.wait(v, sQP, c + 1)
                    v.tensor_tensor(mr[:], mr[:], qp[:], Alu.mult)
                    v.drain()
                    v.tensor_tensor(mi[:], mi[:], sqp[:], Alu.mult)
                    v.drain()
                    ins = v.tensor_tensor(reti[c][:], mr[:], mi[:], Alu.add)
                    v.drain()
                    p.inc(ins, sC, 1)
                p.wait(v, sMs, NCH)
                es = 1.0 / 32.0
                v.tensor_scalar_mul(rmean, pst_s[0:1, :], es / D)
                v.tensor_scalar_mul(rtmp, pst_q[0:1, :], es * es / D)
                v.drain()
                v.tensor_tensor(arow, rmean, rmean, Alu.mult)
                v.drain()
                v.tensor_tensor(rtmp, rtmp, arow, Alu.subtract)
                v.drain()
                v.tensor_scalar_add(rtmp, rtmp, EPS)
                v.drain()
                v.reciprocal(rtmp, rtmp)
                p.inc(v.drain(), sW, 1)
                p.wait(v, sS, 1)
                v.tensor_tensor(brow, rmean, arow, Alu.mult)
                v.drain()
                v.tensor_scalar_mul(brow, brow, -1.0)
                v.tensor_scalar_mul(arow, arow, es)
                v.drain()
                p.inc(v.drain(), sR, 1)
                p.wait(v, sB, 1)
                for c in range(NCH):
                    v.tensor_tensor(sqb[0][:], reti[c][:], pab[0][:], Alu.mult)
                    v.drain()
                    v.tensor_tensor(sqb[0][:], sqb[0][:], pab[1][:], Alu.add)
                    v.drain()
                    ins = v.tensor_scalar(lno[c][:], sqb[0][:],
                                          C["ln_g"][:, c:c + 1], C["ln_b"][:, c:c + 1],
                                          Alu.mult, Alu.add)
                    v.drain()
                    p.inc(ins, sA, 1)
                for c in range(NCH):
                    p.wait(v, sF, c + 1)
                    ins = v.tensor_tensor(self.xres[c][:, tl], self.xres[c][:, tl],
                                          tres[c % 2][:], Alu.add)
                    v.drain()
                    p.inc(ins, sX, 1)
            ph.eng("vector", f_v)

    def emit_armax(self, tag, cols, ccpair, res):
        p = self.p
        cc_in, cc_out = ccpair
        n = len(cols)
        with ExitStack() as sc, p.phase(tag) as ph:
            gbuf = p.sb(tag + "_g", [8, 128], F32, sc)
            gcol = p.sb(tag + "_c", [8, 1], F32, sc)
            zrow = p.sb(tag + "_z", [1, 8], F32, sc)
            sD, sZ, sV, sG = (tag + x for x in "DZVG")

            def f_v(v):
                v.memset(zrow[:], 0.0)
                p.inc(v.drain(), sZ, 1)
                p.wait(v, sD, 16)
                ins = v.tensor_reduce(gcol[0:n, :], gbuf[0:n, :], AX.X, Alu.max)
                v.drain()
                p.inc(ins, sV, 1)
            ph.eng("vector", f_v)

            def f_sync(s):
                p.wait(s, sZ, 1)
                p.inc(s.dma_start(out=cc_in, in_=zrow[:]), sD + "z", 16)
                for i, col in enumerate(cols):
                    p.inc(s.dma_start(out=self.scr["row"][32 + i:33 + i, 0:128],
                                      in_=col), sD + "a", 16)
                p.wait(s, sD + "a", p.val(sD + "a"))
                p.inc(s.dma_start(out=gbuf[0:n, :],
                                  in_=self.scr["row"][32:32 + n, 0:128]), sD, 16)
                p.wait(s, sV, 1)
                p.wait(s, sD + "z", 16)
                p.inc(s.dma_start(out=cc_in[0:1, 0:n], in_=gcol[0:n, :]), sD + "i", 16)
                p.wait(s, sG, 1)
                p.inc(s.dma_start(out=res[:], in_=cc_out), sD + "o", 16)
                p.wait(s, sD + "o", 16)
            ph.eng("sync", f_sync)

            def f_g(g):
                p.wait(g, sD + "i", 16)
                p.inc(g.collective_compute(
                    "AllReduce", Alu.max,
                    replica_groups=[list(range(B))],
                    ins=[cc_in], outs=[cc_out]), sG, 1)
            ph.eng("gpsimd", f_g)
        return res

    def bcast_cols(self, tag, pairs):
        p = self.p
        with ExitStack() as sc, p.phase(tag) as ph:
            pp = p.ps(tag + "_ps", [128, max(1, len(pairs))], F32, sc)
            sM = tag + "M"

            def f_t(t):
                for i, (r, _) in enumerate(pairs):
                    ins = t.matmul(pp[:, i:i + 1], self.onesr[:], r,
                                   start=(i == 0), stop=(i == len(pairs) - 1))
                p.inc(ins, sM, 1)
            ph.eng("tensor", f_t)

            def f_sc(s):
                p.wait(s, sM, 1)
                for i, (_, oc) in enumerate(pairs):
                    s.copy(oc, pp[:, i:i + 1])
                s.drain()
            ph.eng("scalar", f_sc)

    def l0_ffn(self, scf):
        p, dI = self.p, self.dI
        sca = p.sb("ffn_sca", [1, 16], F32, scf)
        arres = [p.sb(f"ffn_arr{i}", [1, 8], F32, scf) for i in range(4)]
        xq = [p.sb(f"ffn_xq{c}", [128, T], BF16, scf) for c in range(NCH)]
        mxh = p.sb("ffn_mxh", [128, 1], F32, scf)
        rxc = p.sb("ffn_rxc", [128, 1], F32, scf)

        with ExitStack() as sch:
            h2 = [p.sb(f"h2_{c}", [128, T], F32, sch) for c in range(NCH)]
            self.ln_mod(dst=h2, ca=self.modfm[0][:, 32:40],
                        sa=self.modfm[0][:, 24:32], tag="ln2")
            self.tap("ln2", [h.ap() for h in h2])
            if self.stop("ln2"):
                return True

            with ExitStack() as sc, p.phase("ffn_mh") as ph:
                tcol = p.sb("ffn_mh_t", [128, 1], F32, sc)

                def f_v(v):
                    for c in range(NCH):
                        dst = mxh[:] if c == 0 else tcol[:]
                        v.tensor_reduce(dst, h2[c][:], AX.X, Alu.max,
                                        apply_absolute_value=True)
                        v.drain()
                        if c:
                            v.tensor_tensor(mxh[:], mxh[:], tcol[:], Alu.max)
                            v.drain()
                ph.eng("vector", f_v)

            ar1 = self.emit_armax("ffn_ar1", [mxh[:]], self.cc[0], arres[0])
            with p.phase("ffn_s1") as ph:
                def f_v(v):
                    v.tensor_scalar_max(sca[0:1, 0:1], ar1[0:1, 0:1], 32767.0e-12)
                    v.drain()
                    v.reciprocal(sca[0:1, 1:2], sca[0:1, 0:1])
                    v.drain()
                    v.tensor_scalar_mul(sca[0:1, 1:2], sca[0:1, 1:2], 32767.0)
                    v.drain()
                ph.eng("vector", f_v)
            self.bcast_cols("ffn_b1", [(sca[0:1, 1:2], rxc[:])])

            with ExitStack() as sc, p.phase("ffn_xq") as ph:
                ui = [p.sb(f"ffn_xq_i{j}", [128, T], I32, sc) for j in range(2)]
                sA, sK = "xqA", "xqK"

                def f_sc(s2):
                    for c in range(NCH):
                        if c >= 2:
                            p.wait(s2, sK, c - 1)
                        p.inc(s2.activation(ui[c % 2][:], h2[c][:],
                                            Act.Identity,
                                            scale=rxc[:, 0:1]), sA, 1)
                ph.eng("scalar", f_sc)

                def f_v(v):
                    for c in range(NCH):
                        p.wait(v, sA, c + 1)
                        ins = v.tensor_scalar(xq[c][:], ui[c % 2][:],
                                              -32768.0, 32767.0,
                                              Alu.max, Alu.min)
                        v.drain()
                        p.inc(ins, sK, 1)
                ph.eng("vector", f_v)
        self.tap("ffnxq", [x.ap() for x in xq])
        if self.stop("ffnxq"):
            return True

        mxu = p.sb("ffn_mxu", [128, 1], F32, scf)
        mxg = p.sb("ffn_mxg", [128, 1], F32, scf)
        self._ffn_passA(xq, "w_up", self.scr["P"], mxu, "fAu")
        self.tap_dram("fAuP", self.scr["P"], DI)
        self._ffn_passA(xq, "w_gate", self.scr["G"], mxg, "fAg")
        self.tap_dram("fAgG", self.scr["G"], DI)
        ar2 = self.emit_armax("ffn_ar2", [mxu[:], mxg[:]], self.cc[1], arres[1])

        with p.phase("ffn_s2") as ph:
            def f_v(v):
                v.tensor_scalar_max(sca[0:1, 2:4], ar2[0:1, 0:2], 1e-30)
                v.drain()
                v.reciprocal(sca[0:1, 4:6], sca[0:1, 2:4])
                v.drain()
                v.tensor_scalar_mul(sca[0:1, 4:6], sca[0:1, 4:6], 32767.0)
                v.drain()
                v.tensor_tensor(sca[0:1, 6:7], sca[0:1, 0:1], sca[0:1, 3:4],
                                Alu.mult)
                v.drain()
                v.tensor_scalar_mul(sca[0:1, 6:7], sca[0:1, 6:7],
                                    1.0 / (32767.0 * 32767.0))
                v.drain()
            ph.eng("vector", f_v)
        kuc = p.sb("ffn_kuc", [128, 1], F32, scf)
        kgc = p.sb("ffn_kgc", [128, 1], F32, scf)
        sgc = p.sb("ffn_sgc", [128, 1], F32, scf)
        self.bcast_cols("ffn_b2", [(sca[0:1, 4:5], kuc[:]),
                                   (sca[0:1, 5:6], kgc[:]),
                                   (sca[0:1, 6:7], sgc[:])])

        mx2 = p.sb("ffn_mx2", [128, 1], F32, scf)
        self._ffn_passB(kuc, kgc, sgc, mx2)
        self.tap_dram("fBUP", self.scr["UP"], DI)
        self.tap_dram("fBG2", self.scr["G"], DI)
        ar3 = self.emit_armax("ffn_ar3", [mx2[:]], self.cc[2], arres[2])
        with p.phase("ffn_s3") as ph:
            def f_v(v):
                v.tensor_scalar_max(sca[0:1, 7:8], ar3[0:1, 0:1], 1e-30)
                v.drain()
                v.reciprocal(sca[0:1, 8:9], sca[0:1, 7:8])
                v.drain()
                v.tensor_scalar_mul(sca[0:1, 8:9], sca[0:1, 8:9], 32767.0)
                v.drain()
            ph.eng("vector", f_v)
        k2c = p.sb("ffn_k2c", [128, 1], F32, scf)
        self.bcast_cols("ffn_b3", [(sca[0:1, 8:9], k2c[:])])

        mxo = p.sb("ffn_mxo", [128, 1], F32, scf)
        self._ffn_passC(k2c, mxo)
        self.tap_dram("fCO", self.scr["P"], DI)
        ar4 = self.emit_armax("ffn_ar4", [mxo[:]], self.cc[3], arres[3])
        with p.phase("ffn_s4") as ph:
            def f_v(v):
                v.tensor_scalar_max(sca[0:1, 9:10], ar4[0:1, 0:1], 1e-30)
                v.drain()
                v.reciprocal(sca[0:1, 10:11], sca[0:1, 9:10])
                v.drain()
                v.tensor_scalar_mul(sca[0:1, 10:11], sca[0:1, 10:11], 32767.0)
                v.drain()
                v.tensor_tensor(sca[0:1, 11:12], sca[0:1, 6:7], sca[0:1, 7:8],
                                Alu.mult)
                v.drain()
                v.tensor_scalar_mul(sca[0:1, 11:12], sca[0:1, 11:12],
                                    1.0 / (65536.0 * 32767.0))
                v.drain()
                v.tensor_tensor(sca[0:1, 12:13], sca[0:1, 0:1], sca[0:1, 2:3],
                                Alu.mult)
                v.drain()
                v.tensor_scalar_mul(sca[0:1, 12:13], sca[0:1, 12:13],
                                    1.0 / (32767.0 * 32767.0))
                v.drain()
                v.tensor_tensor(sca[0:1, 11:12], sca[0:1, 11:12],
                                sca[0:1, 12:13], Alu.mult)
                v.drain()
                v.tensor_tensor(sca[0:1, 11:12], sca[0:1, 11:12],
                                sca[0:1, 9:10], Alu.mult)
                v.drain()
                v.tensor_scalar_mul(sca[0:1, 11:12], sca[0:1, 11:12],
                                    1.0 / 32767.0)
                v.drain()
            ph.eng("vector", f_v)
        koc = p.sb("ffn_koc", [128, 1], F32, scf)
        soc = p.sb("ffn_soc", [128, 1], F32, scf)
        self.bcast_cols("ffn_b4", [(sca[0:1, 10:11], koc[:]),
                                   (sca[0:1, 11:12], soc[:])])
        gbso = p.sb("ffn_gbso", [128, 8], F32, scf)
        with p.phase("ffn_gb") as ph:
            def f_v(v):
                v.tensor_tensor(gbso[:], self.modfm[0][:, 40:48],
                                soc[:, 0:1].broadcast_to((128, 8)), Alu.mult)
                v.drain()
            ph.eng("vector", f_v)

        self._ffn_passD(koc, gbso)
        return False

    def _ffn_passA(self, xq, wname, spill, mxcol, tg):
        p, dI = self.p, self.dI
        ECH = DI // 128
        with ExitStack() as sc, p.phase(tg) as ph:
            wnat = [p.sb(tg + f"wn{j}", [128, D], F32, sc) for j in range(2)]
            srec = p.sb(tg + "srec", [128, 32], F32, sc)
            sblk = p.sb(tg + "sblk", [128, 32], F32, sc)
            ui = p.sb(tg + "ui", [128, D], I32, sc)
            wqb = [p.sb(tg + f"wqb{j}", [128, D], BF16, sc) for j in range(2)]
            wsl = [p.sb(tg + f"wsl{j}", [128, D], BF16, sc) for j in range(2)]
            ptile = [p.sb(tg + f"pt{j}", [128, T], F32, sc) for j in range(2)]
            tcol = p.sb(tg + "tc", [128, 1], F32, sc)
            ptr = p.ps(tg + "ptr", [128, D], BF16, sc)
            pmm = [p.ps(tg + f"pmm{j}", [128, TTS], F32, sc) for j in range(2)]
            sW = [tg + f"W{j}" for j in range(2)]
            sDq, sTr, sWs, sMM, sPE, sRd, sSp = (tg + x for x in
                                                 ("q", "t", "s", "m", "e", "r", "p"))

            def f_g(g):
                for e in range(ECH):
                    if e >= 2:
                        p.wait(g, sDq, e - 1)
                    p.inc(g.dma_start(out=wnat[e % 2][:],
                                      in_=dI[wname][e * 128:(e + 1) * 128, :]),
                          sW[e % 2], 16)
            ph.eng("gpsimd", f_g)

            def f_v(v):
                for e in range(ECH):
                    p.wait(v, sW[e % 2], 16 * (e // 2 + 1))
                    if e >= 2:
                        p.wait(v, sTr, 8 * (e - 1))
                    w3 = wnat[e % 2].ap().rearrange("pp (a b) -> pp a b", b=32)
                    v.tensor_reduce(sblk[:], w3, AX.X, Alu.max,
                                    apply_absolute_value=True)
                    v.drain()
                    v.tensor_scalar(sblk[:], sblk[:], 7.0e-12, 1.0 / 7.0,
                                    Alu.max, Alu.mult)
                    v.drain()
                    v.reciprocal(srec[:], sblk[:])
                    v.drain()
                    v.tensor_tensor(
                        w3, w3,
                        srec.ap().rearrange("pp a -> pp a ()").broadcast_to((128, 32, 32)),
                        Alu.mult)
                    v.drain()
                    v.tensor_copy(ui[:], wnat[e % 2][:])
                    v.drain()
                    v.tensor_scalar(wnat[e % 2][:], ui[:], -8.0, 7.0,
                                    Alu.max, Alu.min)
                    v.drain()
                    ins = v.tensor_tensor(
                        wqb[e % 2].ap().rearrange("pp (a b) -> pp a b", b=32),
                        w3,
                        sblk.ap().rearrange("pp a -> pp a ()").broadcast_to((128, 32, 32)),
                        Alu.mult)
                    v.drain()
                    p.inc(ins, sDq, 1)
                    p.wait(v, sPE, 4 * e + 4)
                    v.tensor_reduce(tcol[:], ptile[e % 2][:], AX.X, Alu.max,
                                    apply_absolute_value=True)
                    v.drain()
                    if e == 0:
                        ins = v.tensor_copy(mxcol[:], tcol[:])
                    else:
                        ins = v.tensor_tensor(mxcol[:], mxcol[:], tcol[:], Alu.max)
                    v.drain()
                    p.inc(ins, sRd, 1)
            ph.eng("vector", f_v)

            def f_t(t):
                p.wait(t, "ldc")
                for e in range(ECH):
                    p.wait(t, sDq, e + 1)
                    p.wait(t, sWs, 8 * e)
                    for k in range(NCH):
                        p.inc(t.transpose(ptr[:, k * 128:(k + 1) * 128],
                                          wqb[e % 2][:, k * 128:(k + 1) * 128],
                                          self.identb[:]), sTr, 1)
                    p.wait(t, sWs, 8 * e + 8)
                    for n in range(NT):
                        if e * NT + n >= 2:
                            p.wait(t, sPE, (e * NT + n) - 1)
                        for k in range(NCH):
                            ins = t.matmul(pmm[n % 2][:],
                                           wsl[e % 2][:, k * 128:(k + 1) * 128],
                                           xq[k][:, n * TTS:(n + 1) * TTS],
                                           start=(k == 0), stop=(k == NCH - 1))
                        p.inc(ins, sMM, 1)
            ph.eng("tensor", f_t)

            def f_sc(s):
                for e in range(ECH):
                    p.wait(s, sTr, 8 * e + 8)
                    for k in range(NCH):
                        p.inc(s.copy(wsl[e % 2][:, k * 128:(k + 1) * 128],
                                     ptr[:, k * 128:(k + 1) * 128]), sWs, 1)
                    for n in range(NT):
                        p.wait(s, sMM, e * NT + n + 1)
                        if e >= 2 and n == 0:
                            p.wait(s, sSp, 16 * (e - 1))
                        p.inc(s.copy(ptile[e % 2][:, n * TTS:(n + 1) * TTS],
                                     pmm[n % 2][:]), sPE, 1)
            ph.eng("scalar", f_sc)

            def f_sync(s):
                for e in range(ECH):
                    p.wait(s, sRd, e + 1)
                    p.inc(s.dma_start(out=spill[e * 128:(e + 1) * 128, :],
                                      in_=ptile[e % 2][:]), sSp, 16)
                p.wait(s, sSp, 16 * ECH)
            ph.eng("sync", f_sync)

    def _ffn_passB(self, kuc, kgc, sgc, mx2):
        p = self.p
        ECH = DI // 128
        with ExitStack() as sc, p.phase("fB") as ph:
            pt = [p.sb(f"fBp{j}", [128, T], F32, sc) for j in range(2)]
            gt = [p.sb(f"fBg{j}", [128, T], F32, sc) for j in range(2)]
            u = p.sb("fBu", [128, T], F32, sc)
            ui = p.sb("fBi", [128, T], I32, sc)
            uiA = p.sb("fBiA", [128, T], I32, sc)
            uiB = p.sb("fBiB", [128, T], I32, sc)
            upi = [p.sb(f"fBup{j}", [128, T], mybir.dt.int16, sc) for j in range(2)]
            sig = p.sb("fBsig", [128, T], F32, sc)
            tcol = p.sb("fBtc", [128, 1], F32, sc)
            sLp = [f"fBLp{j}" for j in range(2)]
            sLg = [f"fBLg{j}" for j in range(2)]
            sV, sS, sSg = "fBV", "fBS", "fBSg"
            sA1, sA2, sVa, sVb = "fBA1", "fBA2", "fBva", "fBvb"
            sSpU, sSpG = "fBSpU", "fBSpG"

            def f_sync(s):
                for e in range(ECH + 2):
                    if e >= 2:
                        p.wait(s, sV, e - 1)
                        p.inc(s.dma_start(
                            out=self.scr["UP"][(e - 2) * 128:(e - 1) * 128, :],
                            in_=upi[e % 2][:]), sSpU, 16)
                        p.inc(s.dma_start(
                            out=self.scr["G"][(e - 2) * 128:(e - 1) * 128, :],
                            in_=gt[e % 2][:]), sSpG, 16)
                    if e < ECH:
                        if e >= 2:
                            p.wait(s, sSpG, 16 * (e - 1))
                        p.inc(s.dma_start(
                            out=pt[e % 2][:],
                            in_=self.scr["P"][e * 128:(e + 1) * 128, :]),
                            sLp[e % 2], 16)
                        p.inc(s.dma_start(
                            out=gt[e % 2][:],
                            in_=self.scr["G"][e * 128:(e + 1) * 128, :]),
                            sLg[e % 2], 16)
                p.wait(s, sSpU, 16 * ECH)
                p.wait(s, sSpG, 16 * ECH)
            ph.eng("sync", f_sync)

            def f_sc(s):
                for e in range(ECH):
                    p.wait(s, sLp[e % 2], 16 * (e // 2 + 1))
                    if e >= 1:
                        p.wait(s, sVa, e)
                    p.inc(s.activation(uiA[:], pt[e % 2][:], Act.Identity,
                                       scale=kuc[:, 0:1]), sA1, 1)
                    p.wait(s, sLg[e % 2], 16 * (e // 2 + 1))
                    if e >= 1:
                        p.wait(s, sVb, e)
                    p.inc(s.activation(uiB[:], gt[e % 2][:], Act.Identity,
                                       scale=kgc[:, 0:1]), sA2, 1)
                    p.wait(s, sS, e + 1)
                    s.activation(sig[:], gt[e % 2][:], Act.Sigmoid,
                                 scale=sgc[:, 0:1], bias=self.zcol[:, 0:1])
                    p.inc(s.drain(), sSg, 1)
            ph.eng("scalar", f_sc)

            def f_v(v):
                for e in range(ECH):
                    p.wait(v, sA1, e + 1)
                    if e >= 2:
                        p.wait(v, sSpU, 16 * (e - 1))
                    ins = v.tensor_scalar(upi[e % 2][:], uiA[:],
                                          -32768.0, 32767.0,
                                          Alu.max, Alu.min)
                    v.drain()
                    p.inc(ins, sVa, 1)
                    p.wait(v, sA2, e + 1)
                    if e >= 2:
                        p.wait(v, sSpG, 16 * (e - 1))
                    ins = v.tensor_scalar(gt[e % 2][:], uiB[:],
                                          -32768.0, 32767.0,
                                          Alu.max, Alu.min)
                    v.drain()
                    p.inc(ins, sVb, 1)
                    p.inc(v.drain(), sS, 1)
                    p.wait(v, sSg, e + 1)
                    v.tensor_scalar_mul(u[:], sig[:], 65536.0)
                    v.drain()
                    v.tensor_copy(ui[:], u[:])
                    v.drain()
                    v.tensor_scalar(sig[:], ui[:], 0.0, 65535.0, Alu.max, Alu.min)
                    v.drain()
                    v.tensor_tensor(gt[e % 2][:], gt[e % 2][:], sig[:], Alu.mult)
                    v.drain()
                    v.tensor_reduce(tcol[:], gt[e % 2][:], AX.X, Alu.max,
                                    apply_absolute_value=True)
                    v.drain()
                    if e == 0:
                        v.tensor_copy(mx2[:], tcol[:])
                    else:
                        v.tensor_tensor(mx2[:], mx2[:], tcol[:], Alu.max)
                    p.inc(v.drain(), sV, 1)
            ph.eng("vector", f_v)

    def _ffn_passC(self, k2c, mxo):
        p = self.p
        ECH = DI // 128
        with ExitStack() as sc, p.phase("fC") as ph:
            gt = [p.sb(f"fCg{j}", [128, T], F32, sc) for j in range(2)]
            upi = [p.sb(f"fCu{j}", [128, T], mybir.dt.int16, sc) for j in range(2)]
            u = p.sb("fCuu", [128, T], F32, sc)
            ui = p.sb("fCi", [128, T], I32, sc)
            tcol = p.sb("fCtc", [128, 1], F32, sc)
            sLg = [f"fCLg{j}" for j in range(2)]
            sLu = [f"fCLu{j}" for j in range(2)]
            sV, sSp = "fCV", "fCSp"
            sA1, sVa = "fCA1", "fCva"

            def f_sync(s):
                for e in range(ECH + 2):
                    if e >= 2:
                        p.wait(s, sV, e - 1)
                        p.inc(s.dma_start(
                            out=self.scr["P"][(e - 2) * 128:(e - 1) * 128, :],
                            in_=gt[e % 2][:]), sSp, 16)
                    if e < ECH:
                        if e >= 2:
                            p.wait(s, sSp, 16 * (e - 1))
                        p.inc(s.dma_start(
                            out=gt[e % 2][:],
                            in_=self.scr["G"][e * 128:(e + 1) * 128, :]),
                            sLg[e % 2], 16)
                        p.inc(s.dma_start(
                            out=upi[e % 2][:],
                            in_=self.scr["UP"][e * 128:(e + 1) * 128, :]),
                            sLu[e % 2], 16)
                p.wait(s, sSp, 16 * ECH)
            ph.eng("sync", f_sync)

            def f_sc(s):
                for e in range(ECH):
                    p.wait(s, sLg[e % 2], 16 * (e // 2 + 1))
                    if e >= 1:
                        p.wait(s, sVa, e)
                    p.inc(s.activation(ui[:], gt[e % 2][:], Act.Identity,
                                       scale=k2c[:, 0:1]), sA1, 1)
            ph.eng("scalar", f_sc)

            def f_v(v):
                for e in range(ECH):
                    p.wait(v, sA1, e + 1)
                    p.wait(v, sLu[e % 2], 16 * (e // 2 + 1))
                    ins = v.tensor_scalar(u[:], ui[:], -32768.0, 32767.0,
                                          Alu.max, Alu.min)
                    v.drain()
                    p.inc(ins, sVa, 1)
                    if e >= 2:
                        p.wait(v, sSp, 16 * (e - 1))
                    v.tensor_tensor(gt[e % 2][:], u[:], upi[e % 2][:], Alu.mult)
                    v.drain()
                    v.tensor_reduce(tcol[:], gt[e % 2][:], AX.X, Alu.max,
                                    apply_absolute_value=True)
                    v.drain()
                    if e == 0:
                        v.tensor_copy(mxo[:], tcol[:])
                    else:
                        v.tensor_tensor(mxo[:], mxo[:], tcol[:], Alu.max)
                    p.inc(v.drain(), sV, 1)
            ph.eng("vector", f_v)

    def _ffn_passD(self, koc, gbso):
        p, dI = self.p, self.dI
        KCH = DI // 128
        with ExitStack() as scw:
            wdq = [p.sb(f"fDw{c}", [128, 3 * D], BF16, scw) for c in range(NCH)]
            with ExitStack() as sc, p.phase("fDw") as ph:
                wnat = [p.sb(f"fDn{j}", [128, 3 * D], F32, sc) for j in range(2)]
                srec = p.sb("fDsr", [128, 96], F32, sc)
                sblk = p.sb("fDsb", [128, 96], F32, sc)
                ui = p.sb("fDi", [128, 3 * D], I32, sc)
                wqb = [p.sb(f"fDq{j}", [128, 3 * D], BF16, sc) for j in range(2)]
                ptr = p.ps("fDptr", [128, 1024], BF16, sc)
                sW = [f"fDW{j}" for j in range(2)]
                sDq, sTr, sWs = "fDq", "fDt", "fDs"

                def f_g(g):
                    for c in range(NCH):
                        if c >= 2:
                            p.wait(g, sDq, c - 1)
                        p.inc(g.dma_start(out=wnat[c % 2][:],
                                          in_=dI["w_down"][c * 128:(c + 1) * 128, :]),
                              sW[c % 2], 16)
                ph.eng("gpsimd", f_g)

                def f_v(v):
                    for c in range(NCH):
                        p.wait(v, sW[c % 2], 16 * (c // 2 + 1))
                        if c >= 2:
                            p.wait(v, sTr, 24 * (c - 1))
                        w3 = wnat[c % 2].ap().rearrange("pp (a b) -> pp a b", b=32)
                        v.tensor_reduce(sblk[:], w3, AX.X, Alu.max,
                                        apply_absolute_value=True)
                        v.drain()
                        v.tensor_scalar(sblk[:], sblk[:], 7.0e-12, 1.0 / 7.0,
                                        Alu.max, Alu.mult)
                        v.drain()
                        v.reciprocal(srec[:], sblk[:])
                        v.drain()
                        v.tensor_tensor(
                            w3, w3,
                            srec.ap().rearrange("pp a -> pp a ()").broadcast_to((128, 96, 32)),
                            Alu.mult)
                        v.drain()
                        v.tensor_copy(ui[:], wnat[c % 2][:])
                        v.drain()
                        v.tensor_scalar(wnat[c % 2][:], ui[:], -8.0, 7.0,
                                        Alu.max, Alu.min)
                        v.drain()
                        ins = v.tensor_tensor(
                            wqb[c % 2].ap().rearrange("pp (a b) -> pp a b", b=32),
                            w3,
                            sblk.ap().rearrange("pp a -> pp a ()").broadcast_to((128, 96, 32)),
                            Alu.mult)
                        v.drain()
                        p.inc(ins, sDq, 1)
                ph.eng("vector", f_v)

                def f_t(t):
                    p.wait(t, "ldc")
                    for c in range(NCH):
                        p.wait(t, sDq, c + 1)
                        for g8 in range(3):
                            p.wait(t, sWs, (c * 3 + g8) * 8)
                            for k8 in range(8):
                                k = g8 * 8 + k8
                                p.inc(t.transpose(
                                    ptr[:, k8 * 128:(k8 + 1) * 128],
                                    wqb[c % 2][:, k * 128:(k + 1) * 128],
                                    self.identb[:]), sTr, 1)
                ph.eng("tensor", f_t)

                def f_sc(s):
                    for c in range(NCH):
                        for g8 in range(3):
                            p.wait(s, sTr, (c * 3 + g8) * 8 + 8)
                            for k8 in range(8):
                                k = g8 * 8 + k8
                                p.inc(s.copy(wdq[c][:, k * 128:(k + 1) * 128],
                                             ptr[:, k8 * 128:(k8 + 1) * 128]),
                                      sWs, 1)
                ph.eng("scalar", f_sc)

            with ExitStack() as sc, p.phase("fD") as ph:
                ot = [p.sb(f"fDo{j}", [128, TTS], F32, sc) for j in range(2)]
                obf = [p.sb(f"fDb{k}", [128, TTS], BF16, sc) for k in range(KCH)]
                u2 = p.sb("fDu2", [128, TTS], F32, sc)
                ui2 = p.sb("fDi2", [128, TTS], I32, sc)
                tres = [p.sb(f"fDt{j}", [128, TTS], F32, sc) for j in range(2)]
                pdo = [p.ps(f"fDp{j}", [128, TTS], F32, sc) for j in range(8)]
                sL = [f"fDL{j}" for j in range(2)]
                sQ, sMM, sEv, sX = "fDQ", "fDM", "fDE", "fDX"

                def f_sync(s):
                    for n in range(NT):
                        for k in range(KCH):
                            i = n * KCH + k
                            if i >= 2:
                                p.wait(s, sQ, i - 1)
                            p.inc(s.dma_start(
                                out=ot[i % 2][:],
                                in_=self.scr["P"][k * 128:(k + 1) * 128,
                                                  n * TTS:(n + 1) * TTS]),
                                sL[i % 2], 16)
                ph.eng("sync", f_sync)

                def f_v(v):
                    for n in range(NT):
                        for k in range(KCH):
                            i = n * KCH + k
                            p.wait(v, sL[i % 2], 16 * (i // 2 + 1))
                            if n >= 1 and k == 0:
                                p.wait(v, sMM, 8 * n)
                            v.tensor_scalar_mul(u2[:], ot[i % 2][:], koc[:, 0:1])
                            v.drain()
                            v.tensor_copy(ui2[:], u2[:])
                            v.drain()
                            ins = v.tensor_scalar(obf[k][:], ui2[:],
                                                  -32768.0, 32767.0,
                                                  Alu.max, Alu.min)
                            v.drain()
                            p.inc(ins, sQ, 1)
                        for c in range(NCH):
                            p.wait(v, sEv, n * NCH + c + 1)
                            ins = v.tensor_tensor(
                                self.xres[c][:, n * TTS:(n + 1) * TTS],
                                self.xres[c][:, n * TTS:(n + 1) * TTS],
                                tres[c % 2][:], Alu.add)
                            v.drain()
                            p.inc(ins, sX, 1)
                ph.eng("vector", f_v)

                def f_t(t):
                    for n in range(NT):
                        p.wait(t, sQ, (n + 1) * KCH)
                        for c in range(NCH):
                            if n >= 1:
                                p.wait(t, sEv, (n - 1) * NCH + c + 1)
                            for k in range(KCH):
                                ins = t.matmul(pdo[c][:],
                                               wdq[c][:, k * 128:(k + 1) * 128],
                                               obf[k][:],
                                               start=(k == 0), stop=(k == KCH - 1))
                            p.inc(ins, sMM, 1)
                ph.eng("tensor", f_t)

                def f_sc(s):
                    for n in range(NT):
                        for c in range(NCH):
                            p.wait(s, sMM, n * NCH + c + 1)
                            if n * NCH + c >= 2:
                                p.wait(s, sX, n * NCH + c - 1)
                            p.inc(s.activation(tres[c % 2][:], pdo[c][:],
                                               Act.Identity,
                                               scale=gbso[:, c:c + 1],
                                               bias=self.zcol[:, 0:1]), sEv, 1)
                ph.eng("scalar", f_sc)


    def l1_attn(self, hb, sc0):
        p, dI = self.p, self.dI
        with ExitStack() as sc, p.phase("l1qk") as ph:
            qw = [p.sb(f"l1qw{c}", [128, D], BF16, sc) for c in range(NCH)]
            kw = [p.sb(f"l1kw{c}", [128, D], BF16, sc) for c in range(NCH)]
            rcos = p.sb("l1rc", [128, T], F32, sc)
            rsin = p.sb("l1rs", [128, T], F32, sc)
            kb8 = p.sb("l1kb8", [128, 8], F32, sc)
            qt = p.sb("l1qt", [128, TTS], F32, sc)
            kt = p.sb("l1kt", [128, TTS], F32, sc)
            sh = p.sb("l1sh", [128, TTS], F32, sc)
            t1 = p.sb("l1t1", [128, TTS], F32, sc)
            qtl = [p.sb(f"l1qtl{j}", [128, TTS], BF16, sc) for j in range(2)]
            ktl = [p.sb(f"l1ktl{j}", [128, TTS], BF16, sc) for j in range(2)]
            ppq = p.ps("l1ppq", [128, TTS], F32, sc)
            ppk = p.ps("l1ppk", [128, TTS], F32, sc)
            sW, sM, sE, sV, sB8 = "l1qkW", "l1qkM", "l1qkE", "l1qkV", "l1qkB"
            sQs = [f"l1qkQ{j}" for j in range(2)]
            sKs = [f"l1qkK{j}" for j in range(2)]

            def f_g(g):
                for c in range(NCH):
                    p.inc(g.dma_start(out=qw[c][:],
                                      in_=dI["q_w"][c * 128:(c + 1) * 128, :]), sW, 16)
                    p.inc(g.dma_start(out=kw[c][:],
                                      in_=dI["k_w"][c * 128:(c + 1) * 128, :]), sW, 16)
                p.inc(g.dma_start(out=rcos[:], in_=dI["rot_cos"]), sW, 16)
                p.inc(g.dma_start(out=rsin[:], in_=dI["rot_sin"]), sW, 16)
            ph.eng("gpsimd", f_g)

            def f_v(v):
                p.wait(v, "ldc")
                v.tensor_scalar_mul(kb8[:], self.cols["k_b"][:], 0.125)
                p.inc(v.drain(), sB8, 1)
                for i in range(NCH * NT):
                    c, n = divmod(i, NT)
                    tl = slice(n * TTS, (n + 1) * TTS)
                    p.wait(v, sE, 2 * i + 2)
                    if i >= 2:
                        p.wait(v, sQs[i % 2], 16 * ((i - 2) // 2 + 1))
                        p.wait(v, sKs[i % 2], 16 * ((i - 2) // 2 + 1))
                    for src_, dst in ((qt, qtl), (kt, ktl)):
                        v.stream_shuffle(sh[:], src_[:], [j ^ 1 for j in range(32)])
                        v.drain()
                        v.tensor_tensor(t1[:], src_[:], rcos[:, tl], Alu.mult)
                        v.drain()
                        v.tensor_tensor(sh[:], sh[:], rsin[:, tl], Alu.mult)
                        v.drain()
                        v.tensor_tensor(dst[i % 2][:], t1[:], sh[:], Alu.add)
                        v.drain()
                    p.inc(v.drain(), sV, 1)
            ph.eng("vector", f_v)

            def f_sync(s):
                for i in range(NCH * NT):
                    c, n = divmod(i, NT)
                    tl = slice(n * TTS, (n + 1) * TTS)
                    p.wait(s, sV, i + 1)
                    p.inc(s.dma_start(out=self.scr["kq"][c * 128:(c + 1) * 128, tl],
                                      in_=qtl[i % 2][:]), sQs[i % 2], 16)
                    p.inc(s.dma_start(
                        out=self.scr["kq"][D + c * 128:D + (c + 1) * 128, tl],
                        in_=ktl[i % 2][:]), sKs[i % 2], 16)
                for j in range(2):
                    p.wait(s, sQs[j])
                    p.wait(s, sKs[j])
            ph.eng("sync", f_sync)

            def f_t(t):
                p.wait(t, sW)
                for i in range(NCH * NT):
                    c, n = divmod(i, NT)
                    if i >= 1:
                        p.wait(t, sE, 2 * i)
                    for k in range(NCH):
                        ins = t.matmul(ppq[:], qw[k][:, c * 128:(c + 1) * 128],
                                       hb[k][:, n * TTS:(n + 1) * TTS],
                                       start=(k == 0), stop=(k == NCH - 1))
                    p.inc(ins, sM, 1)
                    for k in range(NCH):
                        ins = t.matmul(ppk[:], kw[k][:, c * 128:(c + 1) * 128],
                                       hb[k][:, n * TTS:(n + 1) * TTS],
                                       start=(k == 0), stop=(k == NCH - 1))
                    p.inc(ins, sM, 1)
            ph.eng("tensor", f_t)

            def f_sc(s):
                p.wait(s, sB8, 1)
                for i in range(NCH * NT):
                    c, n = divmod(i, NT)
                    p.wait(s, sM, 2 * i + 1)
                    if i >= 1:
                        p.wait(s, sV, i)
                    s.activation(qt[:], ppq[:], Act.Identity,
                                 bias=self.cols["q_b"][:, c:c + 1])
                    p.inc(s.drain(), sE, 1)
                    p.wait(s, sM, 2 * i + 2)
                    s.activation(kt[:], ppk[:], Act.Identity, scale=0.125,
                                 bias=kb8[:, c:c + 1])
                    p.inc(s.drain(), sE, 1)
            ph.eng("scalar", f_sc)

        with ExitStack() as sc, p.phase("l1sc") as ph:
            vw = [p.sb(f"l1vw{c}", [128, 2 * D], BF16, sc) for c in range(NCH)]
            dlt = p.sb("l1dlt", [128, NH * CRET], F32, sc)
            alph = p.sb("l1al", [128, NH], F32, sc)
            beta = p.sb("l1be", [128, NH * CRET], F32, sc)
            vbr = p.sb("l1vbr", [1, 2 * D], F32, sc)
            sall = p.sb("l1S", [128, (NH // 2) * CRET], F32, sc)
            zall = p.sb("l1z", [128, NH // 2], F32, sc)
            vtm = [p.sb(f"l1vt{j}", [128, 2 * D], BF16, sc) for j in range(2)]
            qrc = [p.sb(f"l1qrc{j}", [128, D], BF16, sc) for j in range(2)]
            krc = [p.sb(f"l1krc{j}", [128, D], BF16, sc) for j in range(2)]
            smk = p.sb("l1smk", [128, CRET], BF16, sc)
            qrsf = p.sb("l1qrsf", [128, CRET], F32, sc)
            ktm = p.sb("l1ktm", [128, 64], BF16, sc)
            outsb = p.sb("l1out", [128, CRET], F32, sc)
            zst = p.sb("l1zst", [1, NH * CRET], F32, sc)
            ppv = [p.ps("l1pv0", [128, TTS], F32, sc)]
            psT = p.ps("l1psT", [128, CRET], F32, sc)
            psK = p.ps("l1psK", [128, 64], BF16, sc)
            psA = p.ps("l1psA", [128, CRET], F32, sc)
            psB = p.ps("l1psB", [128, CRET], F32, sc)
            psD = p.ps("l1psD", [128, 136], F32, sc)
            psZ1 = p.ps("l1psZ1", [1, CRET], F32, sc)
            psZ2 = p.ps("l1psZ2", [1, CRET], F32, sc)
            (sW, sVM, sVE, sST, sMK, sKT, sKE, sQS, sOA, sOC, sDm, sSU, sZm,
             sZC, sSp, sZD, sZG) = ("l1s" + x for x in
                                    ("W", "VM", "VE", "ST", "MK", "KT", "KE",
                                     "QS", "OA", "OC", "Dm", "SU", "Zm", "ZC",
                                     "Sp", "ZD", "ZG"))
            sQL = [f"l1sQL{j}" for j in range(2)]
            sAE, sZE = "l1sAE", "l1sZE"

            def f_g(g):
                for c in range(NCH):
                    p.inc(g.dma_start(out=vw[c][:],
                                      in_=dI["v_w"][c * 128:(c + 1) * 128, :]), sW, 16)
                p.inc(g.dma_start(out=dlt[:], in_=dI["dlt"]), sW, 16)
                p.inc(g.dma_start(out=alph[:], in_=dI["alpha"]), sW, 16)
                p.inc(g.dma_start(out=beta[:], in_=dI["beta_bc"]), sW, 16)
                p.inc(g.dma_start(out=vbr[:], in_=dI["v_b_row"]), sW, 16)
                g.memset(sall[:], 0.0)
                g.memset(zall[:], 0.0)
                p.inc(g.drain(), sW + "z", 1)
            ph.eng("gpsimd", f_g)

            def f_t(t):
                p.wait(t, sW)
                p.wait(t, sW + "z", 1)
                for i in range(NH):
                    ci = slice(i * CRET, (i + 1) * CRET)
                    p.wait(t, sQL[i % 2], 16 * 16 * (i // 2 + 1))
                    for e in range(4):
                        ie = 4 * i + e
                        if ie >= 1:
                            p.wait(t, sVE, ie)
                        for k in range(NCH):
                            ins = t.matmul(ppv[0][:], hb[k][:, ci],
                                           vw[k][:, e * TTS:(e + 1) * TTS],
                                           start=(k == 0), stop=False)
                        ins = t.matmul(ppv[0][:], self.onesr[:],
                                       vbr[0:1, e * TTS:(e + 1) * TTS],
                                       start=False, stop=True)
                        p.inc(ins, sVM, 1)
                    for h in range(NH):
                        j = i * NH + h
                        hrow = slice((h % 2) * 64, (h % 2) * 64 + 64)
                        hcol = slice(h * CRET, (h + 1) * CRET)
                        if j >= 1:
                            p.wait(t, sMK, j)
                        ins = t.matmul(psT[:], krc[i % 2][hrow, (h // 2) * 128:(h // 2) * 128 + 128],
                                       qrc[i % 2][hrow, (h // 2) * 128:(h // 2) * 128 + 128],
                                       start=True, stop=True)
                        p.inc(ins, sST, 1)
                        if j >= 1:
                            p.wait(t, sKE, j)
                        idsl = self.identb[hrow, hrow]
                        ins = t.transpose(psK[:, 0:64],
                                          krc[i % 2][hrow, (h // 2) * 128:(h // 2) * 128 + 128],
                                          idsl)
                        p.inc(ins, sKT, 1)
                        p.wait(t, sMK, j + 1)
                        p.wait(t, sVE, 4 * i + 4)
                        if j >= 1:
                            p.wait(t, sOC, j)
                        ins = t.matmul(psA[:], vtm[i % 2][:, hcol], smk[:],
                                       start=True, stop=True)
                        p.inc(ins, sOA, 1)
                        p.wait(t, sQS, j + 1)
                        if i >= 1:
                            p.wait(t, sSU, (i - 1) * NH + h + 1)
                        scol = slice((h // 2) * CRET, (h // 2 + 1) * CRET)
                        ins = t.matmul(psB[:], sall[hrow, scol], qrsf[hrow, :],
                                       start=True, stop=True)
                        p.inc(ins, sOA, 1)
                        if j >= 1:
                            p.wait(t, sZC, j)
                        ins = t.matmul(psZ1[0:1, :], self.onesb[:], smk[:],
                                       start=True, stop=True)
                        ins = t.matmul(psZ2[0:1, :],
                                       zall[hrow, h // 2:h // 2 + 1],
                                       qrsf[hrow, :], start=True, stop=True)
                        p.inc(ins, sZm, 1)
                        p.wait(t, sKE, j + 1)
                        if j >= 1:
                            p.wait(t, sSU, j)
                        prow = slice((h % 2) * 64, (h % 2) * 64 + 64)
                        ins = t.matmul(psD[prow, 0:128], ktm[:, 0:64],
                                       vtm[i % 2][:, hcol], start=True, stop=True)
                        ins = t.matmul(psD[prow, 128:129], ktm[:, 0:64],
                                       self.onesb[:], start=True, stop=True)
                        p.inc(ins, sDm, 1)
            ph.eng("tensor", f_t)

            def f_sc(s):
                for i in range(NH):
                    for e in range(4):
                        p.wait(s, sVM, 4 * i + e + 1)
                        if i >= 2 and e == 0:
                            p.wait(s, sDm, (i - 1) * NH)
                        p.inc(s.copy(vtm[i % 2][:, e * TTS:(e + 1) * TTS],
                                     ppv[0][:]), sVE, 1)
                    for h in range(NH):
                        j = i * NH + h
                        hcol2 = slice(h * CRET, (h + 1) * CRET)
                        p.wait(s, sKT, j + 1)
                        p.inc(s.activation(ktm[:, 0:64], psK[:, 0:64], Act.Copy,
                                           scale=alph[:, h:h + 1]), sKE, 1)
                        p.wait(s, sOA, 2 * j + 1)
                        if j >= 1:
                            p.wait(s, sSp, 16 * j)
                        p.inc(s.copy(outsb[:], psA[:]), sAE, 1)
                        p.wait(s, sZm, j + 1)
                        if i >= 1:
                            p.wait(s, sZG, 16 * i)
                        p.inc(s.copy(zst[0:1, hcol2], psZ1[0:1, :]), sZE, 1)
            ph.eng("scalar", f_sc)

            def f_v(v):
                for i in range(NH):
                    ci = slice(i * CRET, (i + 1) * CRET)
                    p.wait(v, sQL[i % 2], 16 * 16 * (i // 2 + 1))
                    for h in range(NH):
                        j = i * NH + h
                        hrow = slice((h % 2) * 64, (h % 2) * 64 + 64)
                        hcol = slice(h * CRET, (h + 1) * CRET)
                        p.wait(v, sST, j + 1)
                        ins = v.tensor_tensor(smk[:], psT[:], dlt[:, hcol],
                                              Alu.mult)
                        v.drain()
                        p.inc(ins, sMK, 1)
                        ins = v.tensor_tensor(
                            qrsf[hrow, :],
                            qrc[i % 2][hrow, (h // 2) * 128:(h // 2) * 128 + 128],
                            beta[hrow, hcol], Alu.mult)
                        v.drain()
                        p.inc(ins, sQS, 1)
                        p.wait(v, sOA, 2 * j + 2)
                        p.wait(v, sAE, j + 1)
                        ins = v.tensor_tensor(outsb[:], outsb[:], psB[:], Alu.add)
                        v.drain()
                        p.inc(ins, sOC, 1)
                        p.wait(v, sZm, j + 1)
                        p.wait(v, sZE, j + 1)
                        ins = v.tensor_tensor(zst[0:1, hcol],
                                              zst[0:1, hcol], psZ2[0:1, :], Alu.add)
                        v.drain()
                        p.inc(ins, sZC, 1)
                        p.wait(v, sDm, j + 1)
                        lam = float(CONSTS["lam_pc"][h])
                        scol = slice((h // 2) * CRET, (h // 2 + 1) * CRET)
                        zcl = slice(h // 2, h // 2 + 1)
                        prow = slice((h % 2) * 64, (h % 2) * 64 + 64)
                        v.tensor_scalar_mul(sall[hrow, scol], sall[hrow, scol], lam)
                        v.tensor_scalar_mul(zall[hrow, zcl], zall[hrow, zcl], lam)
                        v.drain()
                        v.tensor_tensor(sall[hrow, scol], sall[hrow, scol],
                                        psD[prow, 0:128], Alu.add)
                        ins = v.tensor_tensor(zall[hrow, zcl], zall[hrow, zcl],
                                              psD[prow, 128:129], Alu.add)
                        v.drain()
                        p.inc(ins, sSU, 1)
                    p.inc(v.drain(), sZD, 1)
            ph.eng("vector", f_v)

            def f_sync(s):
                def loads(i):
                    ci2 = slice(i * CRET, (i + 1) * CRET)
                    for k in range(NCH):
                        p.inc(s.dma_start(out=qrc[i % 2][:, k * 128:(k + 1) * 128],
                                          in_=self.scr["kq"][k * 128:(k + 1) * 128, ci2]),
                              sQL[i % 2], 16)
                        p.inc(s.dma_start(out=krc[i % 2][:, k * 128:(k + 1) * 128],
                                          in_=self.scr["kq"][D + k * 128:D + (k + 1) * 128, ci2]),
                              sQL[i % 2], 16)
                loads(0)
                loads(1)
                for i in range(NH):
                    for h in range(NH):
                        j = i * NH + h
                        p.wait(s, sOC, j + 1)
                        p.inc(s.dma_start(
                            out=self.scr["O"][h * CRET:(h + 1) * CRET,
                                              i * CRET:(i + 1) * CRET],
                            in_=outsb[:]), sSp, 16)
                    p.wait(s, sZD, i + 1)
                    p.inc(s.dma_start(out=self.scr["row"][40 + i:41 + i, :],
                                      in_=zst[:]), sZG, 16)
                    if i + 2 < NH:
                        p.wait(s, sDm, (i + 1) * NH)
                        loads(i + 2)
                p.wait(s, sSp, 16 * NH * NH)
                p.wait(s, sZG, 16 * NH)
            ph.eng("sync", f_sync)

        sc_st = sc0.enter_context(ExitStack())
        srows = p.sb("l1srows", [NH, T], F32, sc_st)
        qrows = p.sb("l1qrows", [NH, T], F32, sc_st)
        zrows = p.sb("l1zrows", [NH, T], F32, sc_st)

        with p.phase("l1zg") as ph:
            def f_sync(s):
                src = bass.AP(self.scr["row"].tensor, 40 * T,
                              [[128, NH], [T, NH], [1, CRET]])
                p.inc(s.dma_start(out=zrows.ap().rearrange(
                    "hh (ii r) -> hh ii r", r=CRET), in_=src), "l1zgD", 16)
                p.wait(s, "l1zgD", 16)
            ph.eng("sync", f_sync)

        with ExitStack() as sc, p.phase("l1st") as ph:
            ot = [p.sb(f"l1ot{j}", [128, TTS], F32, sc) for j in range(2)]
            sqt = p.sb("l1sqt", [128, TTS], F32, sc)
            ehot = p.sb("l1eh", [128, NH * NH], F32, sc)
            psS = p.ps("l1psS", [NH, TTS], F32, sc)
            psQ = p.ps("l1psQ", [NH, TTS], F32, sc)
            sW, sQ2, sMS, sEV = "l1tW", "l1tQ", "l1tM", "l1tE"
            sLs = [f"l1tL{j}" for j in range(2)]

            def f_g(g):
                p.inc(g.dma_start(out=ehot[:], in_=dI["ehot"]), sW, 16)
            ph.eng("gpsimd", f_g)

            def f_sync(s):
                for n in range(NT):
                    for h in range(NH):
                        i = n * NH + h
                        if i >= 2:
                            p.wait(s, sQ2, i - 1)
                            p.wait(s, sMS, i - 1)
                        p.inc(s.dma_start(
                            out=ot[i % 2][:],
                            in_=self.scr["O"][h * CRET:(h + 1) * CRET,
                                              n * TTS:(n + 1) * TTS]),
                            sLs[i % 2], 16)
            ph.eng("sync", f_sync)

            def f_sc(s):
                for n in range(NT):
                    for h in range(NH):
                        i = n * NH + h
                        p.wait(s, sLs[i % 2], 16 * (i // 2 + 1))
                        if i >= 1:
                            p.wait(s, sMS, i)
                        p.inc(s.activation(sqt[:], ot[i % 2][:], Act.Square),
                              sQ2, 1)
            ph.eng("scalar", f_sc)

            def f_t(t):
                p.wait(t, sW)
                for n in range(NT):
                    if n >= 1:
                        p.wait(t, sEV, n)
                    for h in range(NH):
                        i = n * NH + h
                        p.wait(t, sQ2, i + 1)
                        ins = t.matmul(psS[:], ehot[:, h * NH:(h + 1) * NH],
                                       ot[i % 2][:], start=(h == 0),
                                       stop=(h == NH - 1))
                        ins = t.matmul(psQ[:], ehot[:, h * NH:(h + 1) * NH],
                                       sqt[:], start=(h == 0), stop=(h == NH - 1))
                        p.inc(ins, sMS, 1)
            ph.eng("tensor", f_t)

            def f_v(v):
                for n in range(NT):
                    p.wait(v, sMS, (n + 1) * NH)
                    tl = slice(n * TTS, (n + 1) * TTS)
                    v.tensor_copy(srows[:, tl], psS[:])
                    ins = v.tensor_copy(qrows[:, tl], psQ[:])
                    v.drain()
                    p.inc(ins, sEV, 1)
            ph.eng("vector", f_v)

        with ExitStack() as sc, p.phase("l1rw") as ph:
            rnorm = p.sb("l1rn", [NH, T], F32, sc)
            dd = p.sb("l1dd", [NH, T], F32, sc)
            sig = p.sb("l1sig", [NH, T], F32, sc)
            mu = p.sb("l1mu", [NH, T], F32, sc)
            var = p.sb("l1var", [NH, T], F32, sc)
            aba = p.sb("l1abra", [NH, T], F32, sc)
            abb = p.sb("l1abrb", [NH, T], F32, sc)
            sW, sV1, sS1, sD1 = "l1rW", "l1rV", "l1rS", "l1rD"

            def f_g(g):
                p.inc(g.dma_start(out=rnorm[:], in_=dI["rnorm"]), sW, 16)
            ph.eng("gpsimd", f_g)

            def f_v(v):
                p.wait(v, sW)
                v.tensor_tensor(dd[:], zrows[0:NH, :], rnorm[:], Alu.mult)
                v.drain()
                v.tensor_scalar_mul(sig[:], dd[:], -1.0)
                v.drain()
                v.tensor_tensor(dd[:], dd[:], sig[:], Alu.max)
                v.drain()
                v.tensor_scalar_max(dd[:], dd[:], 1.0)
                v.drain()
                v.reciprocal(dd[:], dd[:])
                v.drain()
                v.tensor_tensor(sig[:], rnorm[:], dd[:], Alu.mult)
                v.drain()
                v.tensor_scalar_mul(mu[:], srows[:], 1.0 / DVH)
                v.tensor_scalar_mul(var[:], qrows[:], 1.0 / DVH)
                v.drain()
                v.tensor_tensor(dd[:], mu[:], mu[:], Alu.mult)
                v.drain()
                v.tensor_tensor(var[:], var[:], dd[:], Alu.subtract)
                v.drain()
                v.tensor_tensor(dd[:], sig[:], sig[:], Alu.mult)
                v.drain()
                v.tensor_tensor(var[:], var[:], dd[:], Alu.mult)
                v.drain()
                v.tensor_scalar_add(var[:], var[:], EPS)
                v.drain()
                v.reciprocal(var[:], var[:])
                p.inc(v.drain(), sV1, 1)
                p.wait(v, sS1, 1)
                v.tensor_tensor(aba[:], sig[:], var[:], Alu.mult)
                v.drain()
                v.tensor_tensor(abb[:], mu[:], aba[:], Alu.mult)
                v.drain()
                v.tensor_scalar_mul(abb[:], abb[:], -1.0)
                v.drain()
                p.inc(v.drain(), sD1, 1)
            ph.eng("vector", f_v)

            def f_sc(s):
                p.wait(s, sV1, 1)
                p.inc(s.activation(var[:], var[:], Act.Sqrt), sS1, 1)
            ph.eng("scalar", f_sc)

            def f_sync(s):
                p.wait(s, sD1, 1)
                p.inc(s.dma_start(out=self.scr["row"][0:NH, :], in_=aba[:]),
                      "l1rSp", 16)
                p.inc(s.dma_start(out=self.scr["row"][NH:2 * NH, :], in_=abb[:]),
                      "l1rSp2", 16)
                p.wait(s, "l1rSp", 16)
                p.wait(s, "l1rSp2", 16)
            ph.eng("sync", f_sync)

        sc_st.close()

        with ExitStack() as sc, p.phase("l1f") as ph:
            gw = [p.sb(f"l1gw{c}", [128, 2 * D], BF16, sc) for c in range(NCH)]
            ow = [p.sb(f"l1ow{c}", [128, D], BF16, sc) for c in range(2 * NCH)]
            gb16 = p.sb("l1gb16", [128, 16], F32, sc)
            gaob = p.sb("l1gaob", [128, 8], F32, sc)
            gsil = p.sb("l1gs", [128, NH * TTS], BF16, sc)
            ot = [p.sb(f"l1fo{j}", [128, TTS], F32, sc) for j in range(2)]
            abca = [p.sb(f"l1aba{j}", [128, TTS], F32, sc) for j in range(2)]
            abcb = [p.sb(f"l1abb{j}", [128, TTS], F32, sc) for j in range(2)]
            fus = p.sb("l1fu", [128, TTS], BF16, sc)
            tre = [p.sb(f"l1tr{j}", [128, TTS], F32, sc) for j in range(2)]
            psO = [p.ps(f"l1pO{c}", [128, TTS], F32, sc) for c in range(8)]
            (sW, sGM, sGE, sFu, sOM, sOE, sXm, sV0) = (
                "l1f" + x for x in ("W", "GM", "GE", "Fu", "OM", "OE", "Xm", "V"))
            sLo = [f"l1fLo{j}" for j in range(2)]
            sLa = [f"l1fLa{j}" for j in range(2)]
            sLb = [f"l1fLb{j}" for j in range(2)]

            def f_g(g):
                for c in range(NCH):
                    p.inc(g.dma_start(out=gw[c][:],
                                      in_=dI["g_w"][c * 128:(c + 1) * 128, :]), sW, 16)
                for c in range(2 * NCH):
                    p.inc(g.dma_start(out=ow[c][:],
                                      in_=dI["o_w"][c * 128:(c + 1) * 128, :]), sW, 16)
                p.inc(g.dma_start(out=gb16[:], in_=dI["g_b"]), sW, 16)
            ph.eng("gpsimd", f_g)

            def f_sync(s):
                for n in range(NT):
                    for h in range(NH):
                        i = n * NH + h
                        if i >= 2:
                            p.wait(s, sFu, i - 1)
                        p.inc(s.dma_start(
                            out=ot[i % 2][:],
                            in_=self.scr["O"][h * CRET:(h + 1) * CRET,
                                              n * TTS:(n + 1) * TTS]),
                            sLo[i % 2], 16)
                        p.inc(s.dma_start(
                            out=abca[i % 2][:],
                            in_=self.scr["row"][h:h + 1, n * TTS:(n + 1) * TTS]
                            .partition_broadcast(128)), sLa[i % 2], 16)
                        p.inc(s.dma_start(
                            out=abcb[i % 2][:],
                            in_=self.scr["row"][NH + h:NH + h + 1,
                                                n * TTS:(n + 1) * TTS]
                            .partition_broadcast(128)), sLb[i % 2], 16)
            ph.eng("sync", f_sync)

            def f_v(v):
                p.wait(v, "ldc")
                v.tensor_tensor(gaob[:], self.modfm[1][:, 16:24],
                                self.cols["o_b"][:], Alu.mult)
                p.inc(v.drain(), sV0, 1)
                for n in range(NT):
                    for h in range(NH):
                        i = n * NH + h
                        p.wait(v, sLo[i % 2], 16 * (i // 2 + 1))
                        p.wait(v, sLa[i % 2], 16 * (i // 2 + 1))
                        p.wait(v, sLb[i % 2], 16 * (i // 2 + 1))
                        p.wait(v, sGE, i + 1)
                        v.tensor_tensor(ot[i % 2][:], ot[i % 2][:],
                                        abca[i % 2][:], Alu.mult)
                        v.drain()
                        v.tensor_tensor(ot[i % 2][:], ot[i % 2][:],
                                        abcb[i % 2][:], Alu.add)
                        v.drain()
                        if i >= 1:
                            p.wait(v, sOM, i)
                        ins = v.tensor_tensor(fus[:], ot[i % 2][:],
                                              gsil[:, h * TTS:(h + 1) * TTS],
                                              Alu.mult)
                        v.drain()
                        p.inc(ins, sFu, 1)
                    for c in range(NCH):
                        p.wait(v, sOE, n * NCH + c + 1)
                        ins = v.tensor_tensor(
                            self.xres[c][:, n * TTS:(n + 1) * TTS],
                            self.xres[c][:, n * TTS:(n + 1) * TTS],
                            tre[c % 2][:], Alu.add)
                        v.drain()
                        p.inc(ins, sXm, 1)
            ph.eng("vector", f_v)

            def f_t(t):
                p.wait(t, sW)
                for n in range(NT):
                    tl = slice(n * TTS, (n + 1) * TTS)
                    for h in range(NH):
                        i = n * NH + h
                        if i >= 2:
                            p.wait(t, sGE, i - 1)
                        if n >= 1 and h < 2:
                            p.wait(t, sOE, (n - 1) * NCH + h + 1)
                        for k in range(NCH):
                            ins = t.matmul(psO[h % 2][:],
                                           gw[k][:, h * 128:(h + 1) * 128],
                                           hb[k][:, tl],
                                           start=(k == 0), stop=(k == NCH - 1))
                        p.inc(ins, sGM, 1)
                    p.wait(t, sGE, (n + 1) * NH)
                    for h in range(NH):
                        i = n * NH + h
                        p.wait(t, sFu, i + 1)
                        for c in range(NCH):
                            if n >= 1 and h == 0:
                                p.wait(t, sOE, (n - 1) * NCH + c + 1)
                            ins = t.matmul(psO[c][:],
                                           ow[h][:, c * 128:(c + 1) * 128],
                                           fus[:], start=(h == 0),
                                           stop=(h == NH - 1))
                        p.inc(ins, sOM, 1)
            ph.eng("tensor", f_t)

            def f_sc(s):
                p.wait(s, sV0, 1)
                for n in range(NT):
                    for h in range(NH):
                        i = n * NH + h
                        p.wait(s, sGM, i + 1)
                        if n >= 1:
                            p.wait(s, sFu, (n - 1) * NH + h + 1)
                        p.inc(s.activation(gsil[:, h * TTS:(h + 1) * TTS],
                                           psO[h % 2][:], Act.Silu,
                                           bias=gb16[:, h:h + 1]), sGE, 1)
                    for c in range(NCH):
                        p.wait(s, sOM, (n + 1) * NH)
                        if n * NCH + c >= 2:
                            p.wait(s, sXm, n * NCH + c - 1)
                        p.inc(s.activation(tre[c % 2][:], psO[c][:], Act.Identity,
                                           scale=self.modfm[1][:, 16 + c:17 + c],
                                           bias=gaob[:, c:c + 1]), sOE, 1)
            ph.eng("scalar", f_sc)


    def l1_moe(self, hb, scm):
        p, dI = self.p, self.dI
        MATS = ("1g", "1l", "3g", "3l")
        NEH = 18
        TH = T // 2
        routed_upto = [sum(1 for x in range(eh + 1) if x % 9 != 8)
                       for eh in range(NEH)]
        with ExitStack() as sc:
            he = [p.sb(f"mxhe{k}", [128, TH], BF16, sc) for k in range(16)]
            wsl = {m: [p.sb(f"mxw{m}{j}", [128, 1024], BF16, sc)
                       for j in range(2)] for m in MATS}
            bcol = {m: [p.sb(f"mxb{m}{j}", [128, 16], F32, sc)
                        for j in range(2)] for m in MATS}
            b2c = [p.sb(f"mxb2{j}", [128, NCH], F32, sc) for j in range(2)]
            w2b = [p.sb(f"mxu{j}", [128, D], BF16, sc) for j in range(4)]
            tb = {nm: [p.sb(f"mxt{nm}{j}", [128, TTS], F32, sc)
                       for j in range(2)] for nm in ("1", "2", "3", "4")}
            cbb = [p.sb(f"mxc{j}", [128, TTS], F32, sc) for j in range(2)]
            dot = [p.sb(f"mxd{j}", [128, TTS], F32, sc) for j in range(8)]
            gwt = p.sb("mxgw", [128, NCH * NE], F32, sc)
            gbb = p.sb("mxgb", [128, NE], F32, sc)
            hfc = [p.sb(f"mxh{j}", [128, 1024], F32, sc) for j in range(2)]
            sco = p.sb("mxsc", [128, NE], F32, sc)
            bia = p.sb("mxbi", [128, NE], F32, sc)
            m1 = p.sb("mxm1", [128, 1], F32, sc)
            cmb = p.sb("mxcm", [128, NE], F32, sc)
            cfm = p.sb("mxcf", [NE, T], F32, sc)
            psA = [p.ps(f"mxp{j}", [128, TTS], F32, sc) for j in range(8)]
            acc_d = self.scr["x"]

            (gW, gK, gVe, gVx, gV, gTr, gTe, gSp) = (
                "mx" + x for x in ("gW", "gK", "gVe", "gVx", "gV", "gTr",
                                   "gTe", "gSp"))
            gHf = ["mxgH0", "mxgH1"]
            (sWb, sMM, sEv, sG, sS, sSw, sDM, sDE, sDC) = (
                "mx" + x for x in ("Wb", "MM", "Ev", "G", "S", "Hw", "DM",
                                   "DE", "DC"))
            sAc = [f"mxAc{j}" for j in range(8)]
            sWl = ["mxWl0", "mxWl1"]
            sW2 = ["mxW20", "mxW21", "mxW22", "mxW23"]
            sCb = ["mxCb0", "mxCb1"]

            def HLs(hh, n2):
                return slice((hh * 2 + n2) * TTS, (hh * 2 + n2 + 1) * TTS)

            def LLs(n2):
                return slice(n2 * TTS, (n2 + 1) * TTS)

            def wsrc(shared, e, m, i):
                nm = ("sw" if shared else "w") + m[0] + "_" + m[1]
                r0 = ((0 if shared else e * 16) + i) * 128
                return dI[nm][r0:r0 + 128, :]

            with p.phase("mx") as ph:
                def wsl_load(g, eh, i):
                    e_ = eh % 9
                    sh = e_ == NE
                    gi = eh * 16 + i
                    if gi >= 2:
                        p.wait(g, sMM, 8 * (gi - 1))
                    for m in MATS:
                        p.inc(g.dma_start(out=wsl[m][gi % 2][:],
                                          in_=wsrc(sh, e_, m, i)),
                              sWl[gi % 2], 16)

                def f_g(g):
                    p.inc(g.dma_start(out=gwt[:], in_=dI["gate_wp"]), gW, 16)
                    p.inc(g.dma_start(out=gbb[:], in_=dI["gate_bbc"]), gW, 16)

                    def biases(eh):
                        e_ = eh % 9
                        sh = e_ == NE
                        if eh >= 2:
                            p.wait(g, sEv, 128 * (eh - 1))
                            p.wait(g, sDE, 16 * (eh - 1))
                        for m in MATS:
                            bn = (("sb" if sh else "b") + m[0] + "_"
                                  + m[1] + "_c")
                            src = (dI[bn] if sh
                                   else dI[bn][e_ * 128:(e_ + 1) * 128, :])
                            p.inc(g.dma_start(out=bcol[m][eh % 2][:], in_=src),
                                  sWb, 16)
                        if not sh:
                            p.inc(g.dma_start(
                                out=b2c[eh % 2][:],
                                in_=dI["b2_c"][e_ * 128:(e_ + 1) * 128, :]),
                                sWb, 16)

                    def w2load(eh, ks):
                        e_ = eh % 9
                        sh = e_ == NE
                        for kk in ks:
                            gik = eh * 32 + kk
                            if gik >= 4:
                                p.wait(g, sDM, gik - 3)
                            r0 = (0 if sh else e_ * MI) + (kk % 16) * 128
                            nm = "sw2" if sh else "w2"
                            p.inc(g.dma_start(
                                out=w2b[gik % 4][:],
                                in_=dI[nm][r0:r0 + 128, :]),
                                sW2[gik % 4], 16)

                    biases(0)
                    wsl_load(g, 0, 0)
                    wsl_load(g, 0, 1)
                    for eh in range(NEH):
                        w2load(eh, range(0, 4))
                        for i in range(2, 16):
                            wsl_load(g, eh, i)
                        w2load(eh, range(4, 16))
                        self._moe_accum(g, eh, 0, acc_d, dot, sDC, sAc, HLs)
                        if eh + 1 < NEH:
                            biases(eh + 1)
                            wsl_load(g, eh + 1, 0)
                            wsl_load(g, eh + 1, 1)
                        w2load(eh, range(16, 32))
                        self._moe_accum(g, eh, 1, acc_d, dot, sDC, sAc, HLs)
                ph.eng("gpsimd", f_g)

                def f_sy(s):
                    for q in range(16):
                        if q >= 2:
                            p.wait(s, gK, q - 1)
                        k, hf = q // 2, q % 2
                        p.inc(s.dma_start(
                            out=hfc[q % 2][:],
                            in_=self.scr["P"][k * 128:(k + 1) * 128,
                                              hf * 1024:(hf + 1) * 1024]),
                            gHf[q % 2], 16)
                    p.wait(s, gTe, 16)
                    p.inc(s.dma_start(out=self.scr["comb"], in_=cfm[:]),
                          gSp, 16)
                    for eh in range(NEH):
                        hh, e = divmod(eh, 9)
                        if e != NE:
                            for n2 in range(2):
                                if eh == 0:
                                    p.wait(s, gSp, 16)
                                else:
                                    p.wait(s, sDC,
                                           (eh - 1) * 16 + n2 * 8 + 8)
                                p.inc(s.dma_start(
                                    out=cbb[n2][:],
                                    in_=self.scr["comb"][e:e + 1, HLs(hh, n2)]
                                    .partition_broadcast(128)), sCb[n2], 16)
                ph.eng("sync", f_sy)

                def f_t(t):
                    p.wait(t, gW, 32)
                    for q in range(16):
                        k, hf = q // 2, q % 2
                        p.wait(t, gHf[q % 2], 16 * (q // 2 + 1))
                        for it8 in range(8):
                            ig = hf * 8 + it8
                            ins = t.matmul(psA[0][:, ig * 8:(ig + 1) * 8],
                                           hfc[q % 2][:, it8 * 128:(it8 + 1) * 128],
                                           gwt[:, k * 8:(k + 1) * 8],
                                           start=(q == 0 and it8 == 0),
                                           stop=(q == 15 and it8 == 7))
                        p.inc(ins, gK, 1)
                    for ig in range(16):
                        p.wait(t, gV, ig + 1)
                        if ig >= 1:
                            p.wait(t, gTe, ig)
                        p.inc(t.transpose(psA[1][0:NE, 0:128], cmb[:],
                                          self.identf[:]), gTr, 1)
                    for eh in range(NEH):
                        hh, e = divmod(eh, 9)
                        for i in range(16):
                            gi = eh * 16 + i
                            p.wait(t, sWl[gi % 2], 64 * (gi // 2 + 1))
                            for n2 in range(2):
                                it = i * 2 + n2
                                if it <= 1:
                                    if eh == 0:
                                        if it == 0:
                                            p.wait(t, gTe, 16)
                                    else:
                                        p.wait(t, sDE, 16 * eh)
                                else:
                                    p.wait(t, sEv, eh * 128 + 4 * (it - 1))
                                for mi, m in enumerate(MATS):
                                    bank = (it % 2) * 4 + mi
                                    for k in range(NCH):
                                        ins = t.matmul(
                                            psA[bank][:],
                                            wsl[m][gi % 2][:, k * 128:(k + 1) * 128],
                                            hb[k][:, HLs(hh, n2)],
                                            start=(k == 0), stop=(k == 7))
                                    p.inc(ins, sMM, 1)
                        for n2 in range(2):
                            if n2 == 0:
                                p.wait(t, sEv, 128 * (eh + 1))
                            else:
                                p.wait(t, sDE, eh * 16 + 8)
                            for k in range(16):
                                gik = eh * 32 + n2 * 16 + k
                                p.wait(t, sW2[gik % 4], 16 * (gik // 4 + 1))
                                p.wait(t, sSw, eh * 32 + k * 2 + n2 + 1)
                                for c in range(NCH):
                                    ins = t.matmul(
                                        psA[c][:],
                                        w2b[gik % 4][:, c * 128:(c + 1) * 128],
                                        he[k][:, LLs(n2)],
                                        start=(k == 0), stop=(k == 15))
                                p.inc(ins, sDM, 1)
                ph.eng("tensor", f_t)

                def f_sc(s):
                    for ig in range(16):
                        p.wait(s, gVe, ig + 1)
                        p.inc(s.activation(sco[:], sco[:], Act.Exp), gVx, 1)
                    for eh in range(NEH):
                        hh, e = divmod(eh, 9)
                        shared = e == NE
                        for it in range(32):
                            git = eh * 32 + it
                            i = it // 2
                            j = git % 2
                            if it == 0:
                                p.wait(s, sWb,
                                       16 * sum(4 if x % 9 == NE else 5
                                                for x in range(eh + 1)))
                            p.wait(s, sMM, eh * 128 + 4 * (it + 1))
                            if git >= 2:
                                p.wait(s, sSw, git - 1)
                            bank = (it % 2) * 4
                            p.inc(s.activation(tb["1"][j][:], psA[bank][:],
                                               Act.Identity,
                                               bias=bcol["1g"][eh % 2][:, i:i + 1]),
                                  sEv, 1)
                            p.inc(s.activation(tb["2"][j][:], psA[bank + 2][:],
                                               Act.Identity,
                                               bias=bcol["3g"][eh % 2][:, i:i + 1]),
                                  sEv, 1)
                            p.inc(s.activation(tb["3"][j][:], psA[bank + 1][:],
                                               Act.Identity,
                                               bias=bcol["1l"][eh % 2][:, i:i + 1]),
                                  sEv, 1)
                            p.inc(s.activation(tb["4"][j][:], psA[bank + 3][:],
                                               Act.Identity,
                                               bias=bcol["3l"][eh % 2][:, i:i + 1]),
                                  sEv, 1)
                            p.wait(s, sG, git + 1)
                            p.inc(s.activation(tb["1"][j][:], tb["1"][j][:],
                                               Act.Silu, scale=1.702), sS, 1)
                        for n2 in range(2):
                            for c in range(NCH):
                                gdj = eh * 16 + n2 * 8 + c
                                p.wait(s, sDM, eh * 32 + 16 * (n2 + 1))
                                if gdj >= 8:
                                    p.wait(s, sAc[gdj % 8],
                                           16 * (gdj // 8))
                                bc = (self.cols["sb2"] if shared
                                      else b2c[eh % 2])
                                p.inc(s.activation(dot[gdj % 8][:], psA[c][:],
                                                   Act.Identity,
                                                   bias=bc[:, c:c + 1]),
                                      sDE, 1)
                ph.eng("scalar", f_sc)

                def f_v(v):
                    p.wait(v, gK, 16)
                    for ig in range(16):
                        gsl = psA[0][:, ig * 8:(ig + 1) * 8]
                        v.tensor_reduce(m1[:], gsl, AX.X, Alu.max)
                        v.drain()
                        v.tensor_scalar(sco[:], gsl, m1[:, 0:1], None,
                                        Alu.subtract)
                        v.drain()
                        p.inc(v.drain(), gVe, 1)
                        p.wait(v, gVx, ig + 1)
                        v.tensor_reduce(m1[:], sco[:], AX.X, Alu.add)
                        v.drain()
                        v.reciprocal(m1[:], m1[:])
                        v.drain()
                        v.tensor_scalar(sco[:], sco[:], m1[:, 0:1], None,
                                        Alu.mult)
                        v.drain()
                        v.tensor_tensor(bia[:], sco[:], gbb[:], Alu.add)
                        v.drain()
                        v.tensor_reduce(m1[:], bia[:], AX.X, Alu.max)
                        v.drain()
                        v.tensor_scalar(cmb[:], bia[:], m1[:, 0:1], None,
                                        Alu.is_ge)
                        v.drain()
                        v.scalar_tensor_tensor(bia[:], cmb[:], -1e30, bia[:],
                                               Alu.mult, Alu.add)
                        v.drain()
                        v.tensor_reduce(m1[:], bia[:], AX.X, Alu.max)
                        v.drain()
                        v.tensor_scalar(bia[:], bia[:], m1[:, 0:1], None,
                                        Alu.is_ge)
                        v.drain()
                        v.tensor_tensor(cmb[:], cmb[:], bia[:], Alu.add)
                        v.drain()
                        ins = v.tensor_tensor(cmb[:], cmb[:], sco[:], Alu.mult)
                        v.drain()
                        p.inc(ins, gV, 1)
                        p.wait(v, gTr, ig + 1)
                        ins = v.tensor_copy(cfm[0:NE, ig * 128:(ig + 1) * 128],
                                            psA[1][0:NE, 0:128])
                        v.drain()
                        p.inc(ins, gTe, 1)
                    for eh in range(NEH):
                        hh, e = divmod(eh, 9)
                        shared = e == NE
                        for it in range(32):
                            git = eh * 32 + it
                            i, n2 = it // 2, it % 2
                            j = git % 2
                            p.wait(v, sEv, eh * 128 + 4 * (it + 1))
                            v.tensor_tensor(tb["1"][j][:], tb["1"][j][:],
                                            tb["2"][j][:], Alu.mult)
                            v.drain()
                            ins = v.tensor_scalar(tb["1"][j][:], tb["1"][j][:],
                                                  7.0, None, Alu.min)
                            v.drain()
                            p.inc(ins, sG, 1)
                            v.tensor_tensor(tb["3"][j][:], tb["3"][j][:],
                                            tb["4"][j][:], Alu.mult)
                            v.drain()
                            v.tensor_scalar(tb["3"][j][:], tb["3"][j][:],
                                            7.0, -7.0, Alu.min, Alu.max)
                            v.drain()
                            v.tensor_scalar(tb["3"][j][:], tb["3"][j][:],
                                            1.0 / 1.702, 1.0 / 1.702,
                                            Alu.mult, Alu.add)
                            v.drain()
                            p.wait(v, sS, git + 1)
                            if eh >= 1:
                                p.wait(v, sDM, (eh - 1) * 32 + 16 + i + 1)
                            ins = v.tensor_tensor(he[i][:, LLs(n2)],
                                                  tb["1"][j][:], tb["3"][j][:],
                                                  Alu.mult)
                            v.drain()
                            p.inc(ins, sSw, 1)
                        for n2 in range(2):
                            for c in range(NCH):
                                gdj = eh * 16 + n2 * 8 + c
                                p.wait(v, sDE, gdj + 1)
                                if not shared:
                                    p.wait(v, sCb[n2], 16 * routed_upto[eh])
                                    v.tensor_tensor(dot[gdj % 8][:],
                                                    dot[gdj % 8][:],
                                                    cbb[n2][:], Alu.mult)
                                    v.drain()
                                p.inc(v.drain(), sDC, 1)
                ph.eng("vector", f_v)

        with ExitStack() as sc2, p.phase("mxmg") as ph:
            at = [p.sb(f"mxmg{j}", [128, TH], F32, sc2) for j in range(2)]
            ut = p.sb("mxmgu", [128, TH], F32, sc2)
            sL = ["mxmgL0", "mxmgL1"]
            sV = "mxmgV"

            def f_sy(s):
                for j in range(16):
                    c, hhh = j // 2, j % 2
                    if j >= 2:
                        p.wait(s, sV, j - 1)
                    p.inc(s.dma_start(
                        out=at[j % 2][:],
                        in_=self.scr["x"][c * 128:(c + 1) * 128,
                                          hhh * TH:(hhh + 1) * TH]),
                        sL[j % 2], 16)
            ph.eng("sync", f_sy)

            def f_v(v):
                for j in range(16):
                    c, hhh = j // 2, j % 2
                    hl = slice(hhh * TH, (hhh + 1) * TH)
                    p.wait(v, sL[j % 2], 16 * (j // 2 + 1))
                    v.tensor_scalar(ut[:], at[j % 2][:],
                                    self.modfm[1][:, 40 + c:41 + c], None,
                                    Alu.mult)
                    v.drain()
                    ins = v.tensor_tensor(self.xres[c][:, hl],
                                          self.xres[c][:, hl],
                                          ut[:], Alu.add)
                    v.drain()
                    p.inc(ins, sV, 1)
            ph.eng("vector", f_v)
        self.tap("final", [xc.ap() for xc in self.xres])

    def _moe_accum(self, g, eh, n2, acc_d, dot, sDC, sAc, HLs):
        p = self.p
        hh, e = divmod(eh, 9)
        for c in range(NCH):
            gdj = eh * 16 + n2 * 8 + c
            p.wait(g, sDC, gdj + 1)
            if gdj >= 8:
                p.wait(g, sAc[gdj % 8], 16 * (gdj // 8))
            p.inc(g.dma_start(
                out=acc_d[c * 128:(c + 1) * 128, HLs(hh, n2)],
                in_=dot[gdj % 8][:],
                accum_op=(Alu.bypass if e == 0 else Alu.add)),
                sAc[gdj % 8], 16)


def pack_inputs(inputs):
    global CONSTS
    if CONSTS is None:
        CONSTS = host_constants()
    C = CONSTS
    g = lambda k: np.asarray(inputs[k])
    com = {}
    tr = lambda a: np.ascontiguousarray(np.asarray(a).T.astype(np.float32))
    com["wk_t"] = tr(g("L0_attn_Wk")); com["wv_t"] = tr(g("L0_attn_Wv"))
    com["wq_t"] = tr(g("L0_attn_Wq")); com["wkp_t"] = tr(g("L0_attn_Wkp"))
    com["wqp_t"] = tr(g("L0_attn_Wqp")); com["wo_t"] = tr(g("L0_attn_Wo"))
    com["bk"] = g("L0_attn_bk"); com["bv"] = g("L0_attn_bv"); com["bq"] = g("L0_attn_bq")
    com["bkp"] = g("L0_attn_bkp"); com["bqp"] = g("L0_attn_bqp"); com["bo"] = g("L0_attn_bo")
    com["ps_col"] = g("L0_attn_phase_scale")
    com["ln_g"] = g("L0_attn_ln_g"); com["ln_b"] = g("L0_attn_ln_b")
    com["w_up"] = g("L0_ffn_w_up"); com["w_gate"] = g("L0_ffn_w_gate")
    com["w_down"] = g("L0_ffn_w_down")
    com["wmod0"] = g("L0_wmod"); com["wmod1"] = g("L1_wmod")
    com["q_w"] = g("L1_attn_q_w"); com["k_w"] = g("L1_attn_k_w")
    com["v_w"] = g("L1_attn_v_w"); com["g_w"] = g("L1_attn_g_w")
    com["o_w"] = g("L1_attn_o_w")
    com["q_b"] = g("L1_attn_q_b"); com["k_b"] = g("L1_attn_k_b")
    com["v_b"] = g("L1_attn_v_b"); com["g_b"] = g("L1_attn_g_b")
    com["o_b"] = g("L1_attn_o_b")
    gwT = np.asarray(g("L1_ffn_gate_w")).T
    com["gate_wp"] = gwT.reshape(NCH, 128, NE).transpose(1, 0, 2).reshape(128, NCH * NE)
    com["gate_bbc"] = np.broadcast_to(np.asarray(g("L1_ffn_gate_b"))[None, :],
                                      (128, NE))
    w1 = g("L1_ffn_w1"); w3 = g("L1_ffn_w3")
    wpk = lambda a: np.ascontiguousarray(
        np.asarray(a).reshape(NE, NCH, 128, 16, 128)
        .transpose(0, 3, 2, 1, 4).reshape(NE * MI, D))
    com["w1_g"] = wpk(w1[:, :, 0::2]); com["w1_l"] = wpk(w1[:, :, 1::2])
    com["w3_g"] = wpk(w3[:, :, 0::2]); com["w3_l"] = wpk(w3[:, :, 1::2])
    com["w2"] = np.asarray(g("L1_ffn_w2")).reshape(NE * MI, D)
    b1 = g("L1_ffn_b1"); b3 = g("L1_ffn_b3")
    bc = lambda a: np.asarray(a).reshape(NE, 16, 128).transpose(0, 2, 1).reshape(NE * 128, 16)
    com["b1_g_c"] = bc(b1[:, 0::2]); com["b1_l_c"] = bc(b1[:, 1::2])
    com["b3_g_c"] = bc(b3[:, 0::2]); com["b3_l_c"] = bc(b3[:, 1::2])
    com["b2_c"] = np.asarray(g("L1_ffn_b2")).reshape(NE, NCH, 128).transpose(0, 2, 1).reshape(NE * 128, NCH)
    sw1 = g("L1_ffn_sw1"); sw3 = g("L1_ffn_sw3")
    spk = lambda a: np.ascontiguousarray(
        np.asarray(a).reshape(NCH, 128, 16, 128)
        .transpose(2, 1, 0, 3).reshape(MI, D))
    com["sw1_g"] = spk(sw1[:, 0::2]); com["sw1_l"] = spk(sw1[:, 1::2])
    com["sw3_g"] = spk(sw3[:, 0::2]); com["sw3_l"] = spk(sw3[:, 1::2])
    com["sw2"] = g("L1_ffn_sw2")
    sb1 = g("L1_ffn_sb1"); sb3 = g("L1_ffn_sb3")
    sbc = lambda a: np.asarray(a).reshape(16, 128).T
    com["sb1_g_c"] = sbc(sb1[0::2]); com["sb1_l_c"] = sbc(sb1[1::2])
    com["sb3_g_c"] = sbc(sb3[0::2]); com["sb3_l_c"] = sbc(sb3[1::2])
    com["sb2"] = g("L1_ffn_sb2")
    for k in ("ident_f32", "ones_f32", "ones_row", "zero_col", "halfpi_col",
              "rot_cos", "rot_sin", "dlt", "alpha", "beta_bc", "rnorm", "ehot"):
        com[k] = C[k]
    for v in ("bk", "bv", "bq", "bkp", "bqp", "bo", "ps_col", "ln_g", "ln_b",
              "q_b", "k_b", "o_b", "sb2"):
        com[v] = np.asarray(com[v]).reshape(NCH, 128).T
    com["v_b_row"] = np.asarray(com["v_b"]).reshape(1, 2 * D); del com["v_b"]
    com["g_b"] = np.asarray(com["g_b"]).reshape(16, 128).T
    com = {k: np.ascontiguousarray(np.asarray(v, dtype=np.float32)) for k, v in com.items()}
    import ml_dtypes
    com["ident_bf"] = np.eye(128).astype(ml_dtypes.bfloat16)

    x = np.asarray(inputs["L0_attn_x"])
    c0 = np.asarray(inputs["L0_c"]); c1 = np.asarray(inputs["L1_c"])
    maps = []
    for b in range(B):
        m = dict(com)
        m["xT"] = np.ascontiguousarray(x[b].T.astype(np.float32))
        m["c0"] = np.ascontiguousarray(c0[b].reshape(NCH, 128).T.astype(np.float32))
        m["c1"] = np.ascontiguousarray(c1[b].reshape(NCH, 128).T.astype(np.float32))
        maps.append(m)
    return maps


def kernel(**inputs):
    maps = pack_inputs(inputs)
    nc = build_program()
    res = run_bass_kernel_spmd(nc, maps, core_ids=list(range(B)))
    outs = [np.asarray(res.results[b]["out_fm"]).T for b in range(B)]
    return np.ascontiguousarray(np.stack(outs)).astype(np.float32)
